# version 10
# baseline (speedup 1.0000x reference)
import numpy as np
import ml_dtypes
from contextlib import ExitStack

import concourse.bass as bass
import concourse.mybir as mybir
from concourse.bass_utils import run_bass_kernel_spmd

F32 = mybir.dt.float32
BF16 = mybir.dt.bfloat16
AF = mybir.ActivationFunctionType
ALU = mybir.AluOpType
AX = mybir.AxisListType

D = 4096
S = 4096
T = 2048
KVD = 1024
H = 32
HK = 8
HD = 128
ND = 32
NTS = 32
NT = 16
NQT = 4
NP = 16
RSQ = float(1.0 / np.sqrt(HD))
NQF = float(-0.5 * np.sqrt(HD))
NKF = float(-0.5 / np.sqrt(HD))

BF = ml_dtypes.bfloat16


class Ctx:
    pass


class V:

    def __init__(self, ap):
        self._ap = ap

    def ap(self):
        return self._ap


def _sem(c, name):
    return c.st.enter_context(c.nc.semaphore(name))


def vc(c, inst):
    c.vcnt += 1
    inst.then_inc(c.semVC, 1)
    return inst


def vw(c, v):
    v.wait_ge(c.semVC, c.vcnt)


def sc_(c, inst):
    c.scnt += 1
    inst.then_inc(c.semSC, 1)
    return inst


def sw_(c, s):
    s.wait_ge(c.semSC, c.scnt)


class DSem:

    def __init__(self, c, name, n=2):
        self.n = n
        self.s = [_sem(c, name + chr(97 + j)) for j in range(n)]
        self.ub = 0
        self.acc = [0] * n

    def h(self, i):
        return self.s[(self.ub + i) % self.n]

    def v(self, i, per=16):
        return self.acc[(self.ub + i) % self.n] + per * (i // self.n + 1)

    def wait(self, eng, i, per=16):
        if i >= 0:
            eng.wait_ge(self.h(i), self.v(i, per))

    def close(self, n, per):
        for j in range(self.n):
            self.acc[j] += per * (n // self.n)
        self.ub += n


def _sb(c, name, shape, dt):
    return c.st.enter_context(c.nc.sbuf_tensor(name, shape, dt))


def build_program(n_phases=9, dbg=()):
    nc = bass.Bass("TRN2", target_bir_lowering=False)
    c = Ctx()
    c.nc = nc

    def dram(name, shape, dt, kind="ExternalInput"):
        if name in dbg and kind == "Internal" and dt == F32:
            kind = "ExternalOutput"
        return nc.dram_tensor(name, shape, dt, kind=kind)

    c.x = dram("x", [S, D], F32)
    c.wq = dram("wq", [D, D], F32)
    c.wk = dram("wk", [KVD, D], F32)
    c.wv = dram("wv", [KVD, D], F32)
    c.wo = dram("wo", [D, D], F32)
    c.wb = dram("wb", [D, D], F32)
    c.qsc_in = dram("qsc_in", [ND, 128], F32)
    c.ksc_in = dram("ksc_in", [HK, 128], F32)
    c.vsc_in = dram("vsc_in", [HK, 128], F32)
    c.osc_in = dram("osc_in", [ND, 128], F32)
    c.bb_in = dram("bb_in", [ND, 128], F32)
    c.gt_in = dram("gt_in", [128, 256], F32)
    c.lat_in = dram("lat_in", [128, 4096], F32)
    c.lbt_in = dram("lbt_in", [128, 4096], F32)
    c.id_bf = dram("id_bf", [128, 128], BF16)
    c.id_f32 = dram("id_f32", [128, 128], F32)
    c.ones_row = dram("ones_row", [1, 128], BF16)
    c.ones_col = dram("ones_col", [128, 1], BF16)
    c.ones_sq_f32 = dram("ones_sq_f32", [128, 128], F32)
    c.exp8 = dram("exp8", [8, 128], BF16)
    c.y = dram("y", [T, D], F32, kind="ExternalOutput")

    c.XFM = dram("XFM", [128, NTS, ND, 128], BF16, kind="Internal")
    c.QS = dram("QS", [ND, 128, T], BF16, kind="Internal")
    c.KSCR = dram("KSCR", [HK, 128, S], BF16, kind="Internal")
    c.VSCR = dram("VSCR", [HK, 128, NTS, 128], BF16, kind="Internal")
    c.OOUT = dram("OOUT", [NT, 128, D], F32, kind="Internal")
    c.X1TM = dram("X1TM", [NT, 128, D], F32, kind="Internal")
    c.X1FM = dram("X1FM", [ND, 128, T], BF16, kind="Internal")
    c.FFOUT = dram("FFOUT", [NT, 128, D], F32, kind="Internal")
    c.ROWSCR = dram("ROWSCR", [8, 128], F32, kind="Internal")

    st = ExitStack()
    c.st = st
    c.semVC = st.enter_context(nc.semaphore("chainV"))
    c.vcnt = 0
    c.semSC = st.enter_context(nc.semaphore("chainS"))
    c.scnt = 0

    c.BIG1 = _sb(c, "BIG1", [128, 65536], BF16)
    c.BIG2 = _sb(c, "BIG2", [128, 28672], BF16)
    c.PREP = _sb(c, "PREP", [128, 8448], BF16)
    c.sm_idb = _sb(c, "sm_idb", [128, 128], BF16)
    c.sm_idf = _sb(c, "sm_idf", [128, 128], F32)
    c.sm_ones_row = _sb(c, "sm_ones_row", [1, 128], BF16)
    c.sm_ones_col = _sb(c, "sm_ones_col", [128, 1], BF16)
    c.sm_onesqf = _sb(c, "sm_onesqf", [128, 128], F32)
    c.sm_recl = _sb(c, "sm_recl", [1, 512], F32)
    c.sm_recb = _sb(c, "sm_recb", [1, 512], BF16)
    c.sm_exp8 = _sb(c, "sm_exp8", [8, 128], BF16)
    c.sm_qsc = _sb(c, "sm_qsc", [128, 32], F32)
    c.sm_osc = _sb(c, "sm_osc", [128, 32], F32)
    c.sm_bbc = _sb(c, "sm_bbc", [128, 32], F32)
    c.sm_kscin = _sb(c, "sm_kscin", [128, 8], F32)
    c.sm_vscin = _sb(c, "sm_vscin", [128, 8], F32)
    c.sm_kscale = _sb(c, "sm_kscale", [128, 8], F32)
    c.sm_vscale = _sb(c, "sm_vscale", [128, 8], F32)
    c.sm_nk5 = _sb(c, "sm_nk5", [1, 8], F32)
    c.sm_nq32 = _sb(c, "sm_nq32", [1, 32], F32)
    c.sm_sacc = _sb(c, "sm_sacc", [128, 1], F32)
    c.sm_stmp = _sb(c, "sm_stmp", [128, 1], F32)
    c.sm_scl = _sb(c, "sm_scl", [128, 1], F32)
    c.sm_scl2 = _sb(c, "sm_scl2", [128, 2], F32)
    c.sm_negu = _sb(c, "sm_negu", [128, 2], F32)
    c.sm_t1 = _sb(c, "sm_t1", [1, 8], F32)
    c.sm_t1b = _sb(c, "sm_t1b", [1, 8], BF16)
    c.sm_bn = _sb(c, "sm_bn", [128, 48], F32)
    c.sm_mv = _sb(c, "sm_mv", [128, 2], F32)
    c.sm_var = _sb(c, "sm_var", [128, 1], F32)
    c.sm_sd = _sb(c, "sm_sd", [128, 1], F32)
    c.sm_rstd = _sb(c, "sm_rstd", [128, 1], F32)
    c.sm_nmr = _sb(c, "sm_nmr", [128, 1], F32)
    c.sm_lgt = _sb(c, "sm_lgt", [128, 128], F32)
    c.sm_m8 = _sb(c, "sm_m8", [128, 8], F32)
    c.sm_ngm = _sb(c, "sm_ngm", [128, 1], F32)
    c.sm_p8e = _sb(c, "sm_p8e", [128, 8], F32)
    c.sm_sel = _sb(c, "sm_sel", [128, 8], F32)
    c.sm_psel = _sb(c, "sm_psel", [128, 8], F32)
    c.sm_den8 = _sb(c, "sm_den8", [128, 1], F32)
    c.sm_re8 = _sb(c, "sm_re8", [128, 1], F32)
    c.sm_w8 = _sb(c, "sm_w8", [128, 128], BF16)
    big2v = c.BIG2.ap()
    big2vf = big2v.bitcast(F32)
    big1v = c.BIG1.ap()
    c.sm_vscb = V(big2v[:, 22528:23552])
    c.sm_svrow = V(big2v[0:1, 26624:27648])
    c.sm_accv = V(big2vf[:, 11776:12800])
    c.sm_accg = V(big2vf[:, 12800:13824])
    c.sm_bcs = V(big2vf[:, 13824:14336])
    c.sm_wt8 = V(big2v[0:8, 0:2048])

    c.PSQ = st.enter_context(nc.psum_tensor("PSQ", [128, 2048], F32))
    c.PSA = st.enter_context(nc.psum_tensor("PSA", [128, 512], F32))
    c.PSD = st.enter_context(nc.psum_tensor("PSD", [128, 512], F32))
    c.PSB = st.enter_context(nc.psum_tensor("PSB", [128, 2048], BF16))

    c.AT = c.PREP.ap()[:, 0:4096].rearrange("p (a b) -> p a b", a=32)
    c.BT = c.PREP.ap()[:, 4096:8192].rearrange("p (a b) -> p a b", a=32)
    c.GT = c.PREP.ap()[:, 8192:8448].rearrange("p (a b) -> p a b", a=32)
    c.RES = c.BIG1.ap()[:, 0:65536].rearrange("p (a b) -> p a b", a=32)
    c.psbf = c.PSB.ap().bitcast(F32)

    try:
        phase0(c)
        if n_phases >= 1:
            phase1(c)
        if n_phases >= 2:
            phase2(c)
        if n_phases >= 3:
            phase34(c)
        if n_phases >= 5:
            phase5(c)
        if n_phases >= 6:
            phase6(c)
        if n_phases >= 7:
            phase7(c)
        if n_phases >= 8:
            phase8(c)
        if n_phases >= 9:
            phase9(c)
        else:
            _stub_output(c)
        _debug_dumps(c, dbg)
    finally:
        st.close()
    nc.finalize()
    return nc


def _debug_dumps(c, dbg):
    shapes = {"XFM": (c.XFM, 32, 4096), "QS": (c.QS, 32, 2048),
              "KSCR": (c.KSCR, 8, 4096), "VSCR": (c.VSCR, 8, 4096),
              "X1FM": (c.X1FM, 32, 2048)}
    todo = [nm for nm in dbg if nm in shapes]
    if not todo:
        return
    nc = c.nc
    for nm in todo:
        src, outer, inner = shapes[nm]
        out = nc.dram_tensor(nm + "_f32", [outer, 128, inner], F32, kind="ExternalOutput")
        with nc.semaphore("dbgl_" + nm) as semL, \
                nc.semaphore("dbgc_" + nm) as semC, \
                nc.semaphore("dbgs_" + nm) as semS, nc.Block() as blk:
            bufb = c.BIG2.ap()[:, 0:inner]
            buff = c.BIG2.ap().bitcast(F32)[:, 4096:4096 + inner]

            @blk.sync
            def _(sy):
                for o in range(outer):
                    if o >= 1:
                        sy.wait_ge(semC, o)
                    if nm == "XFM":
                        sy.dma_start(bufb, src.ap()[:, o, :, :]
                                     .rearrange("p a b -> p (a b)")).then_inc(semL, 16)
                    elif nm == "VSCR":
                        sy.dma_start(bufb, src.ap()[o, :, :, :]
                                     .rearrange("p a b -> p (a b)")).then_inc(semL, 16)
                    else:
                        sy.dma_start(bufb, src.ap()[o, :, :]).then_inc(semL, 16)

            @blk.vector
            def _(v):
                for o in range(outer):
                    v.wait_ge(semL, 16 * (o + 1))
                    if o >= 1:
                        v.wait_ge(semS, 16 * o)
                    v.tensor_copy(buff, bufb).then_inc(semC, 1)

            @blk.gpsimd
            def _(g):
                for o in range(outer):
                    g.wait_ge(semC, o + 1)
                    g.dma_start(out.ap()[o, :, :], buff).then_inc(semS, 16)
                    g.wait_ge(semS, 16 * (o + 1))


def _stub_output(c):
    nc = c.nc
    with nc.semaphore("stub_sem") as sem, nc.semaphore("stub_dma") as semd, \
            nc.Block() as blk:
        @blk.vector
        def _(v):
            z = c.BIG1.ap().bitcast(F32)[:, 0:4096]
            v.memset(z, 0.0).then_inc(sem, 1)

        @blk.gpsimd
        def _(g):
            g.wait_ge(sem, 1)
            z = c.BIG1.ap().bitcast(F32)[:, 0:4096]
            for tc in range(NT):
                g.dma_start(c.y.ap()[tc * 128:(tc + 1) * 128, :], z).then_inc(semd, 16)
                g.wait_ge(semd, 16 * (tc + 1))


def phase0(c):
    nc = c.nc
    sIN = _sem(c, "p0_in")
    sC = _sem(c, "p0_c")
    big1f = c.BIG1.ap().bitcast(F32)

    with nc.Block() as blk:
        @blk.sync
        def _(sy):
            n = 0
            for dst, src in [
                (c.sm_idb.ap(), c.id_bf.ap()),
                (c.sm_idf.ap(), c.id_f32.ap()),
                (c.sm_ones_row.ap(), c.ones_row.ap()),
                (c.sm_ones_col.ap(), c.ones_col.ap()),
                (c.sm_onesqf.ap(), c.ones_sq_f32.ap()),
                (c.sm_exp8.ap(), c.exp8.ap()),
            ]:
                sy.dma_start(dst, src).then_inc(sIN, 16)
                n += 16
            for i, src in enumerate([c.qsc_in, c.osc_in, c.bb_in]):
                sy.dma_start(big1f[0:32, i * 128:(i + 1) * 128], src.ap()).then_inc(sIN, 16)
                n += 16
            for i, src in enumerate([c.ksc_in, c.vsc_in]):
                sy.dma_start(big1f[0:8, 384 + i * 128:384 + (i + 1) * 128],
                             src.ap()).then_inc(sIN, 16)
                n += 16
            sy.dma_start(big1f[:, 1024:1280], c.gt_in.ap()).then_inc(sIN, 16)
            n += 16
            sy.dma_start(big1f[:, 2048:6144], c.lat_in.ap()).then_inc(sIN, 16)
            n += 16
            sy.dma_start(big1f[:, 6144:10240], c.lbt_in.ap()).then_inc(sIN, 16)
            n += 16
            c.p0_n_in = n

        @blk.tensor
        def _(t):
            t.wait_ge(sIN, c.p0_n_in)
            pf = c.psbf
            t.transpose(pf[:, 0:32], big1f[0:32, 0:128], c.sm_idf.ap()[0:32, 0:32])
            t.transpose(pf[:, 32:64], big1f[0:32, 128:256], c.sm_idf.ap()[0:32, 0:32])
            t.transpose(pf[:, 64:96], big1f[0:32, 256:384], c.sm_idf.ap()[0:32, 0:32])
            t.transpose(pf[:, 96:104], big1f[0:8, 384:512], c.sm_idf.ap()[0:8, 0:8])
            t.transpose(pf[:, 104:112], big1f[0:8, 512:640],
                        c.sm_idf.ap()[0:8, 0:8]).then_inc(sC, 1)

        @blk.vector
        def _(v):
            v.wait_ge(sC, 1)
            v.wait_ge(sIN, c.p0_n_in)
            pf = c.psbf
            v.tensor_copy(c.sm_qsc.ap(), pf[:, 0:32])
            v.tensor_copy(c.sm_osc.ap(), pf[:, 32:64])
            v.tensor_copy(c.sm_bbc.ap(), pf[:, 64:96])
            v.tensor_copy(c.sm_kscin.ap(), pf[:, 96:104])
            v.tensor_copy(c.sm_vscin.ap(), pf[:, 104:112])
            v.tensor_copy(c.GT.rearrange("p a b -> p (a b)"), big1f[:, 1024:1280])
            v.tensor_copy(c.AT.rearrange("p a b -> p (a b)"), big1f[:, 2048:6144])
            v.tensor_copy(c.BT.rearrange("p a b -> p (a b)"),
                          big1f[:, 6144:10240]).then_inc(sC, 1)


def phase1(c):
    nc = c.nc
    sX = DSem(c, "p1_x")
    sCst = _sem(c, "p1_cst")
    sTp = _sem(c, "p1_tp")
    sEv = _sem(c, "p1_ev")
    sMy = _sem(c, "p1_my")
    sOut = DSem(c, "p1_out")

    big2 = c.BIG2.ap()
    big2f = big2.bitcast(F32)
    XST = [big2f[:, 0:4096], big2f[:, 4096:8192]]
    XC = big2[:, 16384:20480]
    XSTG = [big2[:, 20480:24576], big2[:, 24576:28672]]
    psb = c.PSB.ap()

    with nc.Block() as blk:
        @blk.sync
        def _(sy):
            for tc in range(NTS):
                b = tc % 2
                if tc >= 2:
                    sy.wait_ge(sCst, tc - 1)
                sy.dma_start(XST[b],
                             c.x.ap()[tc * 128:(tc + 1) * 128, :]).then_inc(sX.h(tc), 16)

        @blk.scalar
        def _(s):
            for tc in range(NTS):
                b = tc % 2
                sX.wait(s, tc)
                if tc >= 1:
                    s.wait_ge(sTp, 4 * tc)
                s.activation(XC, XST[b], AF.Copy).then_inc(sCst, 1)

        @blk.tensor
        def _(t):
            for tc in range(NTS):
                b = tc % 2
                t.wait_ge(sCst, tc + 1)
                for g in range(4):
                    gi = tc * 4 + g
                    pb = psb[:, (gi % 2) * 1024:(gi % 2) * 1024 + 1024]
                    if gi >= 2:
                        t.wait_ge(sEv, gi - 1)
                    last = None
                    for j in range(8):
                        dc = g * 8 + j
                        last = t.transpose(pb[:, j * 128:(j + 1) * 128],
                                           XC[:, dc * 128:(dc + 1) * 128], c.sm_idb.ap())
                    last.then_inc(sTp, 1)

        @blk.vector
        def _(v):
            for tc in range(NTS):
                b = tc % 2
                for g in range(4):
                    gi = tc * 4 + g
                    pb = psb[:, (gi % 2) * 1024:(gi % 2) * 1024 + 1024]
                    v.wait_ge(sTp, gi + 1)
                    if tc >= 2 and g == 0:
                        sOut.wait(v, tc - 2)
                    v.tensor_copy(
                        XSTG[b].rearrange("p (a b) -> p a b", a=32)[:, g * 8:(g + 1) * 8, :],
                        pb.rearrange("p (a b) -> p a b", a=8)).then_inc(sEv, 1)
                if tc < NT:
                    v.wait_ge(sEv, 4 * (tc + 1))
                    v.tensor_copy(c.RES[:, :, tc * 128:(tc + 1) * 128],
                                  XSTG[b].rearrange("p (a b) -> p a b", a=32)).then_inc(sMy, 1)

        @blk.gpsimd
        def _(g):
            nmy = 0
            for tc in range(NTS):
                b = tc % 2
                g.wait_ge(sEv, 4 * (tc + 1))
                if tc < NT:
                    nmy += 1
                    g.wait_ge(sMy, nmy)
                g.dma_start(c.XFM.ap()[:, tc, :, :],
                            XSTG[b].rearrange("p (a b) -> p a b", a=32)
                            ).then_inc(sOut.h(tc), 16)
            sOut.wait(g, NTS - 2)
            sOut.wait(g, NTS - 1)


class ProjSems:

    KEYS = ["ab", "sg", "tp", "te", "mm", "ev", "ot", "oe", "qsq", "nqm",
            "nqr", "nqd", "pre"]

    def __init__(self, c):
        self.h = {k: _sem(c, "proj_" + k) for k in self.KEYS}
        self.base = {k: 0 for k in self.KEYS}
        self.dW = DSem(c, "proj_w", n=4)
        self.dD = DSem(c, "proj_dm")

    def wg(self, eng, k, v):
        if self.base[k] + v > 0:
            eng.wait_ge(self.h[k], self.base[k] + v)

    def inc(self, inst, k, n=1):
        inst.then_inc(self.h[k], n)

    def fin(self, used):
        for k, n in used.items():
            self.base[k] += n


def emit_proj(c, name, w_dram, scale_col, const_scale, has_sign, out_mode,
              out_dram, preload=None, n_preload=0, lora=False, qnorms=False,
              guard=None):
    nc = c.nc
    if not hasattr(c, "projsems"):
        c.projsems = ProjSems(c)
    P = c.projsems
    sW = P.dW
    sDm = P.dD
    gsem, gval = guard() if guard is not None else (None, 0)

    big2 = c.BIG2.ap()
    big2f = big2.bitcast(F32)
    WST = [big2f[:, j * 1024:(j + 1) * 1024] for j in range(4)]
    WSG = [big2[:, 8192 + j * 1024:8192 + (j + 1) * 1024] for j in range(4)]
    WT = [big2[:, 12288:16384].rearrange("p (a b) -> p a b", a=32),
          big2[:, 16384:20480].rearrange("p (a b) -> p a b", a=32)]
    QEVA = [big2[:, 20480:22528], big2[:, 22528:24576]]
    QSQ2 = [big2[:, 24576:26624], big2[:, 26624:28672]]
    OEVA = [big2f[:, 10240 + q * 512:10240 + (q + 1) * 512] for q in range(4)]
    OSTG = [big2f[:, 13312:13824], big2f[:, 13824:14336]]
    psq = c.PSQ.ap()
    psb = c.PSB.ap()
    pso = [c.PSA.ap(), c.PSD.ap()]
    psn = [c.PSA.ap()[0:1, 0:512], c.PSD.ap()[0:1, 0:512]]
    f32_tm = out_mode == "f32_tm"
    LOWW = big2[:, 24576:26624] if lora else None

    with nc.Block() as blk:
        @blk.sync
        def _(sy):
            if gsem is not None:
                sy.wait_ge(gsem, gval)
            if preload is not None:
                preload(sy, P.h["pre"])
            for do in range(ND):
                for qq in range(4):
                    i = do * 4 + qq
                    b = i % 4
                    if i >= 4:
                        if has_sign:
                            P.wg(sy, "ab", i - 3)
                        P.wg(sy, "sg", i - 3)
                    sy.dma_start(WST[b],
                                 w_dram.ap()[do * 128:(do + 1) * 128,
                                             qq * 1024:(qq + 1) * 1024]
                                 ).then_inc(sW.h(i), 16)

        @blk.scalar
        def _(s):
            if gsem is not None:
                s.wait_ge(gsem, gval)
            func = AF.Sign if has_sign else AF.Copy
            for do in range(ND):
                for qq in range(4):
                    i = do * 4 + qq
                    sW.wait(s, i)
                    if i >= 3:
                        P.wg(s, "tp", i - 2)
                    P.inc(s.activation(WSG[i % 4], WST[i % 4], func), "sg")

        @blk.tensor
        def _(t):
            if gsem is not None:
                t.wait_ge(gsem, gval)
            if preload is not None:
                t.wait_ge(P.h["pre"], P.base["pre"] + n_preload)
            for do in range(ND):
                bdo = do % 2
                for qq in range(4):
                    i = do * 4 + qq
                    P.wg(t, "sg", i + 1)
                    if i >= 2:
                        P.wg(t, "te", i - 1)
                    pb = psb[:, (i % 2) * 1024:(i % 2) * 1024 + 1024]
                    last = None
                    for j in range(8):
                        last = t.transpose(pb[:, j * 128:(j + 1) * 128],
                                           WSG[i % 4][:, j * 128:(j + 1) * 128],
                                           c.sm_idb.ap())
                    P.inc(last, "tp")
                if f32_tm and do >= 1:
                    _proj_out_tr(c, t, do - 1, OEVA, pso, P)
                if do >= 1:
                    P.wg(t, "ev", 4 * do)
                last = None
                for dc in range(ND):
                    if dc % 8 == 0:
                        P.wg(t, "te", 4 * do + dc // 8 + 1)
                    for qt in range(NQT):
                        last = t.matmul(psq[:, qt * 512:(qt + 1) * 512],
                                        WT[bdo][:, dc, :],
                                        c.RES[:, dc, qt * 512:(qt + 1) * 512],
                                        start=(dc == 0),
                                        stop=(dc == ND - 1 and not lora))
                if lora:
                    for qt in range(NQT):
                        last = t.matmul(psq[:, qt * 512:(qt + 1) * 512],
                                        c.BT[:, do, :],
                                        LOWW[:, qt * 512:(qt + 1) * 512],
                                        start=False, stop=True)
                P.inc(last, "mm")
                if qnorms and do >= 2:
                    _proj_nq_mm(c, t, do - 2, QSQ2, psn, P)
            if qnorms:
                _proj_nq_mm(c, t, ND - 2, QSQ2, psn, P)
                _proj_nq_mm(c, t, ND - 1, QSQ2, psn, P)
            if f32_tm:
                _proj_out_tr(c, t, ND - 1, OEVA, pso, P)

        @blk.vector
        def _(v):
            if gsem is not None:
                v.wait_ge(gsem, gval)
            for do in range(ND):
                if do >= 1:
                    _proj_out_evac(c, v, do - 1, out_mode, psq, QEVA, OEVA, OSTG, pso,
                                   P, sDm, has_sign, scale_col, qnorms, QSQ2, psn)
                for qq in range(4):
                    i = do * 4 + qq
                    P.wg(v, "tp", i + 1)
                    if do >= 2 and qq == 0:
                        P.wg(v, "mm", do - 1)
                    P.inc(v.tensor_copy(WT[do % 2][:, qq * 8:(qq + 1) * 8, :],
                                        psb[:, (i % 2) * 1024:(i % 2) * 1024 + 1024]
                                        .rearrange("p (a b) -> p a b", a=8)), "te")
                if has_sign:
                    for qq in range(4):
                        i = do * 4 + qq
                        sW.wait(v, i)
                        if i >= 1:
                            P.wg(v, "ab", i)
                        if qq == 0:
                            vw(c, v)
                            P.inc(v.reduce_sum(c.sm_sacc.ap(), WST[i % 4], axis=AX.X,
                                               apply_absolute_value=True), "ab")
                        else:
                            vc(c, v.reduce_sum(c.sm_stmp.ap(), WST[i % 4], axis=AX.X,
                                               apply_absolute_value=True))
                            vw(c, v)
                            P.inc(v.tensor_add(c.sm_sacc.ap(), c.sm_sacc.ap(),
                                               c.sm_stmp.ap()), "ab")
                    P.wg(v, "ab", 4 * do + 4)
                    vc(c, v.tensor_scalar(c.sm_scl.ap(), c.sm_sacc.ap(), 1.0 / 4096.0,
                                          1e-8, ALU.mult, ALU.max))
                    if do >= 2:
                        P.wg(v, "ev", 4 * do - 4)
                    vw(c, v)
                    vc(c, v.tensor_scalar(c.sm_scl2.ap()[:, do % 2:do % 2 + 1],
                                          c.sm_scl.ap(), scale_col[:, do:do + 1],
                                          const_scale, ALU.mult, ALU.mult))
                if qnorms and do >= 1:
                    _proj_qnorm_sq(c, v, do - 1, QEVA, QSQ2, psn, P)
            _proj_out_evac(c, v, ND - 1, out_mode, psq, QEVA, OEVA, OSTG, pso,
                           P, sDm, has_sign, scale_col, qnorms, QSQ2, psn)
            if qnorms:
                _proj_qnorm_sq(c, v, ND - 1, QEVA, QSQ2, psn, P)
                _proj_qnorm_red(c, v, ND - 1, psn, P)

        @blk.gpsimd
        def _(g):
            if out_mode == "bf16_fm":
                for do in range(ND):
                    P.wg(g, "ev", 4 * do + 4)
                    g.dma_start(out_dram.ap()[do, :, :],
                                QEVA[do % 2]).then_inc(sDm.h(do), 16)
                sDm.wait(g, ND - 2)
                sDm.wait(g, ND - 1)
            else:
                for do in range(ND):
                    for qt in range(NQT):
                        oq = do * 4 + qt
                        P.wg(g, "oe", oq + 1)
                        b = oq % 2
                        for j in range(4):
                            tc = qt * 4 + j
                            g.dma_start(out_dram.ap()[tc, :, do * 128:(do + 1) * 128],
                                        OSTG[b][:, j * 128:(j + 1) * 128]
                                        ).then_inc(sDm.h(oq), 16)
                sDm.wait(g, 4 * ND - 2, per=64)
                sDm.wait(g, 4 * ND - 1, per=64)

    used = {"sg": 128, "tp": 128, "te": 128, "mm": 32, "ev": 4 * ND}
    if has_sign:
        used["ab"] = 128
    if f32_tm:
        used["ot"] = 4 * ND
        used["oe"] = 4 * ND
    if qnorms:
        used["qsq"] = ND
        used["nqm"] = 4 * ND
        used["nqr"] = ND
        used["nqd"] = 4 * ND
        c.sNqr = P.h["nqr"]
        c.nqr_val = P.base["nqr"] + ND
    if preload is not None:
        used["pre"] = n_preload
    sW.close(128, 16)
    if out_mode == "bf16_fm":
        sDm.close(ND, 16)
    else:
        sDm.close(4 * ND, 64)
    P.fin(used)


def _proj_qnorm_sq(c, v, d, QEVA, QSQ2, psn, P):
    P.wg(v, "ev", 4 * d + 4)
    if d >= 2:
        P.wg(v, "nqm", 4 * (d - 1))
    P.inc(v.tensor_tensor(QSQ2[d % 2], QEVA[d % 2], QEVA[d % 2], ALU.mult), "qsq")
    if d >= 1:
        _proj_qnorm_red(c, v, d - 1, psn, P)


def _proj_nq_mm(c, t, dom, QSQ2, psn, P):
    P.wg(t, "qsq", dom + 1)
    if dom >= 1:
        P.wg(t, "nqr", dom)
    for qt in range(NQT):
        if qt >= 2:
            P.wg(t, "nqd", 4 * dom + qt - 1)
        P.inc(t.matmul(psn[qt % 2], c.sm_ones_col.ap(),
                       QSQ2[dom % 2][:, qt * 512:(qt + 1) * 512],
                       start=True, stop=True), "nqm")


def _proj_out_tr(c, t, do, OEVA, pso, P):
    for qt in range(NQT):
        oq = do * 4 + qt
        P.wg(t, "ev", 4 * do + qt + 1)
        if oq >= 2:
            P.wg(t, "oe", oq - 1)
        last = None
        pf = pso[oq % 2]
        for j in range(4):
            last = t.transpose(pf[:, j * 128:(j + 1) * 128],
                               OEVA[qt][:, j * 128:(j + 1) * 128], c.sm_idf.ap())
        P.inc(last, "ot")


def _proj_qnorm_red(c, v, do, psn, P):
    for qt in range(NQT):
        P.wg(v, "nqm", 4 * do + qt + 1)
        vw(c, v)
        if qt == 0:
            P.inc(v.tensor_reduce(c.sm_t1.ap()[:, 0:1], psn[0], AX.X, ALU.max), "nqd")
        else:
            P.inc(v.tensor_reduce(c.sm_t1.ap()[:, 1:2], psn[qt % 2], AX.X,
                                  ALU.max), "nqd")
            P.wg(v, "nqd", 4 * do + qt + 1)
            vc(c, v.tensor_max(c.sm_t1.ap()[:, 0:1], c.sm_t1.ap()[:, 0:1],
                               c.sm_t1.ap()[:, 1:2]))
    P.wg(v, "nqd", 4 * do + 1)
    vw(c, v)
    P.inc(v.tensor_scalar(c.sm_nq32.ap()[:, do:do + 1], c.sm_t1.ap()[:, 0:1],
                          NQF, None, ALU.mult), "nqr")


def _proj_out_evac(c, v, do, out_mode, psq, QEVA, OEVA, OSTG, pso,
                   P, sDm, has_sign, scale_col, qnorms, QSQ2, psn):
    if out_mode == "bf16_fm":
        P.wg(v, "mm", do + 1)
        vw(c, v)
        b = do % 2
        if do >= 2:
            sDm.wait(v, do - 2)
        for qt in range(NQT):
            P.inc(v.tensor_scalar(QEVA[b][:, qt * 512:(qt + 1) * 512],
                                  psq[:, qt * 512:(qt + 1) * 512],
                                  c.sm_scl2.ap()[:, b:b + 1], None, ALU.mult), "ev")

    else:
        if do >= 1:
            for qt in range(NQT):
                oqp = (do - 1) * 4 + qt
                P.wg(v, "ot", oqp + 1)
                if oqp >= 2:
                    sDm.wait(v, oqp - 2, per=64)
                P.inc(v.tensor_copy(OSTG[oqp % 2], pso[oqp % 2]), "oe")
        P.wg(v, "mm", do + 1)
        vw(c, v)
        for qt in range(NQT):
            if has_sign:
                P.inc(v.tensor_scalar(OEVA[qt], psq[:, qt * 512:(qt + 1) * 512],
                                      c.sm_scl2.ap()[:, do % 2:do % 2 + 1], None,
                                      ALU.mult), "ev")
            else:
                P.inc(v.tensor_scalar(OEVA[qt], psq[:, qt * 512:(qt + 1) * 512],
                                      scale_col[:, do:do + 1], None, ALU.add), "ev")
        if do == ND - 1:
            for qt in range(NQT):
                oq = (ND - 1) * 4 + qt
                P.wg(v, "ot", oq + 1)
                sDm.wait(v, oq - 2, per=64)
                P.inc(v.tensor_copy(OSTG[oq % 2], pso[oq % 2]), "oe")


def phase2(c):
    emit_proj(c, "p2", c.wq, c.sm_qsc.ap(), RSQ, True, "bf16_fm", c.QS, qnorms=True)


def phase6(c):
    emit_proj(c, "p6", c.wo, c.sm_osc.ap(), 1.0, True, "f32_tm", c.OOUT,
              guard=lambda: (c.sAo5, H * NQT))


def phase34(c):
    nc = c.nc
    sW = DSem(c, "p34_w")
    sAb = _sem(c, "p34_ab")
    sSg = _sem(c, "p34_sg")
    sTp = _sem(c, "p34_tp")
    sTe = _sem(c, "p34_te")
    sSc = _sem(c, "p34_sc")
    sR = _sem(c, "p34_row")
    sX = DSem(c, "p34_x")
    sMM = _sem(c, "p34_mm")
    sEv = _sem(c, "p34_ev")
    sKt = _sem(c, "p34_kt")
    sKf = _sem(c, "p34_kf")
    sKsq = _sem(c, "p34_ksq")
    sNKm = _sem(c, "p34_nkm")
    sNKr = _sem(c, "p34_nkr")
    c.sNKr = sNKr
    sVd = DSem(c, "p34_vd")
    sKd = DSem(c, "p34_kd")

    big1 = c.BIG1.ap()
    big2 = c.BIG2.ap()
    big2f = big2.bitcast(F32)
    WKT = big1[:, 0:32768].rearrange("p (a b) -> p a b", a=32)
    WVT = big1[:, 32768:65536].rearrange("p (a b) -> p a b", a=32)
    WST = [big2f[:, 0:1024], big2f[:, 1024:2048]]
    WSG = [big2[:, 4096:5120], big2[:, 5120:6144]]
    XT = [big2[:, 6144:10240].rearrange("p (a b) -> p a b", a=32),
          big2[:, 10240:14336].rearrange("p (a b) -> p a b", a=32)]
    KTM = [big2[:, 14336:15360], big2[:, 15360:16384]]
    VTM = [big2[:, 16384:17408], big2[:, 17408:18432]]
    KFM = [big2[:, 18432:19456], big2[:, 19456:20480]]
    KSQB = [big2[:, 20480:21504], big2[:, 21504:22528]]
    psq = c.PSQ.ap()
    psb = c.PSB.ap()
    psn = [c.PSA.ap()[0:1, 0:512], c.PSD.ap()[0:1, 0:512]]

    with nc.Block() as blk:
        @blk.sync
        def _(sy):
            i = 0
            for w in range(2):
                wd = c.wk if w == 0 else c.wv
                for rc in range(8):
                    for qq in range(4):
                        if i >= 2:
                            sy.wait_ge(sAb, i - 1)
                            sy.wait_ge(sSg, i - 1)
                        sy.dma_start(WST[i % 2],
                                     wd.ap()[rc * 128:(rc + 1) * 128,
                                             qq * 1024:(qq + 1) * 1024]
                                     ).then_inc(sW.h(i), 16)
                        i += 1

        @blk.scalar
        def _(s):
            for i in range(64):
                sW.wait(s, i)
                if i >= 1:
                    s.wait_ge(sTp, i)
                s.activation(WSG[i % 2], WST[i % 2], AF.Sign).then_inc(sSg, 1)

        @blk.tensor
        def _(t):
            for i in range(64):
                t.wait_ge(sSg, i + 1)
                if i >= 1:
                    t.wait_ge(sTe, i)
                pb = psb[:, 0:1024]
                last = None
                for j in range(8):
                    last = t.transpose(pb[:, j * 128:(j + 1) * 128],
                                       WSG[i % 2][:, j * 128:(j + 1) * 128], c.sm_idb.ap())
                last.then_inc(sTp, 1)

        @blk.vector
        def _(v):
            i = 0
            for w in range(2):
                WTd = WKT if w == 0 else WVT
                for rc in range(8):
                    for qq in range(4):
                        sW.wait(v, i)
                        if i >= 1:
                            v.wait_ge(sAb, i)
                        if qq == 0:
                            vw(c, v)
                            v.reduce_sum(c.sm_sacc.ap(), WST[i % 2], axis=AX.X,
                                         apply_absolute_value=True).then_inc(sAb, 1)
                        else:
                            vc(c, v.reduce_sum(c.sm_stmp.ap(), WST[i % 2], axis=AX.X,
                                               apply_absolute_value=True))
                            vw(c, v)
                            v.tensor_add(c.sm_sacc.ap(), c.sm_sacc.ap(),
                                         c.sm_stmp.ap()).then_inc(sAb, 1)
                        v.wait_ge(sTp, i + 1)
                        v.tensor_copy(WTd[:, qq * 8:(qq + 1) * 8, rc * 128:(rc + 1) * 128],
                                      psb[:, 0:1024].rearrange("p (a b) -> p a b", a=8)
                                      ).then_inc(sTe, 1)
                        i += 1
                    v.wait_ge(sAb, i)
                    vc(c, v.tensor_scalar(c.sm_scl.ap(), c.sm_sacc.ap(), 1.0 / 4096.0,
                                          1e-8, ALU.mult, ALU.max))
                    dst = c.sm_kscale.ap() if w == 0 else c.sm_vscale.ap()
                    scol = c.sm_kscin.ap() if w == 0 else c.sm_vscin.ap()
                    vw(c, v)
                    lastsc = v.tensor_scalar(dst[:, rc:rc + 1], c.sm_scl.ap(),
                                             scol[:, rc:rc + 1], None, ALU.mult)
            lastsc.then_inc(sSc, 1)

    sRd = _sem(c, "p34_rowd")
    with nc.Block() as blk:
        @blk.tensor
        def _(t):
            t.wait_ge(sSc, 1)
            t.transpose(c.psbf[0:8, 0:128], c.sm_vscale.ap(), c.sm_idf.ap()).then_inc(sR, 1)
            t.wait_ge(sRd, 32)
            t.matmul(psq[:, 0:512], c.sm_ones_row.ap(), c.sm_svrow.ap()[:, 0:512],
                     start=True, stop=True)
            t.matmul(psq[:, 512:1024], c.sm_ones_row.ap(),
                     c.sm_svrow.ap()[:, 512:1024], start=True, stop=True).then_inc(sR, 1)

        @blk.vector
        def _(v):
            v.wait_ge(sR, 1)
            v.tensor_copy(big2f[0:8, 2048:2176], c.psbf[0:8, 0:128]).then_inc(sR, 1)
            v.wait_ge(sR, 3)
            v.tensor_copy(c.sm_vscb.ap(), psq[:, 0:1024]).then_inc(sR, 1)

        @blk.gpsimd
        def _(g):
            g.wait_ge(sR, 2)
            g.dma_start(c.ROWSCR.ap(), big2f[0:8, 2048:2176]).then_inc(sRd, 16)
            g.wait_ge(sRd, 16)
            g.dma_start(c.sm_svrow.ap(),
                        c.ROWSCR.ap().rearrange("a b -> (a b)").unsqueeze(0)).then_inc(sRd, 16)

    with nc.Block() as blk:
        @blk.sync
        def _(sy):
            for tc in range(NTS):
                if tc >= 2:
                    sy.wait_ge(sMM, tc - 1)
                sy.dma_start(XT[tc % 2].rearrange("p a b -> p (a b)"),
                             c.XFM.ap()[:, tc, :, :].rearrange("p a b -> p (a b)")
                             ).then_inc(sX.h(tc), 16)

        @blk.tensor
        def _(t):
            for tc in range(NTS):
                sX.wait(t, tc)
                if tc >= 1:
                    t.wait_ge(sEv, 2 * tc)
                for dc in range(ND):
                    lhs = XT[tc % 2][:, dc, :]
                    t.matmul(psq[:, 0:512], lhs, WKT[:, dc, 0:512],
                             start=(dc == 0), stop=(dc == ND - 1))
                    t.matmul(psq[:, 512:1024], lhs, WKT[:, dc, 512:1024],
                             start=(dc == 0), stop=(dc == ND - 1))
                    t.matmul(psq[:, 1024:1536], lhs, WVT[:, dc, 0:512],
                             start=(dc == 0), stop=(dc == ND - 1))
                    lastmm = t.matmul(psq[:, 1536:2048], lhs, WVT[:, dc, 512:1024],
                                      start=(dc == 0), stop=(dc == ND - 1))
                lastmm.then_inc(sMM, 1)
                if tc >= 1:
                    t.wait_ge(sEv, 2 * (tc - 1) + 1)
                    if tc >= 3:
                        t.wait_ge(sKf, 8 * (tc - 2))
                    pb = psb[:, ((tc - 1) % 2) * 1024:((tc - 1) % 2) * 1024 + 1024]
                    last = None
                    for kvh in range(8):
                        last = t.transpose(pb[:, kvh * 128:(kvh + 1) * 128],
                                           KTM[(tc - 1) % 2][:, kvh * 128:(kvh + 1) * 128],
                                           c.sm_idb.ap())
                    last.then_inc(sKt, 1)
                if tc >= 2:
                    t.wait_ge(sKsq, tc - 1)
                    if tc >= 3:
                        t.wait_ge(sNKr, 8 * (tc - 2))
                    for kvh in range(8):
                        if kvh >= 2:
                            t.wait_ge(sNKr, 8 * (tc - 2) + kvh - 1)
                        t.matmul(psn[kvh % 2][:, 0:128], c.sm_ones_col.ap(),
                                 KSQB[(tc - 2) % 2][:, kvh * 128:(kvh + 1) * 128],
                                 start=True, stop=True).then_inc(sNKm, 1)
            for tcl in [NTS - 1]:
                t.wait_ge(sEv, 2 * tcl + 1)
                t.wait_ge(sKf, 8 * (tcl - 1))
                pb = psb[:, (tcl % 2) * 1024:(tcl % 2) * 1024 + 1024]
                last = None
                for kvh in range(8):
                    last = t.transpose(pb[:, kvh * 128:(kvh + 1) * 128],
                                       KTM[tcl % 2][:, kvh * 128:(kvh + 1) * 128],
                                       c.sm_idb.ap())
                last.then_inc(sKt, 1)
            for tcl in [NTS - 2, NTS - 1]:
                t.wait_ge(sKsq, tcl + 1)
                t.wait_ge(sNKr, 8 * tcl)
                for kvh in range(8):
                    if kvh >= 2:
                        t.wait_ge(sNKr, 8 * tcl + kvh - 1)
                    t.matmul(psn[kvh % 2][:, 0:128], c.sm_ones_col.ap(),
                             KSQB[tcl % 2][:, kvh * 128:(kvh + 1) * 128],
                             start=True, stop=True).then_inc(sNKm, 1)

        @blk.vector
        def _(v):
            for tc in range(NTS):
                v.wait_ge(sMM, tc + 1)
                if tc == 0:
                    v.wait_ge(sR, 4)
                if tc >= 2:
                    v.wait_ge(sKt, tc - 1)
                    sVd.wait(v, tc - 2, per=128)
                v.tensor_copy(KTM[tc % 2], psq[:, 0:1024]).then_inc(sEv, 1)
                v.tensor_tensor(VTM[tc % 2], psq[:, 1024:2048], c.sm_vscb.ap(),
                                ALU.mult).then_inc(sEv, 1)
                if tc >= 1:
                    tcm = tc - 1
                    v.wait_ge(sKt, tcm + 1)
                    if tcm >= 2:
                        sKd.wait(v, tcm - 2, per=128)
                    if tcm >= 2:
                        v.wait_ge(sNKm, 8 * (tcm - 1))
                    pb = psb[:, (tcm % 2) * 1024:(tcm % 2) * 1024 + 1024]
                    for kvh in range(8):
                        v.tensor_scalar(KFM[tcm % 2][:, kvh * 128:(kvh + 1) * 128],
                                        pb[:, kvh * 128:(kvh + 1) * 128],
                                        c.sm_kscale.ap()[:, kvh:kvh + 1], None,
                                        ALU.mult).then_inc(sKf, 1)
                    v.wait_ge(sKf, 8 * (tcm + 1))
                    v.tensor_tensor(KSQB[tcm % 2], KFM[tcm % 2], KFM[tcm % 2],
                                    ALU.mult).then_inc(sKsq, 1)
                if tc >= 2:
                    _p34_nkred(c, v, tc - 2, psn, sNKm, sNKr)
            tcm = NTS - 1
            v.wait_ge(sKt, tcm + 1)
            sKd.wait(v, tcm - 2, per=128)
            v.wait_ge(sNKm, 8 * (tcm - 1))
            pb = psb[:, (tcm % 2) * 1024:(tcm % 2) * 1024 + 1024]
            for kvh in range(8):
                v.tensor_scalar(KFM[tcm % 2][:, kvh * 128:(kvh + 1) * 128],
                                pb[:, kvh * 128:(kvh + 1) * 128],
                                c.sm_kscale.ap()[:, kvh:kvh + 1], None,
                                ALU.mult).then_inc(sKf, 1)
            v.wait_ge(sKf, 8 * (tcm + 1))
            v.tensor_tensor(KSQB[tcm % 2], KFM[tcm % 2], KFM[tcm % 2],
                            ALU.mult).then_inc(sKsq, 1)
            _p34_nkred(c, v, NTS - 2, psn, sNKm, sNKr)
            _p34_nkred(c, v, NTS - 1, psn, sNKm, sNKr)
            v.wait_ge(sNKr, 8 * NTS)
            v.tensor_scalar(c.sm_nk5.ap(), c.sm_nk5.ap(), NKF, None, ALU.mult
                            ).then_inc(sNKr, 1)

        @blk.gpsimd
        def _(g):
            for tc in range(NTS):
                g.wait_ge(sEv, 2 * (tc + 1))
                for kvh in range(8):
                    g.dma_start(c.VSCR.ap()[kvh, :, tc, :],
                                VTM[tc % 2][:, kvh * 128:(kvh + 1) * 128]
                                ).then_inc(sVd.h(tc), 16)
                g.wait_ge(sKf, 8 * (tc + 1))
                for kvh in range(8):
                    g.dma_start(c.KSCR.ap()[kvh, :, tc * 128:(tc + 1) * 128],
                                KFM[tc % 2][:, kvh * 128:(kvh + 1) * 128]
                                ).then_inc(sKd.h(tc), 16)
            sVd.wait(g, NTS - 2, per=128)
            sVd.wait(g, NTS - 1, per=128)
            sKd.wait(g, NTS - 2, per=128)
            sKd.wait(g, NTS - 1, per=128)


def _p34_nkred(c, v, tc, psn, sNKm, sNKr):
    for kvh in range(8):
        v.wait_ge(sNKm, 8 * tc + kvh + 1)
        if tc == 0:
            v.tensor_reduce(c.sm_nk5.ap()[:, kvh:kvh + 1], psn[kvh % 2][:, 0:128],
                            AX.X, ALU.max).then_inc(sNKr, 1)
        else:
            v.wait_ge(sNKr, 8 * tc + kvh)
            vc(c, v.tensor_reduce(c.sm_t1.ap()[:, 3:4], psn[kvh % 2][:, 0:128],
                                  AX.X, ALU.max))
            v.wait_ge(sNKr, 8 * (tc - 1) + kvh + 1)
            vw(c, v)
            v.tensor_max(c.sm_nk5.ap()[:, kvh:kvh + 1],
                         c.sm_nk5.ap()[:, kvh:kvh + 1],
                         c.sm_t1.ap()[:, 3:4]).then_inc(sNKr, 1)


def phase5(c):
    nc = c.nc
    sKV = DSem(c, "p5_kv")
    sQ = DSem(c, "p5_q")
    sNr = _sem(c, "p5_nr")
    sBm = _sem(c, "p5_bm")
    sNg = _sem(c, "p5_ng")
    sQK = _sem(c, "p5_qk")
    sEx = _sem(c, "p5_ex")
    sPV = _sem(c, "p5_pv")
    sAcV = _sem(c, "p5_acv")
    sAcG = _sem(c, "p5_acg")
    sFn = _sem(c, "p5_fn")
    sDn = _sem(c, "p5_dn")
    sRc = _sem(c, "p5_rc")
    sBc = _sem(c, "p5_bc")
    sAo = _sem(c, "p5_ao")
    c.sAo5 = sAo

    big2 = c.BIG2.ap()
    KBUF = [big2[:, 0:4096], big2[:, 4096:8192]]
    VBUF = [big2[:, 8192:12288], big2[:, 12288:16384]]
    QBUF = [big2[:, 16384:18432], big2[:, 18432:20480]]
    PT = [big2[:, 20480 + j * 1024:20480 + (j + 1) * 1024] for j in range(3)]
    ACCV = c.sm_accv.ap()
    ACCG = c.sm_accg.ap()
    BCS = c.sm_bcs.ap()
    psq = c.PSQ.ap()
    psa = c.PSA.ap()
    psd = c.PSD.ap()
    DEN = c.psbf[0:1, 512:1024]
    DENF = c.psbf[:, 512:1024]
    NI = H * NQT

    with nc.Block() as blk:
        @blk.sync
        def _(sy):
            for kvh in range(HK):
                if kvh >= 2:
                    sy.wait_ge(sPV, NP * 16 * (kvh - 1))
                sy.dma_start(KBUF[kvh % 2],
                             c.KSCR.ap()[kvh, :, :]).then_inc(sKV.h(kvh), 16)
                sy.dma_start(VBUF[kvh % 2], c.VSCR.ap()[kvh, :, :, :]
                             .rearrange("p a b -> p (a b)")).then_inc(sKV.h(kvh), 16)
                for r in range(4):
                    h = kvh * 4 + r
                    if h >= 2:
                        sy.wait_ge(sQK, NP * 4 * (h - 1))
                    sy.dma_start(QBUF[h % 2], c.QS.ap()[h, :, :]).then_inc(sQ.h(h), 16)

        @blk.tensor
        def _(t):
            t.wait_ge(sNr, 1)
            t.matmul(c.psbf[:, 0:1], c.sm_ones_row.ap(), c.sm_t1b.ap()[:, 0:1],
                     start=True, stop=True).then_inc(sBm, 1)
            for kvh in range(HK):
                bk = kvh % 2
                sKV.wait(t, kvh, per=32)
                for r in range(4):
                    h = kvh * 4 + r
                    bq = h % 2
                    sQ.wait(t, h)
                    for qt in range(NQT):
                        i = h * 4 + qt
                        for p in range(NP):
                            G = NP * i + p
                            if G >= 2:
                                t.wait_ge(sEx, G - 1)
                            last = None
                            for u in range(2):
                                cc = 2 * p + u
                                last = t.matmul(psq[:, (cc % 4) * 512:(cc % 4) * 512 + 512],
                                                KBUF[bk][:, cc * 128:(cc + 1) * 128],
                                                QBUF[bq][:, qt * 512:(qt + 1) * 512],
                                                start=True, stop=True)
                            last.then_inc(sQK, 1)
                            if p == 0 and i >= 1:
                                t.wait_ge(sFn, i)
                                if i >= 2:
                                    t.wait_ge(sRc, i - 1)
                                t.matmul(DENF, c.sm_onesqf.ap(), ACCV[:, 0:512],
                                         start=True, stop=True).then_inc(sDn, 1)
                            if p == 1 and i >= 1:
                                t.wait_ge(sRc, i)
                                if i >= 2:
                                    t.wait_ge(sAo, i - 1)
                                t.matmul(psd[:, 0:512], c.sm_ones_row.ap(),
                                         c.sm_recb.ap(), start=True,
                                         stop=True).then_inc(sBc, 1)
                            if p >= 2:
                                _p5_pv(c, t, i, p - 2, bk, VBUF, PT, psa, sEx, sPV, sAo)
                            if qt == 3 and p == 4 and h + 1 < H:
                                t.wait_ge(sNr, h + 2)
                                t.wait_ge(sNg, h + 1)
                                t.matmul(c.psbf[:, 0:1], c.sm_ones_row.ap(),
                                         c.sm_t1b.ap()[:, (h + 1) % 2:(h + 1) % 2 + 1],
                                         start=True, stop=True).then_inc(sBm, 1)
                        _p5_pv(c, t, i, NP - 2, bk, VBUF, PT, psa, sEx, sPV, sAo)
                        _p5_pv(c, t, i, NP - 1, bk, VBUF, PT, psa, sEx, sPV, sAo)
            t.wait_ge(sFn, NI)
            t.wait_ge(sRc, NI - 1)
            t.matmul(DENF, c.sm_onesqf.ap(), ACCV[:, 0:512],
                     start=True, stop=True).then_inc(sDn, 1)
            t.wait_ge(sRc, NI)
            t.wait_ge(sAo, NI - 1)
            t.matmul(psd[:, 0:512], c.sm_ones_row.ap(), c.sm_recb.ap(),
                     start=True, stop=True).then_inc(sBc, 1)

        @blk.vector
        def _(v):
            v.wait_ge(c.sNqr, c.nqr_val)
            v.wait_ge(c.sNKr, 8 * NTS + 1)
            vc(c, v.tensor_add(c.sm_t1.ap()[:, 0:1], c.sm_nq32.ap()[:, 0:1],
                               c.sm_nk5.ap()[:, 0:1]))
            vw(c, v)
            v.tensor_copy(c.sm_t1b.ap()[:, 0:1], c.sm_t1.ap()[:, 0:1]).then_inc(sNr, 1)
            v.wait_ge(sBm, 1)
            v.tensor_copy(c.sm_negu.ap()[:, 0:1], c.psbf[:, 0:1]).then_inc(sNg, 1)
            for h in range(H):
                for qt in range(NQT):
                    i = h * 4 + qt
                    for p in range(0, NP, 2):
                        G = NP * i + p
                        k = G // 2
                        v.wait_ge(sEx, G + 1)
                        if k >= 1:
                            v.wait_ge(sAcV, k)
                        if p == 0:
                            if i >= 1:
                                v.wait_ge(sDn, i)
                            v.tensor_copy(ACCV, PT[p % 3]).then_inc(sAcV, 1)
                        else:
                            v.tensor_tensor(ACCV, ACCV, PT[p % 3],
                                            ALU.add).then_inc(sAcV, 1)
                    if qt == 2 and h + 1 < H:
                        hn = h + 1
                        vc(c, v.tensor_add(c.sm_t1.ap()[:, hn % 2:hn % 2 + 1],
                                           c.sm_nq32.ap()[:, hn:hn + 1],
                                           c.sm_nk5.ap()[:, hn // 4:hn // 4 + 1]))
                        v.wait_ge(sBm, hn)
                        vw(c, v)
                        v.tensor_copy(c.sm_t1b.ap()[:, hn % 2:hn % 2 + 1],
                                      c.sm_t1.ap()[:, hn % 2:hn % 2 + 1]).then_inc(sNr, 1)
                    if qt == 3 and h + 1 < H:
                        hn = h + 1
                        v.wait_ge(sBm, hn + 1)
                        v.tensor_copy(c.sm_negu.ap()[:, hn % 2:hn % 2 + 1],
                                      c.psbf[:, 0:1]).then_inc(sNg, 1)
                    v.wait_ge(sAcV, 8 * (i + 1))
                    v.wait_ge(sAcG, 8 * (i + 1))
                    vc(c, v.tensor_add(ACCV[:, 0:512], ACCV[:, 0:512],
                                       ACCV[:, 512:1024]))
                    vw(c, v)
                    vc(c, v.tensor_add(ACCV[:, 0:512], ACCV[:, 0:512],
                                       ACCG[:, 0:512]))
                    vw(c, v)
                    v.tensor_add(ACCV[:, 0:512], ACCV[:, 0:512],
                                 ACCG[:, 512:1024]).then_inc(sFn, 1)
                    v.wait_ge(sBc, i + 1)
                    v.wait_ge(sPV, NP * (i + 1))
                    vc(c, v.tensor_copy(BCS, psd[:, 0:512]))
                    vw(c, v)
                    v.tensor_tensor(c.RES[:, h, qt * 512:(qt + 1) * 512],
                                    psa[:, 0:512], BCS, ALU.mult).then_inc(sAo, 1)

        @blk.scalar
        def _(s):
            for h in range(H):
                for qt in range(NQT):
                    i = h * 4 + qt
                    for p in range(NP):
                        G = NP * i + p
                        s.wait_ge(sQK, G + 1)
                        if p == 0 and qt == 0:
                            s.wait_ge(sNg, h + 1)
                        f = G - 3
                        if f >= 0:
                            s.wait_ge(sPV, f + 1)
                            if f % 2 == 0:
                                s.wait_ge(sAcV, f // 2 + 1)
                            else:
                                s.wait_ge(sAcG, f // 2 + 1)
                        bank = 0 if (p % 2 == 0) else 1024
                        s.activation(PT[p % 3], psq[:, bank:bank + 1024], AF.Exp,
                                     bias=c.sm_negu.ap()[:, h % 2:h % 2 + 1]
                                     ).then_inc(sEx, 1)
                        if p == 0 and i >= 1:
                            s.wait_ge(sDn, i)
                            if i >= 2:
                                s.wait_ge(sBc, i - 1)
                            sc_(c, s.activation(c.sm_recl.ap(), DEN, AF.Ln))
                            sw_(c, s)
                            s.activation(c.sm_recb.ap(), c.sm_recl.ap(), AF.Exp,
                                         scale=-1.0).then_inc(sRc, 1)
            s.wait_ge(sDn, NI)
            s.wait_ge(sBc, NI - 1)
            sc_(c, s.activation(c.sm_recl.ap(), DEN, AF.Ln))
            sw_(c, s)
            s.activation(c.sm_recb.ap(), c.sm_recl.ap(), AF.Exp,
                         scale=-1.0).then_inc(sRc, 1)

        @blk.gpsimd
        def _(g):
            for i in range(NI):
                for p in range(1, NP, 2):
                    G = NP * i + p
                    k = G // 2
                    g.wait_ge(sEx, G + 1)
                    if k >= 1:
                        g.wait_ge(sAcG, k)
                    if p == 1:
                        if i >= 1:
                            g.wait_ge(sFn, i)
                        g.tensor_copy(ACCG, PT[p % 3]).then_inc(sAcG, 1)
                    else:
                        g.tensor_tensor(ACCG, ACCG, PT[p % 3],
                                        ALU.add).then_inc(sAcG, 1)


def _p5_pv(c, t, i, p, bk, VBUF, PT, psa, sEx, sPV, sAo):
    t.wait_ge(sEx, NP * i + p + 1)
    if p == 0 and i >= 1:
        t.wait_ge(sAo, i)
    last = None
    for u in range(2):
        cc = 2 * p + u
        last = t.matmul(psa[:, 0:512], VBUF[bk][:, cc * 128:(cc + 1) * 128],
                        PT[p % 3][:, u * 512:(u + 1) * 512],
                        start=(cc == 0), stop=(cc == 31))
    last.then_inc(sPV, 1)


class LnSems:
    KEYS = ["st", "sd", "ap", "cs", "tp", "ev", "add", "nm"]

    def __init__(self, c):
        self.h = {k: _sem(c, "ln_" + k) for k in self.KEYS}
        self.base = {k: 0 for k in self.KEYS}
        self.dA = DSem(c, "ln_a")
        self.dB = DSem(c, "ln_b")
        self.dO = DSem(c, "ln_od")
        self.dF = DSem(c, "ln_fd")
        self.dF2 = DSem(c, "ln_fd2")

    def wg(self, eng, k, v):
        if self.base[k] + v > 0:
            eng.wait_ge(self.h[k], self.base[k] + v)

    def inc(self, inst, k, n=1):
        inst.then_inc(self.h[k], n)

    def fin(self, used):
        for k, n in used.items():
            self.base[k] += n


def emit_ln(c, name, in_a_dram, in_b_getter, out_writer, fm_out):
    nc = c.nc
    if not hasattr(c, "lnsems"):
        c.lnsems = LnSems(c)
    L = c.lnsems
    sA, sB, sOd, sFd, sFd2 = L.dA, L.dB, L.dO, L.dF, L.dF2

    big1f = c.BIG1.ap().bitcast(F32)
    OB = [big1f[:, 0:4096], big1f[:, 4096:8192]]
    XB = [big1f[:, 8192:12288], big1f[:, 12288:16384]]
    big2 = c.BIG2.ap()
    XB16 = [big2[:, 0:4096], big2[:, 4096:8192]]
    XSTG = [big2[:, 8192:12288], big2[:, 12288:16384]]
    psb = c.PSB.ap()

    with nc.Block() as blk:
        @blk.sync
        def _(sy):
            for tc in range(NT):
                b = tc % 2
                if tc >= 2:
                    L.wg(sy, "ap", tc - 1)
                    sOd.wait(sy, tc - 2)
                    if fm_out:
                        L.wg(sy, "cs", tc - 1)
                sy.dma_start(OB[b], in_a_dram.ap()[tc, :, :]).then_inc(sA.h(tc), 16)
                sy.dma_start(XB[b], in_b_getter(tc)).then_inc(sB.h(tc), 16)

        @blk.vector
        def _(v):
            for tc in range(NT):
                b = tc % 2
                sA.wait(v, tc)
                sB.wait(v, tc)
                vc(c, v.tensor_add(OB[b], OB[b], XB[b]))
                vw(c, v)
                for j in range(8):
                    if j == 7:
                        vc(c, v.bn_stats(c.sm_bn.ap()[:, j * 6:(j + 1) * 6],
                                         OB[b][:, j * 512:(j + 1) * 512]))
                    else:
                        v.bn_stats(c.sm_bn.ap()[:, j * 6:(j + 1) * 6],
                                   OB[b][:, j * 512:(j + 1) * 512])
                vw(c, v)
                vc(c, v.bn_aggr(c.sm_mv.ap(), c.sm_bn.ap()))
                vw(c, v)
                L.inc(v.tensor_scalar(c.sm_var.ap(), c.sm_mv.ap()[:, 1:2], 1e-5, None,
                                      ALU.add), "st")
                L.wg(v, "sd", tc + 1)
                vc(c, v.reciprocal(c.sm_rstd.ap(), c.sm_sd.ap()))
                vw(c, v)
                L.inc(v.scalar_tensor_tensor(c.sm_nmr.ap(), c.sm_mv.ap()[:, 0:1], -1.0,
                                             c.sm_rstd.ap(), ALU.mult, ALU.mult), "nm")
                if fm_out:
                    for g in range(4):
                        gi = tc * 4 + g
                        pb = psb[:, (gi % 2) * 1024:(gi % 2) * 1024 + 1024]
                        L.wg(v, "tp", gi + 1)
                        if tc >= 2 and g == 0:
                            sFd.wait(v, tc - 2, per=256)
                            sFd2.wait(v, tc - 2, per=256)
                        L.inc(v.tensor_copy(
                            XSTG[b].rearrange("p (a b) -> p a b", a=32)
                            [:, g * 8:(g + 1) * 8, :],
                            pb.rearrange("p (a b) -> p a b", a=8)), "ev")

        @blk.scalar
        def _(s):
            for tc in range(NT):
                b = tc % 2
                L.wg(s, "st", tc + 1)
                L.inc(s.activation(c.sm_sd.ap(), c.sm_var.ap(), AF.Sqrt), "sd")
                L.wg(s, "nm", tc + 1)
                L.inc(s.activation(XB[b], OB[b], AF.Identity, scale=c.sm_rstd.ap(),
                                   bias=c.sm_nmr.ap()), "ap")
                if fm_out:
                    L.wg(s, "ap", tc + 1)
                    if tc >= 2:
                        L.wg(s, "tp", 4 * (tc - 1))
                    L.inc(s.activation(XB16[b], XB[b], AF.Copy), "cs")
                    s.wait_ge(L.h["ev"], L.base["ev"] + 4 * (tc + 1))
                    for dc in range(1, ND, 2):
                        s.dma_start(c.X1FM.ap()[dc, :, tc * 128:(tc + 1) * 128],
                                    XSTG[b].rearrange("p (a b) -> p a b", a=32)[:, dc, :]
                                    ).then_inc(sFd2.h(tc), 16)

        if fm_out:
            @blk.tensor
            def _(t):
                for tc in range(NT):
                    b = tc % 2
                    L.wg(t, "cs", tc + 1)
                    for g in range(4):
                        gi = tc * 4 + g
                        if gi >= 2:
                            L.wg(t, "ev", gi - 1)
                        pb = psb[:, (gi % 2) * 1024:(gi % 2) * 1024 + 1024]
                        last = None
                        for j in range(8):
                            dc = g * 8 + j
                            last = t.transpose(pb[:, j * 128:(j + 1) * 128],
                                               XB16[b][:, dc * 128:(dc + 1) * 128],
                                               c.sm_idb.ap())
                        L.inc(last, "tp")

        @blk.gpsimd
        def _(g):
            for tc in range(NT):
                b = tc % 2
                L.wg(g, "ap", tc + 1)
                out_writer(g, tc, XB[b], sOd.h(tc))
                if fm_out:
                    L.wg(g, "ev", 4 * (tc + 1))
                    for dc in range(0, ND, 2):
                        g.dma_start(c.X1FM.ap()[dc, :, tc * 128:(tc + 1) * 128],
                                    XSTG[b].rearrange("p (a b) -> p a b", a=32)[:, dc, :]
                                    ).then_inc(sFd.h(tc), 16)
            sOd.wait(g, NT - 2)
            sOd.wait(g, NT - 1)
            if fm_out:
                sFd.wait(g, NT - 2, per=256)
                sFd.wait(g, NT - 1, per=256)
                sFd2.wait(g, NT - 2, per=256)
                sFd2.wait(g, NT - 1, per=256)

    used = {"st": NT, "sd": NT, "ap": NT, "nm": NT}
    if fm_out:
        used["cs"] = NT
        used["tp"] = 4 * NT
        used["ev"] = 4 * NT
        sFd.close(NT, 256)
        sFd2.close(NT, 256)
    sA.close(NT, 16)
    sB.close(NT, 16)
    sOd.close(NT, 16)
    L.fin(used)


def phase7(c):
    def in_b(tc):
        return c.x.ap()[tc * 128:(tc + 1) * 128, :]

    def out_w(g, tc, src, semh):
        g.dma_start(c.X1TM.ap()[tc, :, :], src).then_inc(semh, 16)

    emit_ln(c, "p7", c.OOUT, in_b, out_w, fm_out=True)


def phase9(c):
    def in_b(tc):
        return c.X1TM.ap()[tc, :, :]

    def out_w(g, tc, src, semh):
        g.dma_start(c.y.ap()[tc * 128:(tc + 1) * 128, :], src).then_inc(semh, 16)

    emit_ln(c, "p9", c.FFOUT, in_b, out_w, fm_out=False)


def phase8(c):
    nc = c.nc
    big2 = c.BIG2.ap()
    big2f = big2.bitcast(F32)
    LOWW = big2[:, 24576:26624]
    WBC = big2[:, 26624:28672]
    GL = big2f[0:8, 0:2048]
    psq = c.PSQ.ap()
    psb = c.PSB.ap()
    W83 = c.sm_w8.ap().rearrange("p (a b) -> p a b", a=16)
    LGT3 = c.sm_lgt.ap().rearrange("p (a b) -> p a b", a=16)

    sL = _sem(c, "p8_l")
    sG = _sem(c, "p8_g")
    sT2 = _sem(c, "p8_t2")
    sE2 = _sem(c, "p8_e2")
    sW2 = _sem(c, "p8_w2")

    with nc.Block() as blk:
        @blk.sync
        def _(sy):
            for dc in range(ND):
                sy.dma_start(c.RES[:, dc, :], c.X1FM.ap()[dc, :, :]).then_inc(sL, 16)

        @blk.tensor
        def _(t):
            t.wait_ge(sL, 16 * ND)
            last = None
            for qt in range(NQT):
                for dc in range(ND):
                    last = t.matmul(psq[0:8, qt * 512:(qt + 1) * 512], c.GT[:, dc, :],
                                    c.RES[:, dc, qt * 512:(qt + 1) * 512],
                                    start=(dc == 0), stop=(dc == ND - 1))
            last.then_inc(sG, 1)
            t.wait_ge(sG, 2)
            for tc in range(NT):
                last = t.transpose(c.psbf[:, tc * 8:tc * 8 + 8],
                                   GL[:, tc * 128:(tc + 1) * 128], c.sm_idf.ap()[0:8, 0:8])
            last.then_inc(sG, 1)
            t.wait_ge(sT2, 2 * NT)
            for tc in range(NT):
                last = t.transpose(psb[0:8, tc * 128:(tc + 1) * 128], W83[:, tc, :],
                                   c.sm_idb.ap())
            last.then_inc(sW2, 1)
            t.wait_ge(sW2, 2)
            for qt in range(NQT):
                last = t.matmul(psq[:, qt * 512:(qt + 1) * 512], c.sm_exp8.ap(),
                                c.sm_wt8.ap()[:, qt * 512:(qt + 1) * 512],
                                start=True, stop=True)
            last.then_inc(sW2, 1)
            t.wait_ge(sW2, 4)
            for qt in range(NQT):
                for dc in range(ND):
                    last = t.matmul(psq[:, qt * 512:(qt + 1) * 512], c.AT[:, dc, :],
                                    c.RES[:, dc, qt * 512:(qt + 1) * 512],
                                    start=(dc == 0), stop=(dc == ND - 1))
            last.then_inc(sW2, 1)

        @blk.vector
        def _(v):
            v.wait_ge(sG, 1)
            v.tensor_copy(GL, psq[0:8, 0:2048]).then_inc(sG, 1)
            v.wait_ge(sG, 3)
            v.tensor_copy(c.sm_lgt.ap(), c.psbf[:, 0:128]).then_inc(sG, 1)
            v.wait_ge(sG, 4)
            for tc in range(NT):
                vw(c, v)
                vc(c, v.max(c.sm_m8.ap(), LGT3[:, tc, :]))
                vw(c, v)
                v.tensor_scalar(c.sm_ngm.ap(), c.sm_m8.ap()[:, 0:1], -1.0, None,
                                ALU.mult).then_inc(sT2, 1)
                v.wait_ge(sE2, tc + 1)
                vc(c, v.tensor_scalar(c.sm_sel.ap(), LGT3[:, tc, :],
                                      c.sm_m8.ap()[:, 1:2], None, ALU.is_ge))
                vw(c, v)
                vc(c, v.tensor_tensor(c.sm_psel.ap(), c.sm_p8e.ap(), c.sm_sel.ap(),
                                      ALU.mult))
                vw(c, v)
                vc(c, v.reduce_sum(c.sm_den8.ap(), c.sm_psel.ap(), axis=AX.X))
                vw(c, v)
                vc(c, v.reciprocal(c.sm_re8.ap(), c.sm_den8.ap()))
                vw(c, v)
                v.tensor_scalar(W83[:, tc, :], c.sm_psel.ap(), c.sm_re8.ap(), None,
                                ALU.mult).then_inc(sT2, 1)
            v.wait_ge(sW2, 1)
            v.tensor_copy(c.sm_wt8.ap(), psb[0:8, 0:2048]).then_inc(sW2, 1)
            v.wait_ge(sW2, 3)
            v.tensor_copy(WBC, psq[:, 0:2048]).then_inc(sW2, 1)
            v.wait_ge(sW2, 5)
            for qt in range(NQT):
                last = v.scalar_tensor_tensor(LOWW[:, qt * 512:(qt + 1) * 512],
                                              psq[:, qt * 512:(qt + 1) * 512], 2.0,
                                              WBC[:, qt * 512:(qt + 1) * 512],
                                              ALU.mult, ALU.mult)
            last.then_inc(sW2, 1)

        @blk.scalar
        def _(s):
            for tc in range(NT):
                s.wait_ge(sT2, 2 * tc + 1)
                s.activation(c.sm_p8e.ap(), LGT3[:, tc, :], AF.Exp,
                             bias=c.sm_ngm.ap()).then_inc(sE2, 1)

    emit_proj(c, "p8b", c.wb, c.sm_bbc.ap(), 1.0, False, "f32_tm", c.FFOUT,
              lora=True)


def kernel(L0_attn_hidden_states, L0_attn_q_weight, L0_attn_q_scale,
           L0_attn_k_weight, L0_attn_k_scale, L0_attn_v_weight, L0_attn_v_scale,
           L0_attn_o_weight, L0_attn_o_scale, L0_ffn_gate_w, L0_ffn_base_w,
           L0_ffn_base_b, L0_ffn_lora_A, L0_ffn_lora_B):
    hs = np.asarray(L0_attn_hidden_states)
    gt = np.ascontiguousarray(
        np.asarray(L0_ffn_gate_w, np.float32).T.reshape(32, 128, 8)
        .transpose(1, 0, 2).reshape(128, 256))
    lat = np.ascontiguousarray(
        np.asarray(L0_ffn_lora_A, np.float32).reshape(128, D).T
        .reshape(32, 128, 128).transpose(1, 0, 2).reshape(128, 4096))
    lbt = np.ascontiguousarray(
        np.asarray(L0_ffn_lora_B, np.float32).transpose(0, 2, 1).reshape(128, D))
    consts = dict(
        wq=np.ascontiguousarray(L0_attn_q_weight, dtype=np.float32),
        wk=np.ascontiguousarray(L0_attn_k_weight, dtype=np.float32),
        wv=np.ascontiguousarray(L0_attn_v_weight, dtype=np.float32),
        wo=np.ascontiguousarray(L0_attn_o_weight, dtype=np.float32),
        wb=np.ascontiguousarray(L0_ffn_base_w, dtype=np.float32),
        qsc_in=np.ascontiguousarray(np.asarray(L0_attn_q_scale, np.float32).reshape(ND, 128)),
        ksc_in=np.ascontiguousarray(np.asarray(L0_attn_k_scale, np.float32).reshape(HK, 128)),
        vsc_in=np.ascontiguousarray(np.asarray(L0_attn_v_scale, np.float32).reshape(HK, 128)),
        osc_in=np.ascontiguousarray(np.asarray(L0_attn_o_scale, np.float32).reshape(ND, 128)),
        bb_in=np.ascontiguousarray(np.asarray(L0_ffn_base_b, np.float32).reshape(ND, 128)),
        gt_in=gt, lat_in=lat, lbt_in=lbt,
        id_bf=np.eye(128, dtype=BF),
        id_f32=np.eye(128, dtype=np.float32),
        ones_row=np.ones((1, 128), dtype=BF),
        ones_col=np.ones((128, 1), dtype=BF),
        ones_sq_f32=np.ones((128, 128), dtype=np.float32),
        exp8=_exp8_const(),
    )
    in_maps = []
    for core in range(8):
        seq, half = core // 2, core % 2
        m = dict(consts)
        xs = np.ascontiguousarray(hs[seq], dtype=np.float32)
        if half == 1:
            xs = np.ascontiguousarray(np.roll(xs, -T, axis=0))
        m["x"] = xs
        in_maps.append(m)

    nc = build_program()
    res = run_bass_kernel_spmd(nc, in_maps, core_ids=list(range(8)))
    full = np.empty((4, S, D), np.float32)
    for core in range(8):
        seq, half = core // 2, core % 2
        full[seq, half * T:(half + 1) * T, :] = res.results[core]["y"]
    return full


def _exp8_const():
    e = np.zeros((8, 128), dtype=BF)
    for i in range(8):
        e[i, i * 16:(i + 1) * 16] = 1
    return e


# version 11
# speedup vs baseline: 1.2394x; 1.2394x over previous
import numpy as np
import ml_dtypes
from contextlib import ExitStack

import concourse.bass as bass
import concourse.mybir as mybir
from concourse.bass_utils import run_bass_kernel_spmd

F32 = mybir.dt.float32
BF16 = mybir.dt.bfloat16
AF = mybir.ActivationFunctionType
ALU = mybir.AluOpType
AX = mybir.AxisListType

D = 4096
S = 4096
T = 2048
KVD = 1024
H = 32
HK = 8
HD = 128
ND = 32
NTS = 32
NT = 16
NQT = 4
NP = 16
RSQ = float(1.0 / np.sqrt(HD))
NQF = float(-0.5 * np.sqrt(HD))
NKF = float(-0.5 / np.sqrt(HD))

BF = ml_dtypes.bfloat16


class Ctx:
    pass


class V:

    def __init__(self, ap):
        self._ap = ap

    def ap(self):
        return self._ap


def _sem(c, name):
    return c.st.enter_context(c.nc.semaphore(name))


def vc(c, inst):
    c.vcnt += 1
    inst.then_inc(c.semVC, 1)
    return inst


def vw(c, v):
    v.wait_ge(c.semVC, c.vcnt)


def sc_(c, inst):
    c.scnt += 1
    inst.then_inc(c.semSC, 1)
    return inst


def sw_(c, s):
    s.wait_ge(c.semSC, c.scnt)


class DSem:

    def __init__(self, c, name, n=2):
        self.n = n
        self.s = [_sem(c, name + chr(97 + j)) for j in range(n)]
        self.ub = 0
        self.acc = [0] * n

    def h(self, i):
        return self.s[(self.ub + i) % self.n]

    def v(self, i, per=16):
        return self.acc[(self.ub + i) % self.n] + per * (i // self.n + 1)

    def wait(self, eng, i, per=16):
        if i >= 0:
            eng.wait_ge(self.h(i), self.v(i, per))

    def close(self, n, per):
        for j in range(self.n):
            self.acc[j] += per * (n // self.n)
        self.ub += n


def _sb(c, name, shape, dt):
    return c.st.enter_context(c.nc.sbuf_tensor(name, shape, dt))


def build_program(n_phases=9, dbg=()):
    nc = bass.Bass("TRN2", target_bir_lowering=False)
    c = Ctx()
    c.nc = nc

    def dram(name, shape, dt, kind="ExternalInput"):
        if name in dbg and kind == "Internal" and dt == F32:
            kind = "ExternalOutput"
        return nc.dram_tensor(name, shape, dt, kind=kind)

    c.x = dram("x", [S, D], F32)
    c.wq = dram("wq", [D, D], F32)
    c.wk = dram("wk", [KVD, D], F32)
    c.wv = dram("wv", [KVD, D], F32)
    c.wo = dram("wo", [D, D], F32)
    c.wb = dram("wb", [D, D], F32)
    c.qsc_in = dram("qsc_in", [ND, 128], F32)
    c.ksc_in = dram("ksc_in", [HK, 128], F32)
    c.vsc_in = dram("vsc_in", [HK, 128], F32)
    c.osc_in = dram("osc_in", [ND, 128], F32)
    c.bb_in = dram("bb_in", [ND, 128], F32)
    c.gt_in = dram("gt_in", [128, 256], F32)
    c.lat_in = dram("lat_in", [128, 4096], F32)
    c.lbt_in = dram("lbt_in", [128, 4096], F32)
    c.id_bf = dram("id_bf", [128, 128], BF16)
    c.id_f32 = dram("id_f32", [128, 128], F32)
    c.ones_row = dram("ones_row", [1, 128], BF16)
    c.ones_col = dram("ones_col", [128, 1], BF16)
    c.ones_sq_f32 = dram("ones_sq_f32", [128, 128], F32)
    c.exp8 = dram("exp8", [8, 128], BF16)
    c.y = dram("y", [T, D], F32, kind="ExternalOutput")

    c.XFM = dram("XFM", [128, NTS, ND, 128], BF16, kind="Internal")
    c.QS = dram("QS", [ND, 128, T], BF16, kind="Internal")
    c.KSCR = dram("KSCR", [HK, 128, S], BF16, kind="Internal")
    c.VSCR = dram("VSCR", [HK, 128, NTS, 128], BF16, kind="Internal")
    c.OOUT = dram("OOUT", [NT, 128, D], F32, kind="Internal")
    c.X1TM = dram("X1TM", [NT, 128, D], F32, kind="Internal")
    c.X1FM = dram("X1FM", [ND, 128, T], BF16, kind="Internal")
    c.FFOUT = dram("FFOUT", [NT, 128, D], F32, kind="Internal")
    c.ROWSCR = dram("ROWSCR", [8, 128], F32, kind="Internal")

    st = ExitStack()
    c.st = st
    c.semVC = st.enter_context(nc.semaphore("chainV"))
    c.vcnt = 0
    c.semSC = st.enter_context(nc.semaphore("chainS"))
    c.scnt = 0

    c.BIG1 = _sb(c, "BIG1", [128, 65536], BF16)
    c.BIG2 = _sb(c, "BIG2", [128, 28672], BF16)
    c.PREP = _sb(c, "PREP", [128, 8448], BF16)
    c.sm_idb = _sb(c, "sm_idb", [128, 128], BF16)
    c.sm_idf = _sb(c, "sm_idf", [128, 128], F32)
    c.sm_ones_row = _sb(c, "sm_ones_row", [1, 128], BF16)
    c.sm_ones_col = _sb(c, "sm_ones_col", [128, 1], BF16)
    c.sm_onesqf = _sb(c, "sm_onesqf", [128, 128], F32)
    c.sm_recl = _sb(c, "sm_recl", [1, 512], F32)
    c.sm_recb = _sb(c, "sm_recb", [1, 512], BF16)
    c.sm_exp8 = _sb(c, "sm_exp8", [8, 128], BF16)
    c.sm_qsc = _sb(c, "sm_qsc", [128, 32], F32)
    c.sm_osc = _sb(c, "sm_osc", [128, 32], F32)
    c.sm_bbc = _sb(c, "sm_bbc", [128, 32], F32)
    c.sm_kscin = _sb(c, "sm_kscin", [128, 8], F32)
    c.sm_vscin = _sb(c, "sm_vscin", [128, 8], F32)
    c.sm_kscale = _sb(c, "sm_kscale", [128, 8], F32)
    c.sm_vscale = _sb(c, "sm_vscale", [128, 8], F32)
    c.sm_nk5 = _sb(c, "sm_nk5", [1, 8], F32)
    c.sm_nq32 = _sb(c, "sm_nq32", [1, 32], F32)
    c.sm_sacc = _sb(c, "sm_sacc", [128, 1], F32)
    c.sm_stmp = _sb(c, "sm_stmp", [128, 1], F32)
    c.sm_scl = _sb(c, "sm_scl", [128, 1], F32)
    c.sm_scl2 = _sb(c, "sm_scl2", [128, 2], F32)
    c.sm_negu = _sb(c, "sm_negu", [128, 2], F32)
    c.sm_t1 = _sb(c, "sm_t1", [1, 8], F32)
    c.sm_t1b = _sb(c, "sm_t1b", [1, 8], BF16)
    c.sm_bn = _sb(c, "sm_bn", [128, 48], F32)
    c.sm_mv = _sb(c, "sm_mv", [128, 2], F32)
    c.sm_var = _sb(c, "sm_var", [128, 1], F32)
    c.sm_sd = _sb(c, "sm_sd", [128, 1], F32)
    c.sm_rstd = _sb(c, "sm_rstd", [128, 1], F32)
    c.sm_nmr = _sb(c, "sm_nmr", [128, 1], F32)
    c.sm_lgt = _sb(c, "sm_lgt", [128, 128], F32)
    c.sm_m8 = _sb(c, "sm_m8", [128, 8], F32)
    c.sm_ngm = _sb(c, "sm_ngm", [128, 1], F32)
    c.sm_p8e = _sb(c, "sm_p8e", [128, 8], F32)
    c.sm_sel = _sb(c, "sm_sel", [128, 8], F32)
    c.sm_psel = _sb(c, "sm_psel", [128, 8], F32)
    c.sm_den8 = _sb(c, "sm_den8", [128, 1], F32)
    c.sm_re8 = _sb(c, "sm_re8", [128, 1], F32)
    c.sm_w8 = _sb(c, "sm_w8", [128, 128], BF16)
    big2v = c.BIG2.ap()
    big2vf = big2v.bitcast(F32)
    big1v = c.BIG1.ap()
    c.sm_vscb = V(big2v[:, 22528:23552])
    c.sm_svrow = V(big2v[0:1, 26624:27648])
    c.sm_accv = V(big2vf[:, 11776:12800])
    c.sm_accg = V(big2vf[:, 12800:13824])
    c.sm_bcs = V(big2vf[:, 13824:14336])
    c.sm_wt8 = V(big2v[0:8, 0:2048])

    c.PSQ = st.enter_context(nc.psum_tensor("PSQ", [128, 2048], F32))
    c.PSA = st.enter_context(nc.psum_tensor("PSA", [128, 512], F32))
    c.PSD = st.enter_context(nc.psum_tensor("PSD", [128, 512], F32))
    c.PSB = st.enter_context(nc.psum_tensor("PSB", [128, 2048], BF16))

    c.AT = c.PREP.ap()[:, 0:4096].rearrange("p (a b) -> p a b", a=32)
    c.BT = c.PREP.ap()[:, 4096:8192].rearrange("p (a b) -> p a b", a=32)
    c.GT = c.PREP.ap()[:, 8192:8448].rearrange("p (a b) -> p a b", a=32)
    c.RES = c.BIG1.ap()[:, 0:65536].rearrange("p (a b) -> p a b", a=32)
    c.psbf = c.PSB.ap().bitcast(F32)

    try:
        phase0(c)
        if n_phases >= 1:
            phase1(c)
        if n_phases >= 2:
            phase2(c)
        if n_phases >= 3:
            phase34(c)
        if n_phases >= 5:
            phase5(c)
        if n_phases >= 6:
            phase6(c)
        if n_phases >= 7:
            phase7(c)
        if n_phases >= 8:
            phase8(c)
        if n_phases >= 9:
            phase9(c)
        else:
            _stub_output(c)
        _debug_dumps(c, dbg)
    finally:
        st.close()
    nc.finalize()
    return nc


def _debug_dumps(c, dbg):
    shapes = {"XFM": (c.XFM, 32, 4096), "QS": (c.QS, 32, 2048),
              "KSCR": (c.KSCR, 8, 4096), "VSCR": (c.VSCR, 8, 4096),
              "X1FM": (c.X1FM, 32, 2048)}
    todo = [nm for nm in dbg if nm in shapes]
    if not todo:
        return
    nc = c.nc
    for nm in todo:
        src, outer, inner = shapes[nm]
        out = nc.dram_tensor(nm + "_f32", [outer, 128, inner], F32, kind="ExternalOutput")
        with nc.semaphore("dbgl_" + nm) as semL, \
                nc.semaphore("dbgc_" + nm) as semC, \
                nc.semaphore("dbgs_" + nm) as semS, nc.Block() as blk:
            bufb = c.BIG2.ap()[:, 0:inner]
            buff = c.BIG2.ap().bitcast(F32)[:, 4096:4096 + inner]

            @blk.sync
            def _(sy):
                for o in range(outer):
                    if o >= 1:
                        sy.wait_ge(semC, o)
                    if nm == "XFM":
                        sy.dma_start(bufb, src.ap()[:, o, :, :]
                                     .rearrange("p a b -> p (a b)")).then_inc(semL, 16)
                    elif nm == "VSCR":
                        sy.dma_start(bufb, src.ap()[o, :, :, :]
                                     .rearrange("p a b -> p (a b)")).then_inc(semL, 16)
                    else:
                        sy.dma_start(bufb, src.ap()[o, :, :]).then_inc(semL, 16)

            @blk.vector
            def _(v):
                for o in range(outer):
                    v.wait_ge(semL, 16 * (o + 1))
                    if o >= 1:
                        v.wait_ge(semS, 16 * o)
                    v.tensor_copy(buff, bufb).then_inc(semC, 1)

            @blk.gpsimd
            def _(g):
                for o in range(outer):
                    g.wait_ge(semC, o + 1)
                    g.dma_start(out.ap()[o, :, :], buff).then_inc(semS, 16)
                    g.wait_ge(semS, 16 * (o + 1))


def _stub_output(c):
    nc = c.nc
    with nc.semaphore("stub_sem") as sem, nc.semaphore("stub_dma") as semd, \
            nc.Block() as blk:
        @blk.vector
        def _(v):
            z = c.BIG1.ap().bitcast(F32)[:, 0:4096]
            v.memset(z, 0.0).then_inc(sem, 1)

        @blk.gpsimd
        def _(g):
            g.wait_ge(sem, 1)
            z = c.BIG1.ap().bitcast(F32)[:, 0:4096]
            for tc in range(NT):
                g.dma_start(c.y.ap()[tc * 128:(tc + 1) * 128, :], z).then_inc(semd, 16)
                g.wait_ge(semd, 16 * (tc + 1))


def phase0(c):
    nc = c.nc
    sIN = _sem(c, "p0_in")
    sC = _sem(c, "p0_c")
    big1f = c.BIG1.ap().bitcast(F32)

    with nc.Block() as blk:
        @blk.sync
        def _(sy):
            n = 0
            for dst, src in [
                (c.sm_idb.ap(), c.id_bf.ap()),
                (c.sm_idf.ap(), c.id_f32.ap()),
                (c.sm_ones_row.ap(), c.ones_row.ap()),
                (c.sm_ones_col.ap(), c.ones_col.ap()),
                (c.sm_onesqf.ap(), c.ones_sq_f32.ap()),
                (c.sm_exp8.ap(), c.exp8.ap()),
            ]:
                sy.dma_start(dst, src).then_inc(sIN, 16)
                n += 16
            for i, src in enumerate([c.qsc_in, c.osc_in, c.bb_in]):
                sy.dma_start(big1f[0:32, i * 128:(i + 1) * 128], src.ap()).then_inc(sIN, 16)
                n += 16
            for i, src in enumerate([c.ksc_in, c.vsc_in]):
                sy.dma_start(big1f[0:8, 384 + i * 128:384 + (i + 1) * 128],
                             src.ap()).then_inc(sIN, 16)
                n += 16
            sy.dma_start(big1f[:, 1024:1280], c.gt_in.ap()).then_inc(sIN, 16)
            n += 16
            sy.dma_start(big1f[:, 2048:6144], c.lat_in.ap()).then_inc(sIN, 16)
            n += 16
            sy.dma_start(big1f[:, 6144:10240], c.lbt_in.ap()).then_inc(sIN, 16)
            n += 16
            c.p0_n_in = n

        @blk.tensor
        def _(t):
            t.wait_ge(sIN, c.p0_n_in)
            pf = c.psbf
            t.transpose(pf[:, 0:32], big1f[0:32, 0:128], c.sm_idf.ap()[0:32, 0:32])
            t.transpose(pf[:, 32:64], big1f[0:32, 128:256], c.sm_idf.ap()[0:32, 0:32])
            t.transpose(pf[:, 64:96], big1f[0:32, 256:384], c.sm_idf.ap()[0:32, 0:32])
            t.transpose(pf[:, 96:104], big1f[0:8, 384:512], c.sm_idf.ap()[0:8, 0:8])
            t.transpose(pf[:, 104:112], big1f[0:8, 512:640],
                        c.sm_idf.ap()[0:8, 0:8]).then_inc(sC, 1)

        @blk.vector
        def _(v):
            v.wait_ge(sC, 1)
            v.wait_ge(sIN, c.p0_n_in)
            pf = c.psbf
            v.tensor_copy(c.sm_qsc.ap(), pf[:, 0:32])
            v.tensor_copy(c.sm_osc.ap(), pf[:, 32:64])
            v.tensor_copy(c.sm_bbc.ap(), pf[:, 64:96])
            v.tensor_copy(c.sm_kscin.ap(), pf[:, 96:104])
            v.tensor_copy(c.sm_vscin.ap(), pf[:, 104:112])
            v.tensor_copy(c.GT.rearrange("p a b -> p (a b)"), big1f[:, 1024:1280])
            v.tensor_copy(c.AT.rearrange("p a b -> p (a b)"), big1f[:, 2048:6144])
            v.tensor_copy(c.BT.rearrange("p a b -> p (a b)"),
                          big1f[:, 6144:10240]).then_inc(sC, 1)


def phase1(c):
    nc = c.nc
    sX = DSem(c, "p1_x")
    sCst = _sem(c, "p1_cst")
    sTp = _sem(c, "p1_tp")
    sEv = _sem(c, "p1_ev")
    sMy = _sem(c, "p1_my")
    sOut = DSem(c, "p1_out")

    big2 = c.BIG2.ap()
    big2f = big2.bitcast(F32)
    XST = [big2f[:, 0:4096], big2f[:, 4096:8192]]
    XC = big2[:, 16384:20480]
    XSTG = [big2[:, 20480:24576], big2[:, 24576:28672]]
    psb = c.PSB.ap()

    with nc.Block() as blk:
        @blk.sync
        def _(sy):
            for tc in range(NTS):
                b = tc % 2
                if tc >= 2:
                    sy.wait_ge(sCst, tc - 1)
                sy.dma_start(XST[b],
                             c.x.ap()[tc * 128:(tc + 1) * 128, :]).then_inc(sX.h(tc), 16)

        @blk.scalar
        def _(s):
            for tc in range(NTS):
                b = tc % 2
                sX.wait(s, tc)
                if tc >= 1:
                    s.wait_ge(sTp, 4 * tc)
                s.activation(XC, XST[b], AF.Copy).then_inc(sCst, 1)

        @blk.tensor
        def _(t):
            for tc in range(NTS):
                b = tc % 2
                t.wait_ge(sCst, tc + 1)
                for g in range(4):
                    gi = tc * 4 + g
                    pb = psb[:, (gi % 2) * 1024:(gi % 2) * 1024 + 1024]
                    if gi >= 2:
                        t.wait_ge(sEv, gi - 1)
                    last = None
                    for j in range(8):
                        dc = g * 8 + j
                        last = t.transpose(pb[:, j * 128:(j + 1) * 128],
                                           XC[:, dc * 128:(dc + 1) * 128], c.sm_idb.ap())
                    last.then_inc(sTp, 1)

        @blk.vector
        def _(v):
            for tc in range(NTS):
                b = tc % 2
                for g in range(4):
                    gi = tc * 4 + g
                    pb = psb[:, (gi % 2) * 1024:(gi % 2) * 1024 + 1024]
                    v.wait_ge(sTp, gi + 1)
                    if tc >= 2 and g == 0:
                        sOut.wait(v, tc - 2)
                    v.tensor_copy(
                        XSTG[b].rearrange("p (a b) -> p a b", a=32)[:, g * 8:(g + 1) * 8, :],
                        pb.rearrange("p (a b) -> p a b", a=8)).then_inc(sEv, 1)
                if tc < NT:
                    v.wait_ge(sEv, 4 * (tc + 1))
                    v.tensor_copy(c.RES[:, :, tc * 128:(tc + 1) * 128],
                                  XSTG[b].rearrange("p (a b) -> p a b", a=32)).then_inc(sMy, 1)

        @blk.gpsimd
        def _(g):
            nmy = 0
            for tc in range(NTS):
                b = tc % 2
                g.wait_ge(sEv, 4 * (tc + 1))
                if tc < NT:
                    nmy += 1
                    g.wait_ge(sMy, nmy)
                g.dma_start(c.XFM.ap()[:, tc, :, :],
                            XSTG[b].rearrange("p (a b) -> p a b", a=32)
                            ).then_inc(sOut.h(tc), 16)
            sOut.wait(g, NTS - 2)
            sOut.wait(g, NTS - 1)


class ProjSems:

    KEYS = ["ab", "sg", "tp", "te", "mm", "ev", "ot", "oe", "qsq", "nqm",
            "nqr", "nqd", "pre"]

    def __init__(self, c):
        self.h = {k: _sem(c, "proj_" + k) for k in self.KEYS}
        self.base = {k: 0 for k in self.KEYS}
        self.dW = DSem(c, "proj_w", n=4)
        self.dD = DSem(c, "proj_dm")

    def wg(self, eng, k, v):
        if self.base[k] + v > 0:
            eng.wait_ge(self.h[k], self.base[k] + v)

    def inc(self, inst, k, n=1):
        inst.then_inc(self.h[k], n)

    def fin(self, used):
        for k, n in used.items():
            self.base[k] += n


def emit_proj(c, name, w_dram, scale_col, const_scale, has_sign, out_mode,
              out_dram, preload=None, n_preload=0, lora=False, qnorms=False,
              guard=None):
    nc = c.nc
    if not hasattr(c, "projsems"):
        c.projsems = ProjSems(c)
    P = c.projsems
    sW = P.dW
    sDm = P.dD
    gsem, gval = guard() if guard is not None else (None, 0)

    big2 = c.BIG2.ap()
    big2f = big2.bitcast(F32)
    WST = [big2f[:, j * 1024:(j + 1) * 1024] for j in range(4)]
    WSG = [big2[:, 8192 + j * 1024:8192 + (j + 1) * 1024] for j in range(4)]
    WT = [big2[:, 12288:16384].rearrange("p (a b) -> p a b", a=32),
          big2[:, 16384:20480].rearrange("p (a b) -> p a b", a=32)]
    QEVA = [big2[:, 20480:22528], big2[:, 22528:24576]]
    QSQ2 = [big2[:, 24576:26624], big2[:, 26624:28672]]
    OEVA = [big2f[:, 10240 + q * 512:10240 + (q + 1) * 512] for q in range(4)]
    OSTG = [big2f[:, 13312:13824], big2f[:, 13824:14336]]
    psq = c.PSQ.ap()
    psb = c.PSB.ap()
    pso = [c.PSA.ap(), c.PSD.ap()]
    psn = [c.PSA.ap()[0:1, 0:512], c.PSD.ap()[0:1, 0:512]]
    f32_tm = out_mode == "f32_tm"
    LOWW = big2[:, 24576:26624] if lora else None

    with nc.Block() as blk:
        @blk.sync
        def _(sy):
            if gsem is not None:
                sy.wait_ge(gsem, gval)
            if preload is not None:
                preload(sy, P.h["pre"])
            for do in range(ND):
                for qq in range(4):
                    i = do * 4 + qq
                    b = i % 4
                    if i >= 4:
                        if has_sign:
                            P.wg(sy, "ab", i - 3)
                        P.wg(sy, "sg", i - 3)
                    sy.dma_start(WST[b],
                                 w_dram.ap()[do * 128:(do + 1) * 128,
                                             qq * 1024:(qq + 1) * 1024]
                                 ).then_inc(sW.h(i), 16)

        @blk.scalar
        def _(s):
            if gsem is not None:
                s.wait_ge(gsem, gval)
            func = AF.Sign if has_sign else AF.Copy
            for do in range(ND):
                for qq in range(4):
                    i = do * 4 + qq
                    sW.wait(s, i)
                    if i >= 3:
                        P.wg(s, "tp", i - 2)
                    P.inc(s.activation(WSG[i % 4], WST[i % 4], func), "sg")

        @blk.tensor
        def _(t):
            if gsem is not None:
                t.wait_ge(gsem, gval)
            if preload is not None:
                t.wait_ge(P.h["pre"], P.base["pre"] + n_preload)
            for do in range(ND):
                bdo = do % 2
                for qq in range(4):
                    i = do * 4 + qq
                    P.wg(t, "sg", i + 1)
                    if i >= 2:
                        P.wg(t, "te", i - 1)
                    pb = psb[:, (i % 2) * 1024:(i % 2) * 1024 + 1024]
                    last = None
                    for j in range(8):
                        last = t.transpose(pb[:, j * 128:(j + 1) * 128],
                                           WSG[i % 4][:, j * 128:(j + 1) * 128],
                                           c.sm_idb.ap())
                    P.inc(last, "tp")
                if f32_tm and do >= 1:
                    _proj_out_tr(c, t, do - 1, OEVA, pso, P)
                if do >= 1:
                    P.wg(t, "ev", 4 * do)
                last = None
                for dc in range(ND):
                    if dc % 8 == 0:
                        P.wg(t, "te", 4 * do + dc // 8 + 1)
                    for qt in range(NQT):
                        last = t.matmul(psq[:, qt * 512:(qt + 1) * 512],
                                        WT[bdo][:, dc, :],
                                        c.RES[:, dc, qt * 512:(qt + 1) * 512],
                                        start=(dc == 0),
                                        stop=(dc == ND - 1 and not lora))
                if lora:
                    for qt in range(NQT):
                        last = t.matmul(psq[:, qt * 512:(qt + 1) * 512],
                                        c.BT[:, do, :],
                                        LOWW[:, qt * 512:(qt + 1) * 512],
                                        start=False, stop=True)
                P.inc(last, "mm")
                if qnorms and do >= 2:
                    _proj_nq_mm(c, t, do - 2, QSQ2, psn, P)
            if qnorms:
                _proj_nq_mm(c, t, ND - 2, QSQ2, psn, P)
                _proj_nq_mm(c, t, ND - 1, QSQ2, psn, P)
            if f32_tm:
                _proj_out_tr(c, t, ND - 1, OEVA, pso, P)

        @blk.vector
        def _(v):
            if gsem is not None:
                v.wait_ge(gsem, gval)
            for do in range(ND):
                if do >= 1:
                    _proj_out_evac(c, v, do - 1, out_mode, psq, QEVA, OEVA, OSTG, pso,
                                   P, sDm, has_sign, scale_col, qnorms, QSQ2, psn)
                for qq in range(4):
                    i = do * 4 + qq
                    P.wg(v, "tp", i + 1)
                    if do >= 2 and qq == 0:
                        P.wg(v, "mm", do - 1)
                    P.inc(v.tensor_copy(WT[do % 2][:, qq * 8:(qq + 1) * 8, :],
                                        psb[:, (i % 2) * 1024:(i % 2) * 1024 + 1024]
                                        .rearrange("p (a b) -> p a b", a=8)), "te")
                if has_sign:
                    for qq in range(4):
                        i = do * 4 + qq
                        sW.wait(v, i)
                        if i >= 1:
                            P.wg(v, "ab", i)
                        if qq == 0:
                            vw(c, v)
                            P.inc(v.reduce_sum(c.sm_sacc.ap(), WST[i % 4], axis=AX.X,
                                               apply_absolute_value=True), "ab")
                        else:
                            vc(c, v.reduce_sum(c.sm_stmp.ap(), WST[i % 4], axis=AX.X,
                                               apply_absolute_value=True))
                            vw(c, v)
                            P.inc(v.tensor_add(c.sm_sacc.ap(), c.sm_sacc.ap(),
                                               c.sm_stmp.ap()), "ab")
                    P.wg(v, "ab", 4 * do + 4)
                    vc(c, v.tensor_scalar(c.sm_scl.ap(), c.sm_sacc.ap(), 1.0 / 4096.0,
                                          1e-8, ALU.mult, ALU.max))
                    if do >= 2:
                        P.wg(v, "ev", 4 * do - 4)
                    vw(c, v)
                    vc(c, v.tensor_scalar(c.sm_scl2.ap()[:, do % 2:do % 2 + 1],
                                          c.sm_scl.ap(), scale_col[:, do:do + 1],
                                          const_scale, ALU.mult, ALU.mult))
                if qnorms and do >= 1:
                    _proj_qnorm_sq(c, v, do - 1, QEVA, QSQ2, psn, P)
            _proj_out_evac(c, v, ND - 1, out_mode, psq, QEVA, OEVA, OSTG, pso,
                           P, sDm, has_sign, scale_col, qnorms, QSQ2, psn)
            if qnorms:
                _proj_qnorm_sq(c, v, ND - 1, QEVA, QSQ2, psn, P)
                _proj_qnorm_red(c, v, ND - 1, psn, P)

        @blk.gpsimd
        def _(g):
            if out_mode == "bf16_fm":
                for do in range(ND):
                    P.wg(g, "ev", 4 * do + 4)
                    g.dma_start(out_dram.ap()[do, :, :],
                                QEVA[do % 2]).then_inc(sDm.h(do), 16)
                sDm.wait(g, ND - 2)
                sDm.wait(g, ND - 1)
            else:
                for do in range(ND):
                    for qt in range(NQT):
                        oq = do * 4 + qt
                        P.wg(g, "oe", oq + 1)
                        b = oq % 2
                        for j in range(4):
                            tc = qt * 4 + j
                            g.dma_start(out_dram.ap()[tc, :, do * 128:(do + 1) * 128],
                                        OSTG[b][:, j * 128:(j + 1) * 128]
                                        ).then_inc(sDm.h(oq), 16)
                sDm.wait(g, 4 * ND - 2, per=64)
                sDm.wait(g, 4 * ND - 1, per=64)

    used = {"sg": 128, "tp": 128, "te": 128, "mm": 32, "ev": 4 * ND}
    if has_sign:
        used["ab"] = 128
    if f32_tm:
        used["ot"] = 4 * ND
        used["oe"] = 4 * ND
    if qnorms:
        used["qsq"] = ND
        used["nqm"] = 4 * ND
        used["nqr"] = ND
        used["nqd"] = 4 * ND
        c.sNqr = P.h["nqr"]
        c.nqr_val = P.base["nqr"] + ND
    if preload is not None:
        used["pre"] = n_preload
    sW.close(128, 16)
    if out_mode == "bf16_fm":
        sDm.close(ND, 16)
    else:
        sDm.close(4 * ND, 64)
    P.fin(used)


def _proj_qnorm_sq(c, v, d, QEVA, QSQ2, psn, P):
    P.wg(v, "ev", 4 * d + 4)
    if d >= 2:
        P.wg(v, "nqm", 4 * (d - 1))
    P.inc(v.tensor_tensor(QSQ2[d % 2], QEVA[d % 2], QEVA[d % 2], ALU.mult), "qsq")
    if d >= 1:
        _proj_qnorm_red(c, v, d - 1, psn, P)


def _proj_nq_mm(c, t, dom, QSQ2, psn, P):
    P.wg(t, "qsq", dom + 1)
    if dom >= 1:
        P.wg(t, "nqr", dom)
    for qt in range(NQT):
        if qt >= 2:
            P.wg(t, "nqd", 4 * dom + qt - 1)
        P.inc(t.matmul(psn[qt % 2], c.sm_ones_col.ap(),
                       QSQ2[dom % 2][:, qt * 512:(qt + 1) * 512],
                       start=True, stop=True), "nqm")


def _proj_out_tr(c, t, do, OEVA, pso, P):
    for qt in range(NQT):
        oq = do * 4 + qt
        P.wg(t, "ev", 4 * do + qt + 1)
        if oq >= 2:
            P.wg(t, "oe", oq - 1)
        last = None
        pf = pso[oq % 2]
        for j in range(4):
            last = t.transpose(pf[:, j * 128:(j + 1) * 128],
                               OEVA[qt][:, j * 128:(j + 1) * 128], c.sm_idf.ap())
        P.inc(last, "ot")


def _proj_qnorm_red(c, v, do, psn, P):
    for qt in range(NQT):
        P.wg(v, "nqm", 4 * do + qt + 1)
        vw(c, v)
        if qt == 0:
            P.inc(v.tensor_reduce(c.sm_t1.ap()[:, 0:1], psn[0], AX.X, ALU.max), "nqd")
        else:
            P.inc(v.tensor_reduce(c.sm_t1.ap()[:, 1:2], psn[qt % 2], AX.X,
                                  ALU.max), "nqd")
            P.wg(v, "nqd", 4 * do + qt + 1)
            vc(c, v.tensor_max(c.sm_t1.ap()[:, 0:1], c.sm_t1.ap()[:, 0:1],
                               c.sm_t1.ap()[:, 1:2]))
    P.wg(v, "nqd", 4 * do + 1)
    vw(c, v)
    P.inc(v.tensor_scalar(c.sm_nq32.ap()[:, do:do + 1], c.sm_t1.ap()[:, 0:1],
                          NQF, None, ALU.mult), "nqr")


def _proj_out_evac(c, v, do, out_mode, psq, QEVA, OEVA, OSTG, pso,
                   P, sDm, has_sign, scale_col, qnorms, QSQ2, psn):
    if out_mode == "bf16_fm":
        P.wg(v, "mm", do + 1)
        vw(c, v)
        b = do % 2
        if do >= 2:
            sDm.wait(v, do - 2)
        for qt in range(NQT):
            P.inc(v.tensor_scalar(QEVA[b][:, qt * 512:(qt + 1) * 512],
                                  psq[:, qt * 512:(qt + 1) * 512],
                                  c.sm_scl2.ap()[:, b:b + 1], None, ALU.mult), "ev")

    else:
        if do >= 1:
            for qt in range(NQT):
                oqp = (do - 1) * 4 + qt
                P.wg(v, "ot", oqp + 1)
                if oqp >= 2:
                    sDm.wait(v, oqp - 2, per=64)
                P.inc(v.tensor_copy(OSTG[oqp % 2], pso[oqp % 2]), "oe")
        P.wg(v, "mm", do + 1)
        vw(c, v)
        for qt in range(NQT):
            if has_sign:
                P.inc(v.tensor_scalar(OEVA[qt], psq[:, qt * 512:(qt + 1) * 512],
                                      c.sm_scl2.ap()[:, do % 2:do % 2 + 1], None,
                                      ALU.mult), "ev")
            else:
                P.inc(v.tensor_scalar(OEVA[qt], psq[:, qt * 512:(qt + 1) * 512],
                                      scale_col[:, do:do + 1], None, ALU.add), "ev")
        if do == ND - 1:
            for qt in range(NQT):
                oq = (ND - 1) * 4 + qt
                P.wg(v, "ot", oq + 1)
                sDm.wait(v, oq - 2, per=64)
                P.inc(v.tensor_copy(OSTG[oq % 2], pso[oq % 2]), "oe")


def phase2(c):
    emit_proj(c, "p2", c.wq, c.sm_qsc.ap(), RSQ, True, "bf16_fm", c.QS, qnorms=True)


def phase6(c):
    emit_proj(c, "p6", c.wo, c.sm_osc.ap(), 1.0, True, "f32_tm", c.OOUT,
              guard=lambda: (c.sAo5, H * NQT))


def phase34(c):
    nc = c.nc
    sW = DSem(c, "p34_w")
    sAb = _sem(c, "p34_ab")
    sSg = _sem(c, "p34_sg")
    sTp = _sem(c, "p34_tp")
    sTe = _sem(c, "p34_te")
    sSc = _sem(c, "p34_sc")
    sR = _sem(c, "p34_row")
    sX = DSem(c, "p34_x")
    sMM = _sem(c, "p34_mm")
    sEv = _sem(c, "p34_ev")
    sKt = _sem(c, "p34_kt")
    sKf = _sem(c, "p34_kf")
    sKsq = _sem(c, "p34_ksq")
    sNKm = _sem(c, "p34_nkm")
    sNKr = _sem(c, "p34_nkr")
    c.sNKr = sNKr
    sVd = DSem(c, "p34_vd")
    sKd = DSem(c, "p34_kd")

    big1 = c.BIG1.ap()
    big2 = c.BIG2.ap()
    big2f = big2.bitcast(F32)
    WKT = big1[:, 0:32768].rearrange("p (a b) -> p a b", a=32)
    WVT = big1[:, 32768:65536].rearrange("p (a b) -> p a b", a=32)
    WST = [big2f[:, 0:1024], big2f[:, 1024:2048]]
    WSG = [big2[:, 4096:5120], big2[:, 5120:6144]]
    XT = [big2[:, 6144:10240].rearrange("p (a b) -> p a b", a=32),
          big2[:, 10240:14336].rearrange("p (a b) -> p a b", a=32)]
    KTM = [big2[:, 14336:15360], big2[:, 15360:16384]]
    VTM = [big2[:, 16384:17408], big2[:, 17408:18432]]
    KFM = [big2[:, 18432:19456], big2[:, 19456:20480]]
    KSQB = [big2[:, 20480:21504], big2[:, 21504:22528]]
    psq = c.PSQ.ap()
    psb = c.PSB.ap()
    psn = [c.PSA.ap()[0:1, 0:512], c.PSD.ap()[0:1, 0:512]]

    with nc.Block() as blk:
        @blk.sync
        def _(sy):
            i = 0
            for w in range(2):
                wd = c.wk if w == 0 else c.wv
                for rc in range(8):
                    for qq in range(4):
                        if i >= 2:
                            sy.wait_ge(sAb, i - 1)
                            sy.wait_ge(sSg, i - 1)
                        sy.dma_start(WST[i % 2],
                                     wd.ap()[rc * 128:(rc + 1) * 128,
                                             qq * 1024:(qq + 1) * 1024]
                                     ).then_inc(sW.h(i), 16)
                        i += 1

        @blk.scalar
        def _(s):
            for i in range(64):
                sW.wait(s, i)
                if i >= 1:
                    s.wait_ge(sTp, i)
                s.activation(WSG[i % 2], WST[i % 2], AF.Sign).then_inc(sSg, 1)

        @blk.tensor
        def _(t):
            for i in range(64):
                t.wait_ge(sSg, i + 1)
                if i >= 1:
                    t.wait_ge(sTe, i)
                pb = psb[:, 0:1024]
                last = None
                for j in range(8):
                    last = t.transpose(pb[:, j * 128:(j + 1) * 128],
                                       WSG[i % 2][:, j * 128:(j + 1) * 128], c.sm_idb.ap())
                last.then_inc(sTp, 1)

        @blk.vector
        def _(v):
            i = 0
            for w in range(2):
                WTd = WKT if w == 0 else WVT
                for rc in range(8):
                    for qq in range(4):
                        sW.wait(v, i)
                        if i >= 1:
                            v.wait_ge(sAb, i)
                        if qq == 0:
                            vw(c, v)
                            v.reduce_sum(c.sm_sacc.ap(), WST[i % 2], axis=AX.X,
                                         apply_absolute_value=True).then_inc(sAb, 1)
                        else:
                            vc(c, v.reduce_sum(c.sm_stmp.ap(), WST[i % 2], axis=AX.X,
                                               apply_absolute_value=True))
                            vw(c, v)
                            v.tensor_add(c.sm_sacc.ap(), c.sm_sacc.ap(),
                                         c.sm_stmp.ap()).then_inc(sAb, 1)
                        v.wait_ge(sTp, i + 1)
                        v.tensor_copy(WTd[:, qq * 8:(qq + 1) * 8, rc * 128:(rc + 1) * 128],
                                      psb[:, 0:1024].rearrange("p (a b) -> p a b", a=8)
                                      ).then_inc(sTe, 1)
                        i += 1
                    v.wait_ge(sAb, i)
                    vc(c, v.tensor_scalar(c.sm_scl.ap(), c.sm_sacc.ap(), 1.0 / 4096.0,
                                          1e-8, ALU.mult, ALU.max))
                    dst = c.sm_kscale.ap() if w == 0 else c.sm_vscale.ap()
                    scol = c.sm_kscin.ap() if w == 0 else c.sm_vscin.ap()
                    vw(c, v)
                    lastsc = v.tensor_scalar(dst[:, rc:rc + 1], c.sm_scl.ap(),
                                             scol[:, rc:rc + 1], None, ALU.mult)
            lastsc.then_inc(sSc, 1)

    sRd = _sem(c, "p34_rowd")
    with nc.Block() as blk:
        @blk.tensor
        def _(t):
            t.wait_ge(sSc, 1)
            t.transpose(c.psbf[0:8, 0:128], c.sm_vscale.ap(), c.sm_idf.ap()).then_inc(sR, 1)
            t.wait_ge(sRd, 32)
            t.matmul(psq[:, 0:512], c.sm_ones_row.ap(), c.sm_svrow.ap()[:, 0:512],
                     start=True, stop=True)
            t.matmul(psq[:, 512:1024], c.sm_ones_row.ap(),
                     c.sm_svrow.ap()[:, 512:1024], start=True, stop=True).then_inc(sR, 1)

        @blk.vector
        def _(v):
            v.wait_ge(sR, 1)
            v.tensor_copy(big2f[0:8, 2048:2176], c.psbf[0:8, 0:128]).then_inc(sR, 1)
            v.wait_ge(sR, 3)
            v.tensor_copy(c.sm_vscb.ap(), psq[:, 0:1024]).then_inc(sR, 1)

        @blk.gpsimd
        def _(g):
            g.wait_ge(sR, 2)
            g.dma_start(c.ROWSCR.ap(), big2f[0:8, 2048:2176]).then_inc(sRd, 16)
            g.wait_ge(sRd, 16)
            g.dma_start(c.sm_svrow.ap(),
                        c.ROWSCR.ap().rearrange("a b -> (a b)").unsqueeze(0)).then_inc(sRd, 16)

    with nc.Block() as blk:
        @blk.sync
        def _(sy):
            for tc in range(NTS):
                if tc >= 2:
                    sy.wait_ge(sMM, tc - 1)
                sy.dma_start(XT[tc % 2].rearrange("p a b -> p (a b)"),
                             c.XFM.ap()[:, tc, :, :].rearrange("p a b -> p (a b)")
                             ).then_inc(sX.h(tc), 16)

        @blk.tensor
        def _(t):
            for tc in range(NTS):
                sX.wait(t, tc)
                if tc >= 1:
                    t.wait_ge(sEv, 2 * tc)
                for dc in range(ND):
                    lhs = XT[tc % 2][:, dc, :]
                    t.matmul(psq[:, 0:512], lhs, WKT[:, dc, 0:512],
                             start=(dc == 0), stop=(dc == ND - 1))
                    t.matmul(psq[:, 512:1024], lhs, WKT[:, dc, 512:1024],
                             start=(dc == 0), stop=(dc == ND - 1))
                    t.matmul(psq[:, 1024:1536], lhs, WVT[:, dc, 0:512],
                             start=(dc == 0), stop=(dc == ND - 1))
                    lastmm = t.matmul(psq[:, 1536:2048], lhs, WVT[:, dc, 512:1024],
                                      start=(dc == 0), stop=(dc == ND - 1))
                lastmm.then_inc(sMM, 1)
                if tc >= 1:
                    t.wait_ge(sEv, 2 * (tc - 1) + 1)
                    if tc >= 3:
                        t.wait_ge(sKf, 8 * (tc - 2))
                    pb = psb[:, ((tc - 1) % 2) * 1024:((tc - 1) % 2) * 1024 + 1024]
                    last = None
                    for kvh in range(8):
                        last = t.transpose(pb[:, kvh * 128:(kvh + 1) * 128],
                                           KTM[(tc - 1) % 2][:, kvh * 128:(kvh + 1) * 128],
                                           c.sm_idb.ap())
                    last.then_inc(sKt, 1)
                if tc >= 2:
                    t.wait_ge(sKsq, tc - 1)
                    if tc >= 3:
                        t.wait_ge(sNKr, 8 * (tc - 2))
                    for kvh in range(8):
                        if kvh >= 2:
                            t.wait_ge(sNKr, 8 * (tc - 2) + kvh - 1)
                        t.matmul(psn[kvh % 2][:, 0:128], c.sm_ones_col.ap(),
                                 KSQB[(tc - 2) % 2][:, kvh * 128:(kvh + 1) * 128],
                                 start=True, stop=True).then_inc(sNKm, 1)
            for tcl in [NTS - 1]:
                t.wait_ge(sEv, 2 * tcl + 1)
                t.wait_ge(sKf, 8 * (tcl - 1))
                pb = psb[:, (tcl % 2) * 1024:(tcl % 2) * 1024 + 1024]
                last = None
                for kvh in range(8):
                    last = t.transpose(pb[:, kvh * 128:(kvh + 1) * 128],
                                       KTM[tcl % 2][:, kvh * 128:(kvh + 1) * 128],
                                       c.sm_idb.ap())
                last.then_inc(sKt, 1)
            for tcl in [NTS - 2, NTS - 1]:
                t.wait_ge(sKsq, tcl + 1)
                t.wait_ge(sNKr, 8 * tcl)
                for kvh in range(8):
                    if kvh >= 2:
                        t.wait_ge(sNKr, 8 * tcl + kvh - 1)
                    t.matmul(psn[kvh % 2][:, 0:128], c.sm_ones_col.ap(),
                             KSQB[tcl % 2][:, kvh * 128:(kvh + 1) * 128],
                             start=True, stop=True).then_inc(sNKm, 1)

        @blk.vector
        def _(v):
            for tc in range(NTS):
                v.wait_ge(sMM, tc + 1)
                if tc == 0:
                    v.wait_ge(sR, 4)
                if tc >= 2:
                    v.wait_ge(sKt, tc - 1)
                    sVd.wait(v, tc - 2, per=128)
                v.tensor_copy(KTM[tc % 2], psq[:, 0:1024]).then_inc(sEv, 1)
                v.tensor_tensor(VTM[tc % 2], psq[:, 1024:2048], c.sm_vscb.ap(),
                                ALU.mult).then_inc(sEv, 1)
                if tc >= 1:
                    tcm = tc - 1
                    v.wait_ge(sKt, tcm + 1)
                    if tcm >= 2:
                        sKd.wait(v, tcm - 2, per=128)
                    if tcm >= 2:
                        v.wait_ge(sNKm, 8 * (tcm - 1))
                    pb = psb[:, (tcm % 2) * 1024:(tcm % 2) * 1024 + 1024]
                    for kvh in range(8):
                        v.tensor_scalar(KFM[tcm % 2][:, kvh * 128:(kvh + 1) * 128],
                                        pb[:, kvh * 128:(kvh + 1) * 128],
                                        c.sm_kscale.ap()[:, kvh:kvh + 1], None,
                                        ALU.mult).then_inc(sKf, 1)
                    v.wait_ge(sKf, 8 * (tcm + 1))
                    v.tensor_tensor(KSQB[tcm % 2], KFM[tcm % 2], KFM[tcm % 2],
                                    ALU.mult).then_inc(sKsq, 1)
                if tc >= 2:
                    _p34_nkred(c, v, tc - 2, psn, sNKm, sNKr)
            tcm = NTS - 1
            v.wait_ge(sKt, tcm + 1)
            sKd.wait(v, tcm - 2, per=128)
            v.wait_ge(sNKm, 8 * (tcm - 1))
            pb = psb[:, (tcm % 2) * 1024:(tcm % 2) * 1024 + 1024]
            for kvh in range(8):
                v.tensor_scalar(KFM[tcm % 2][:, kvh * 128:(kvh + 1) * 128],
                                pb[:, kvh * 128:(kvh + 1) * 128],
                                c.sm_kscale.ap()[:, kvh:kvh + 1], None,
                                ALU.mult).then_inc(sKf, 1)
            v.wait_ge(sKf, 8 * (tcm + 1))
            v.tensor_tensor(KSQB[tcm % 2], KFM[tcm % 2], KFM[tcm % 2],
                            ALU.mult).then_inc(sKsq, 1)
            _p34_nkred(c, v, NTS - 2, psn, sNKm, sNKr)
            _p34_nkred(c, v, NTS - 1, psn, sNKm, sNKr)
            v.wait_ge(sNKr, 8 * NTS)
            v.tensor_scalar(c.sm_nk5.ap(), c.sm_nk5.ap(), NKF, None, ALU.mult
                            ).then_inc(sNKr, 1)

        @blk.gpsimd
        def _(g):
            for tc in range(NTS):
                g.wait_ge(sEv, 2 * (tc + 1))
                for kvh in range(8):
                    g.dma_start(c.VSCR.ap()[kvh, :, tc, :],
                                VTM[tc % 2][:, kvh * 128:(kvh + 1) * 128]
                                ).then_inc(sVd.h(tc), 16)
                g.wait_ge(sKf, 8 * (tc + 1))
                for kvh in range(8):
                    g.dma_start(c.KSCR.ap()[kvh, :, tc * 128:(tc + 1) * 128],
                                KFM[tc % 2][:, kvh * 128:(kvh + 1) * 128]
                                ).then_inc(sKd.h(tc), 16)
            sVd.wait(g, NTS - 2, per=128)
            sVd.wait(g, NTS - 1, per=128)
            sKd.wait(g, NTS - 2, per=128)
            sKd.wait(g, NTS - 1, per=128)


def _p34_nkred(c, v, tc, psn, sNKm, sNKr):
    for kvh in range(8):
        v.wait_ge(sNKm, 8 * tc + kvh + 1)
        if tc == 0:
            v.tensor_reduce(c.sm_nk5.ap()[:, kvh:kvh + 1], psn[kvh % 2][:, 0:128],
                            AX.X, ALU.max).then_inc(sNKr, 1)
        else:
            v.wait_ge(sNKr, 8 * tc + kvh)
            vc(c, v.tensor_reduce(c.sm_t1.ap()[:, 3:4], psn[kvh % 2][:, 0:128],
                                  AX.X, ALU.max))
            v.wait_ge(sNKr, 8 * (tc - 1) + kvh + 1)
            vw(c, v)
            v.tensor_max(c.sm_nk5.ap()[:, kvh:kvh + 1],
                         c.sm_nk5.ap()[:, kvh:kvh + 1],
                         c.sm_t1.ap()[:, 3:4]).then_inc(sNKr, 1)


def phase5(c):
    nc = c.nc
    sKV = DSem(c, "p5_kv")
    sQ = DSem(c, "p5_q")
    sNr = _sem(c, "p5_nr")
    sBm = _sem(c, "p5_bm")
    sNg = _sem(c, "p5_ng")
    sQK = _sem(c, "p5_qk")
    sEx = _sem(c, "p5_ex")
    sF1 = _sem(c, "p5_f1")
    sPV = _sem(c, "p5_pv")
    sRc = _sem(c, "p5_rc")
    sBc = _sem(c, "p5_bc")
    sAo = _sem(c, "p5_ao")
    c.sAo5 = sAo

    big2 = c.BIG2.ap()
    KBUF = [big2[:, 0:4096], big2[:, 4096:8192]]
    VBUF = [big2[:, 8192:12288], big2[:, 12288:16384]]
    QBUF = [big2[:, 16384:18432], big2[:, 18432:20480]]
    PT = [big2[:, 20480 + j * 1024:20480 + (j + 1) * 1024] for j in range(3)]
    FOLD = [big2[:, 23552 + j * 512:23552 + (j + 1) * 512] for j in range(3)]
    BCS = c.sm_bcs.ap()
    psq = c.PSQ.ap()
    psa = c.PSA.ap()
    psd = c.PSD.ap()
    DEN = c.psbf[0:1, 512:1024]

    with nc.Block() as blk:
        @blk.sync
        def _(sy):
            for kvh in range(HK):
                if kvh >= 2:
                    sy.wait_ge(sPV, NP * 16 * (kvh - 1))
                sy.dma_start(KBUF[kvh % 2],
                             c.KSCR.ap()[kvh, :, :]).then_inc(sKV.h(kvh), 16)
                sy.dma_start(VBUF[kvh % 2], c.VSCR.ap()[kvh, :, :, :]
                             .rearrange("p a b -> p (a b)")).then_inc(sKV.h(kvh), 16)
                for r in range(4):
                    h = kvh * 4 + r
                    if h >= 2:
                        sy.wait_ge(sQK, NP * 4 * (h - 1))
                    sy.dma_start(QBUF[h % 2], c.QS.ap()[h, :, :]).then_inc(sQ.h(h), 16)

        @blk.tensor
        def _(t):
            t.wait_ge(sNr, 1)
            t.matmul(c.psbf[:, 0:1], c.sm_ones_row.ap(), c.sm_t1b.ap()[:, 0:1],
                     start=True, stop=True).then_inc(sBm, 1)
            for kvh in range(HK):
                bk = kvh % 2
                sKV.wait(t, kvh, per=32)
                for r in range(4):
                    h = kvh * 4 + r
                    bq = h % 2
                    sQ.wait(t, h)
                    for qt in range(NQT):
                        i = h * 4 + qt
                        for p in range(NP):
                            if NP * i + p - 1 > 0:
                                t.wait_ge(sEx, NP * i + p - 1)
                            last = None
                            for u in range(2):
                                cc = 2 * p + u
                                last = t.matmul(psq[:, (cc % 4) * 512:(cc % 4) * 512 + 512],
                                                KBUF[bk][:, cc * 128:(cc + 1) * 128],
                                                QBUF[bq][:, qt * 512:(qt + 1) * 512],
                                                start=True, stop=True)
                            last.then_inc(sQK, 1)
                            if p == 0 and i >= 1:
                                t.wait_ge(sRc, i)
                                t.wait_ge(sAo, i - 1)
                                t.matmul(psd[:, 0:512], c.sm_ones_row.ap(),
                                         c.sm_recb.ap(), start=True,
                                         stop=True).then_inc(sBc, 1)
                            if p >= 1:
                                _p5_pvden(c, t, i, p - 1, bk, VBUF, PT, FOLD, psa,
                                          sEx, sF1, sPV, sAo, sRc)
                            if qt == 3 and p == 4 and h + 1 < H:
                                t.wait_ge(sNr, h + 2)
                                t.wait_ge(sNg, h + 1)
                                t.matmul(c.psbf[:, 0:1], c.sm_ones_row.ap(),
                                         c.sm_t1b.ap()[:, (h + 1) % 2:(h + 1) % 2 + 1],
                                         start=True, stop=True).then_inc(sBm, 1)
                        _p5_pvden(c, t, i, NP - 1, bk, VBUF, PT, FOLD, psa,
                                  sEx, sF1, sPV, sAo, sRc)
                        if i == H * NQT - 1:
                            t.wait_ge(sRc, i + 1)
                            t.wait_ge(sAo, i)
                            t.matmul(psd[:, 0:512], c.sm_ones_row.ap(),
                                     c.sm_recb.ap(), start=True,
                                     stop=True).then_inc(sBc, 1)

        @blk.vector
        def _(v):
            v.wait_ge(c.sNqr, c.nqr_val)
            v.wait_ge(c.sNKr, 8 * NTS + 1)
            vc(c, v.tensor_add(c.sm_t1.ap()[:, 0:1], c.sm_nq32.ap()[:, 0:1],
                               c.sm_nk5.ap()[:, 0:1]))
            vw(c, v)
            v.tensor_copy(c.sm_t1b.ap()[:, 0:1], c.sm_t1.ap()[:, 0:1]).then_inc(sNr, 1)
            v.wait_ge(sBm, 1)
            v.tensor_copy(c.sm_negu.ap()[:, 0:1], c.psbf[:, 0:1]).then_inc(sNg, 1)
            for h in range(H):
                for qt in range(NQT):
                    i = h * 4 + qt
                    for p in range(NP):
                        G = NP * i + p
                        v.wait_ge(sEx, G + 1)
                        if G >= 3:
                            v.wait_ge(sPV, G - 2)
                        v.tensor_tensor(FOLD[p % 3], PT[p % 3][:, 0:512],
                                        PT[p % 3][:, 512:1024],
                                        ALU.add).then_inc(sF1, 1)
                        if qt == 2 and p == 7 and h + 1 < H:
                            hn = h + 1
                            vc(c, v.tensor_add(c.sm_t1.ap()[:, hn % 2:hn % 2 + 1],
                                               c.sm_nq32.ap()[:, hn:hn + 1],
                                               c.sm_nk5.ap()[:, hn // 4:hn // 4 + 1]))
                            v.wait_ge(sBm, hn)
                            vw(c, v)
                            v.tensor_copy(c.sm_t1b.ap()[:, hn % 2:hn % 2 + 1],
                                          c.sm_t1.ap()[:, hn % 2:hn % 2 + 1]
                                          ).then_inc(sNr, 1)
                        if qt == 3 and p == 7 and h + 1 < H:
                            hn = h + 1
                            v.wait_ge(sBm, hn + 1)
                            v.tensor_copy(c.sm_negu.ap()[:, hn % 2:hn % 2 + 1],
                                          c.psbf[:, 0:1]).then_inc(sNg, 1)
                    v.wait_ge(sBc, i + 1)
                    v.wait_ge(sPV, NP * (i + 1))
                    vc(c, v.tensor_copy(BCS, psd[:, 0:512]))
                    vw(c, v)
                    v.tensor_tensor(c.RES[:, h, qt * 512:(qt + 1) * 512],
                                    psa[:, 0:512], BCS, ALU.mult).then_inc(sAo, 1)

        @blk.scalar
        def _(s):
            for h in range(H):
                for qt in range(NQT):
                    i = h * 4 + qt
                    for p in range(NP):
                        G = NP * i + p
                        s.wait_ge(sQK, G + 1)
                        if p == 0 and qt == 0:
                            s.wait_ge(sNg, h + 1)
                        if G - 2 > 0:
                            s.wait_ge(sPV, G - 2)
                            s.wait_ge(sF1, G - 2)
                        bank = 0 if (p % 2 == 0) else 1024
                        s.activation(PT[p % 3], psq[:, bank:bank + 1024], AF.Exp,
                                     bias=c.sm_negu.ap()[:, h % 2:h % 2 + 1]
                                     ).then_inc(sEx, 1)
                    s.wait_ge(sPV, NP * (i + 1))
                    sc_(c, s.activation(c.sm_recl.ap(), DEN, AF.Ln))
                    sw_(c, s)
                    s.activation(c.sm_recb.ap(), c.sm_recl.ap(), AF.Exp,
                                 scale=-1.0).then_inc(sRc, 1)


def _p5_pvden(c, t, i, p, bk, VBUF, PT, FOLD, psa, sEx, sF1, sPV, sAo, sRc):
    G = NP * i + p
    t.wait_ge(sEx, G + 1)
    if p == 0:
        t.wait_ge(sAo, i)
        if i >= 1:
            t.wait_ge(sRc, i)
    for u in range(2):
        cc = 2 * p + u
        t.matmul(psa[:, 0:512], VBUF[bk][:, cc * 128:(cc + 1) * 128],
                 PT[p % 3][:, u * 512:(u + 1) * 512],
                 start=(cc == 0), stop=(cc == 31))
    t.wait_ge(sF1, G + 1)
    t.matmul(c.psbf[0:1, 512:1024], c.sm_ones_col.ap(), FOLD[p % 3],
             start=(p == 0), stop=(p == NP - 1)).then_inc(sPV, 1)


class LnSems:
    KEYS = ["st", "sd", "ap", "cs", "tp", "ev", "add", "nm"]

    def __init__(self, c):
        self.h = {k: _sem(c, "ln_" + k) for k in self.KEYS}
        self.base = {k: 0 for k in self.KEYS}
        self.dA = DSem(c, "ln_a")
        self.dB = DSem(c, "ln_b")
        self.dO = DSem(c, "ln_od")
        self.dF = DSem(c, "ln_fd")
        self.dF2 = DSem(c, "ln_fd2")

    def wg(self, eng, k, v):
        if self.base[k] + v > 0:
            eng.wait_ge(self.h[k], self.base[k] + v)

    def inc(self, inst, k, n=1):
        inst.then_inc(self.h[k], n)

    def fin(self, used):
        for k, n in used.items():
            self.base[k] += n


def emit_ln(c, name, in_a_dram, in_b_getter, out_writer, fm_out):
    nc = c.nc
    if not hasattr(c, "lnsems"):
        c.lnsems = LnSems(c)
    L = c.lnsems
    sA, sB, sOd, sFd, sFd2 = L.dA, L.dB, L.dO, L.dF, L.dF2

    big1f = c.BIG1.ap().bitcast(F32)
    OB = [big1f[:, 0:4096], big1f[:, 4096:8192]]
    XB = [big1f[:, 8192:12288], big1f[:, 12288:16384]]
    big2 = c.BIG2.ap()
    XB16 = [big2[:, 0:4096], big2[:, 4096:8192]]
    XSTG = [big2[:, 8192:12288], big2[:, 12288:16384]]
    psb = c.PSB.ap()

    with nc.Block() as blk:
        @blk.sync
        def _(sy):
            for tc in range(NT):
                b = tc % 2
                if tc >= 2:
                    L.wg(sy, "ap", tc - 1)
                    sOd.wait(sy, tc - 2)
                    if fm_out:
                        L.wg(sy, "cs", tc - 1)
                sy.dma_start(OB[b], in_a_dram.ap()[tc, :, :]).then_inc(sA.h(tc), 16)
                sy.dma_start(XB[b], in_b_getter(tc)).then_inc(sB.h(tc), 16)

        @blk.vector
        def _(v):
            for tc in range(NT):
                b = tc % 2
                sA.wait(v, tc)
                sB.wait(v, tc)
                vc(c, v.tensor_add(OB[b], OB[b], XB[b]))
                vw(c, v)
                for j in range(8):
                    if j == 7:
                        vc(c, v.bn_stats(c.sm_bn.ap()[:, j * 6:(j + 1) * 6],
                                         OB[b][:, j * 512:(j + 1) * 512]))
                    else:
                        v.bn_stats(c.sm_bn.ap()[:, j * 6:(j + 1) * 6],
                                   OB[b][:, j * 512:(j + 1) * 512])
                vw(c, v)
                vc(c, v.bn_aggr(c.sm_mv.ap(), c.sm_bn.ap()))
                vw(c, v)
                L.inc(v.tensor_scalar(c.sm_var.ap(), c.sm_mv.ap()[:, 1:2], 1e-5, None,
                                      ALU.add), "st")
                L.wg(v, "sd", tc + 1)
                vc(c, v.reciprocal(c.sm_rstd.ap(), c.sm_sd.ap()))
                vw(c, v)
                L.inc(v.scalar_tensor_tensor(c.sm_nmr.ap(), c.sm_mv.ap()[:, 0:1], -1.0,
                                             c.sm_rstd.ap(), ALU.mult, ALU.mult), "nm")
                if fm_out:
                    for g in range(4):
                        gi = tc * 4 + g
                        pb = psb[:, (gi % 2) * 1024:(gi % 2) * 1024 + 1024]
                        L.wg(v, "tp", gi + 1)
                        if tc >= 2 and g == 0:
                            sFd.wait(v, tc - 2, per=256)
                            sFd2.wait(v, tc - 2, per=256)
                        L.inc(v.tensor_copy(
                            XSTG[b].rearrange("p (a b) -> p a b", a=32)
                            [:, g * 8:(g + 1) * 8, :],
                            pb.rearrange("p (a b) -> p a b", a=8)), "ev")

        @blk.scalar
        def _(s):
            for tc in range(NT):
                b = tc % 2
                L.wg(s, "st", tc + 1)
                L.inc(s.activation(c.sm_sd.ap(), c.sm_var.ap(), AF.Sqrt), "sd")
                L.wg(s, "nm", tc + 1)
                L.inc(s.activation(XB[b], OB[b], AF.Identity, scale=c.sm_rstd.ap(),
                                   bias=c.sm_nmr.ap()), "ap")
                if fm_out:
                    L.wg(s, "ap", tc + 1)
                    if tc >= 2:
                        L.wg(s, "tp", 4 * (tc - 1))
                    L.inc(s.activation(XB16[b], XB[b], AF.Copy), "cs")
                    s.wait_ge(L.h["ev"], L.base["ev"] + 4 * (tc + 1))
                    for dc in range(1, ND, 2):
                        s.dma_start(c.X1FM.ap()[dc, :, tc * 128:(tc + 1) * 128],
                                    XSTG[b].rearrange("p (a b) -> p a b", a=32)[:, dc, :]
                                    ).then_inc(sFd2.h(tc), 16)

        if fm_out:
            @blk.tensor
            def _(t):
                for tc in range(NT):
                    b = tc % 2
                    L.wg(t, "cs", tc + 1)
                    for g in range(4):
                        gi = tc * 4 + g
                        if gi >= 2:
                            L.wg(t, "ev", gi - 1)
                        pb = psb[:, (gi % 2) * 1024:(gi % 2) * 1024 + 1024]
                        last = None
                        for j in range(8):
                            dc = g * 8 + j
                            last = t.transpose(pb[:, j * 128:(j + 1) * 128],
                                               XB16[b][:, dc * 128:(dc + 1) * 128],
                                               c.sm_idb.ap())
                        L.inc(last, "tp")

        @blk.gpsimd
        def _(g):
            for tc in range(NT):
                b = tc % 2
                L.wg(g, "ap", tc + 1)
                out_writer(g, tc, XB[b], sOd.h(tc))
                if fm_out:
                    L.wg(g, "ev", 4 * (tc + 1))
                    for dc in range(0, ND, 2):
                        g.dma_start(c.X1FM.ap()[dc, :, tc * 128:(tc + 1) * 128],
                                    XSTG[b].rearrange("p (a b) -> p a b", a=32)[:, dc, :]
                                    ).then_inc(sFd.h(tc), 16)
            sOd.wait(g, NT - 2)
            sOd.wait(g, NT - 1)
            if fm_out:
                sFd.wait(g, NT - 2, per=256)
                sFd.wait(g, NT - 1, per=256)
                sFd2.wait(g, NT - 2, per=256)
                sFd2.wait(g, NT - 1, per=256)

    used = {"st": NT, "sd": NT, "ap": NT, "nm": NT}
    if fm_out:
        used["cs"] = NT
        used["tp"] = 4 * NT
        used["ev"] = 4 * NT
        sFd.close(NT, 256)
        sFd2.close(NT, 256)
    sA.close(NT, 16)
    sB.close(NT, 16)
    sOd.close(NT, 16)
    L.fin(used)


def phase7(c):
    def in_b(tc):
        return c.x.ap()[tc * 128:(tc + 1) * 128, :]

    def out_w(g, tc, src, semh):
        g.dma_start(c.X1TM.ap()[tc, :, :], src).then_inc(semh, 16)

    emit_ln(c, "p7", c.OOUT, in_b, out_w, fm_out=True)


def phase9(c):
    def in_b(tc):
        return c.X1TM.ap()[tc, :, :]

    def out_w(g, tc, src, semh):
        g.dma_start(c.y.ap()[tc * 128:(tc + 1) * 128, :], src).then_inc(semh, 16)

    emit_ln(c, "p9", c.FFOUT, in_b, out_w, fm_out=False)


def phase8(c):
    nc = c.nc
    big2 = c.BIG2.ap()
    big2f = big2.bitcast(F32)
    LOWW = big2[:, 24576:26624]
    WBC = big2[:, 26624:28672]
    GL = big2f[0:8, 0:2048]
    psq = c.PSQ.ap()
    psb = c.PSB.ap()
    W83 = c.sm_w8.ap().rearrange("p (a b) -> p a b", a=16)
    LGT3 = c.sm_lgt.ap().rearrange("p (a b) -> p a b", a=16)

    sL = _sem(c, "p8_l")
    sG = _sem(c, "p8_g")
    sT2 = _sem(c, "p8_t2")
    sE2 = _sem(c, "p8_e2")
    sW2 = _sem(c, "p8_w2")

    with nc.Block() as blk:
        @blk.sync
        def _(sy):
            for dc in range(ND):
                sy.dma_start(c.RES[:, dc, :], c.X1FM.ap()[dc, :, :]).then_inc(sL, 16)

        @blk.tensor
        def _(t):
            t.wait_ge(sL, 16 * ND)
            last = None
            for qt in range(NQT):
                for dc in range(ND):
                    last = t.matmul(psq[0:8, qt * 512:(qt + 1) * 512], c.GT[:, dc, :],
                                    c.RES[:, dc, qt * 512:(qt + 1) * 512],
                                    start=(dc == 0), stop=(dc == ND - 1))
            last.then_inc(sG, 1)
            t.wait_ge(sG, 2)
            for tc in range(NT):
                last = t.transpose(c.psbf[:, tc * 8:tc * 8 + 8],
                                   GL[:, tc * 128:(tc + 1) * 128], c.sm_idf.ap()[0:8, 0:8])
            last.then_inc(sG, 1)
            t.wait_ge(sT2, 2 * NT)
            for tc in range(NT):
                last = t.transpose(psb[0:8, tc * 128:(tc + 1) * 128], W83[:, tc, :],
                                   c.sm_idb.ap())
            last.then_inc(sW2, 1)
            t.wait_ge(sW2, 2)
            for qt in range(NQT):
                last = t.matmul(psq[:, qt * 512:(qt + 1) * 512], c.sm_exp8.ap(),
                                c.sm_wt8.ap()[:, qt * 512:(qt + 1) * 512],
                                start=True, stop=True)
            last.then_inc(sW2, 1)
            t.wait_ge(sW2, 4)
            for qt in range(NQT):
                for dc in range(ND):
                    last = t.matmul(psq[:, qt * 512:(qt + 1) * 512], c.AT[:, dc, :],
                                    c.RES[:, dc, qt * 512:(qt + 1) * 512],
                                    start=(dc == 0), stop=(dc == ND - 1))
            last.then_inc(sW2, 1)

        @blk.vector
        def _(v):
            v.wait_ge(sG, 1)
            v.tensor_copy(GL, psq[0:8, 0:2048]).then_inc(sG, 1)
            v.wait_ge(sG, 3)
            v.tensor_copy(c.sm_lgt.ap(), c.psbf[:, 0:128]).then_inc(sG, 1)
            v.wait_ge(sG, 4)
            for tc in range(NT):
                vw(c, v)
                vc(c, v.max(c.sm_m8.ap(), LGT3[:, tc, :]))
                vw(c, v)
                v.tensor_scalar(c.sm_ngm.ap(), c.sm_m8.ap()[:, 0:1], -1.0, None,
                                ALU.mult).then_inc(sT2, 1)
                v.wait_ge(sE2, tc + 1)
                vc(c, v.tensor_scalar(c.sm_sel.ap(), LGT3[:, tc, :],
                                      c.sm_m8.ap()[:, 1:2], None, ALU.is_ge))
                vw(c, v)
                vc(c, v.tensor_tensor(c.sm_psel.ap(), c.sm_p8e.ap(), c.sm_sel.ap(),
                                      ALU.mult))
                vw(c, v)
                vc(c, v.reduce_sum(c.sm_den8.ap(), c.sm_psel.ap(), axis=AX.X))
                vw(c, v)
                vc(c, v.reciprocal(c.sm_re8.ap(), c.sm_den8.ap()))
                vw(c, v)
                v.tensor_scalar(W83[:, tc, :], c.sm_psel.ap(), c.sm_re8.ap(), None,
                                ALU.mult).then_inc(sT2, 1)
            v.wait_ge(sW2, 1)
            v.tensor_copy(c.sm_wt8.ap(), psb[0:8, 0:2048]).then_inc(sW2, 1)
            v.wait_ge(sW2, 3)
            v.tensor_copy(WBC, psq[:, 0:2048]).then_inc(sW2, 1)
            v.wait_ge(sW2, 5)
            for qt in range(NQT):
                last = v.scalar_tensor_tensor(LOWW[:, qt * 512:(qt + 1) * 512],
                                              psq[:, qt * 512:(qt + 1) * 512], 2.0,
                                              WBC[:, qt * 512:(qt + 1) * 512],
                                              ALU.mult, ALU.mult)
            last.then_inc(sW2, 1)

        @blk.scalar
        def _(s):
            for tc in range(NT):
                s.wait_ge(sT2, 2 * tc + 1)
                s.activation(c.sm_p8e.ap(), LGT3[:, tc, :], AF.Exp,
                             bias=c.sm_ngm.ap()).then_inc(sE2, 1)

    emit_proj(c, "p8b", c.wb, c.sm_bbc.ap(), 1.0, False, "f32_tm", c.FFOUT,
              lora=True)


def kernel(L0_attn_hidden_states, L0_attn_q_weight, L0_attn_q_scale,
           L0_attn_k_weight, L0_attn_k_scale, L0_attn_v_weight, L0_attn_v_scale,
           L0_attn_o_weight, L0_attn_o_scale, L0_ffn_gate_w, L0_ffn_base_w,
           L0_ffn_base_b, L0_ffn_lora_A, L0_ffn_lora_B):
    hs = np.asarray(L0_attn_hidden_states)
    gt = np.ascontiguousarray(
        np.asarray(L0_ffn_gate_w, np.float32).T.reshape(32, 128, 8)
        .transpose(1, 0, 2).reshape(128, 256))
    lat = np.ascontiguousarray(
        np.asarray(L0_ffn_lora_A, np.float32).reshape(128, D).T
        .reshape(32, 128, 128).transpose(1, 0, 2).reshape(128, 4096))
    lbt = np.ascontiguousarray(
        np.asarray(L0_ffn_lora_B, np.float32).transpose(0, 2, 1).reshape(128, D))
    consts = dict(
        wq=np.ascontiguousarray(L0_attn_q_weight, dtype=np.float32),
        wk=np.ascontiguousarray(L0_attn_k_weight, dtype=np.float32),
        wv=np.ascontiguousarray(L0_attn_v_weight, dtype=np.float32),
        wo=np.ascontiguousarray(L0_attn_o_weight, dtype=np.float32),
        wb=np.ascontiguousarray(L0_ffn_base_w, dtype=np.float32),
        qsc_in=np.ascontiguousarray(np.asarray(L0_attn_q_scale, np.float32).reshape(ND, 128)),
        ksc_in=np.ascontiguousarray(np.asarray(L0_attn_k_scale, np.float32).reshape(HK, 128)),
        vsc_in=np.ascontiguousarray(np.asarray(L0_attn_v_scale, np.float32).reshape(HK, 128)),
        osc_in=np.ascontiguousarray(np.asarray(L0_attn_o_scale, np.float32).reshape(ND, 128)),
        bb_in=np.ascontiguousarray(np.asarray(L0_ffn_base_b, np.float32).reshape(ND, 128)),
        gt_in=gt, lat_in=lat, lbt_in=lbt,
        id_bf=np.eye(128, dtype=BF),
        id_f32=np.eye(128, dtype=np.float32),
        ones_row=np.ones((1, 128), dtype=BF),
        ones_col=np.ones((128, 1), dtype=BF),
        ones_sq_f32=np.ones((128, 128), dtype=np.float32),
        exp8=_exp8_const(),
    )
    in_maps = []
    for core in range(8):
        seq, half = core // 2, core % 2
        m = dict(consts)
        xs = np.ascontiguousarray(hs[seq], dtype=np.float32)
        if half == 1:
            xs = np.ascontiguousarray(np.roll(xs, -T, axis=0))
        m["x"] = xs
        in_maps.append(m)

    nc = build_program()
    res = run_bass_kernel_spmd(nc, in_maps, core_ids=list(range(8)))
    full = np.empty((4, S, D), np.float32)
    for core in range(8):
        seq, half = core // 2, core % 2
        full[seq, half * T:(half + 1) * T, :] = res.results[core]["y"]
    return full


def _exp8_const():
    e = np.zeros((8, 128), dtype=BF)
    for i in range(8):
        e[i, i * 16:(i + 1) * 16] = 1
    return e


# version 12
# speedup vs baseline: 1.3363x; 1.0783x over previous
import numpy as np
import ml_dtypes
from contextlib import ExitStack

import concourse.bass as bass
import concourse.mybir as mybir
from concourse.bass_utils import run_bass_kernel_spmd

F32 = mybir.dt.float32
BF16 = mybir.dt.bfloat16
AF = mybir.ActivationFunctionType
ALU = mybir.AluOpType
AX = mybir.AxisListType

D = 4096
S = 4096
T = 2048
KVD = 1024
H = 32
HK = 8
HD = 128
ND = 32
NTS = 32
NT = 16
NQT = 4
NP = 16
RSQ = float(1.0 / np.sqrt(HD))
NQF = float(-0.5 * np.sqrt(HD))
NKF = float(-0.5 / np.sqrt(HD))

BF = ml_dtypes.bfloat16


class Ctx:
    pass


class V:

    def __init__(self, ap):
        self._ap = ap

    def ap(self):
        return self._ap


def _sem(c, name):
    return c.st.enter_context(c.nc.semaphore(name))


def vc(c, inst):
    c.vcnt += 1
    inst.then_inc(c.semVC, 1)
    return inst


def vw(c, v):
    v.wait_ge(c.semVC, c.vcnt)


def sc_(c, inst):
    c.scnt += 1
    inst.then_inc(c.semSC, 1)
    return inst


def sw_(c, s):
    s.wait_ge(c.semSC, c.scnt)


class DSem:

    def __init__(self, c, name, n=2):
        self.n = n
        self.s = [_sem(c, name + chr(97 + j)) for j in range(n)]
        self.ub = 0
        self.acc = [0] * n

    def h(self, i):
        return self.s[(self.ub + i) % self.n]

    def v(self, i, per=16):
        return self.acc[(self.ub + i) % self.n] + per * (i // self.n + 1)

    def wait(self, eng, i, per=16):
        if i >= 0:
            eng.wait_ge(self.h(i), self.v(i, per))

    def close(self, n, per):
        for j in range(self.n):
            self.acc[j] += per * (n // self.n)
        self.ub += n


def _sb(c, name, shape, dt):
    return c.st.enter_context(c.nc.sbuf_tensor(name, shape, dt))


def build_program(n_phases=9, dbg=()):
    nc = bass.Bass("TRN2", target_bir_lowering=False)
    c = Ctx()
    c.nc = nc

    def dram(name, shape, dt, kind="ExternalInput"):
        if name in dbg and kind == "Internal" and dt == F32:
            kind = "ExternalOutput"
        return nc.dram_tensor(name, shape, dt, kind=kind)

    c.x = dram("x", [S, D], F32)
    c.wq = dram("wq", [D, D], F32)
    c.wk = dram("wk", [KVD, D], F32)
    c.wv = dram("wv", [KVD, D], F32)
    c.wo = dram("wo", [D, D], F32)
    c.wb = dram("wb", [D, D], F32)
    c.qsc_in = dram("qsc_in", [ND, 128], F32)
    c.ksc_in = dram("ksc_in", [HK, 128], F32)
    c.vsc_in = dram("vsc_in", [HK, 128], F32)
    c.osc_in = dram("osc_in", [ND, 128], F32)
    c.bb_in = dram("bb_in", [ND, 128], F32)
    c.gt_in = dram("gt_in", [128, 256], F32)
    c.lat_in = dram("lat_in", [128, 4096], F32)
    c.lbt_in = dram("lbt_in", [128, 4096], F32)
    c.id_bf = dram("id_bf", [128, 128], BF16)
    c.id_f32 = dram("id_f32", [128, 128], F32)
    c.ones_row = dram("ones_row", [1, 128], BF16)
    c.ones_col = dram("ones_col", [128, 1], BF16)
    c.ones_sq_f32 = dram("ones_sq_f32", [128, 128], F32)
    c.exp8 = dram("exp8", [8, 128], BF16)
    c.y = dram("y", [T, D], F32, kind="ExternalOutput")

    c.XFM = dram("XFM", [128, NTS, ND, 128], BF16, kind="Internal")
    c.QS = dram("QS", [ND, 128, T], BF16, kind="Internal")
    c.KSCR = dram("KSCR", [HK, 128, S], BF16, kind="Internal")
    c.VSCR = dram("VSCR", [HK, 128, NTS, 128], BF16, kind="Internal")
    c.OOUT = dram("OOUT", [NT, 128, D], F32, kind="Internal")
    c.X1TM = dram("X1TM", [NT, 128, D], F32, kind="Internal")
    c.X1FM = dram("X1FM", [ND, 128, T], BF16, kind="Internal")
    c.FFOUT = dram("FFOUT", [NT, 128, D], F32, kind="Internal")
    c.ROWSCR = dram("ROWSCR", [8, 128], F32, kind="Internal")

    st = ExitStack()
    c.st = st
    c.semVC = st.enter_context(nc.semaphore("chainV"))
    c.vcnt = 0
    c.semSC = st.enter_context(nc.semaphore("chainS"))
    c.scnt = 0

    c.BIG1 = _sb(c, "BIG1", [128, 65536], BF16)
    c.BIG2 = _sb(c, "BIG2", [128, 28672], BF16)
    c.PREP = _sb(c, "PREP", [128, 8448], BF16)
    c.sm_idb = _sb(c, "sm_idb", [128, 128], BF16)
    c.sm_idf = _sb(c, "sm_idf", [128, 128], F32)
    c.sm_ones_row = _sb(c, "sm_ones_row", [1, 128], BF16)
    c.sm_ones_col = _sb(c, "sm_ones_col", [128, 1], BF16)
    c.sm_onesqf = _sb(c, "sm_onesqf", [128, 128], F32)
    c.sm_recl = _sb(c, "sm_recl", [1, 512], F32)
    c.sm_recb = _sb(c, "sm_recb", [1, 512], BF16)
    c.sm_exp8 = _sb(c, "sm_exp8", [8, 128], BF16)
    c.sm_qsc = _sb(c, "sm_qsc", [128, 32], F32)
    c.sm_osc = _sb(c, "sm_osc", [128, 32], F32)
    c.sm_bbc = _sb(c, "sm_bbc", [128, 32], F32)
    c.sm_kscin = _sb(c, "sm_kscin", [128, 8], F32)
    c.sm_vscin = _sb(c, "sm_vscin", [128, 8], F32)
    c.sm_kscale = _sb(c, "sm_kscale", [128, 8], F32)
    c.sm_vscale = _sb(c, "sm_vscale", [128, 8], F32)
    c.sm_nk5 = _sb(c, "sm_nk5", [1, 8], F32)
    c.sm_nq32 = _sb(c, "sm_nq32", [1, 32], F32)
    c.sm_sacc = _sb(c, "sm_sacc", [128, 1], F32)
    c.sm_stmp = _sb(c, "sm_stmp", [128, 1], F32)
    c.sm_scl = _sb(c, "sm_scl", [128, 1], F32)
    c.sm_scl2 = _sb(c, "sm_scl2", [128, 2], F32)
    c.sm_negu = _sb(c, "sm_negu", [128, 2], F32)
    c.sm_t1 = _sb(c, "sm_t1", [1, 8], F32)
    c.sm_t1b = _sb(c, "sm_t1b", [1, 8], BF16)
    c.sm_bn = _sb(c, "sm_bn", [128, 48], F32)
    c.sm_mv = _sb(c, "sm_mv", [128, 2], F32)
    c.sm_var = _sb(c, "sm_var", [128, 1], F32)
    c.sm_sd = _sb(c, "sm_sd", [128, 1], F32)
    c.sm_rstd = _sb(c, "sm_rstd", [128, 1], F32)
    c.sm_nmr = _sb(c, "sm_nmr", [128, 1], F32)
    c.sm_lgt = _sb(c, "sm_lgt", [128, 128], F32)
    c.sm_m8 = _sb(c, "sm_m8", [128, 8], F32)
    c.sm_ngm = _sb(c, "sm_ngm", [128, 1], F32)
    c.sm_p8e = _sb(c, "sm_p8e", [128, 8], F32)
    c.sm_sel = _sb(c, "sm_sel", [128, 8], F32)
    c.sm_psel = _sb(c, "sm_psel", [128, 8], F32)
    c.sm_den8 = _sb(c, "sm_den8", [128, 1], F32)
    c.sm_re8 = _sb(c, "sm_re8", [128, 1], F32)
    c.sm_w8 = _sb(c, "sm_w8", [128, 128], BF16)
    big2v = c.BIG2.ap()
    big2vf = big2v.bitcast(F32)
    big1v = c.BIG1.ap()
    c.sm_vscb = V(big2v[:, 22528:23552])
    c.sm_svrow = V(big2v[0:1, 26624:27648])
    c.sm_accv = V(big2vf[:, 11776:12800])
    c.sm_accg = V(big2vf[:, 12800:13824])
    c.sm_bcs = V(big2vf[:, 13824:14336])
    c.sm_wt8 = V(big2v[0:8, 0:2048])

    c.PSQ = st.enter_context(nc.psum_tensor("PSQ", [128, 2048], F32))
    c.PSA = st.enter_context(nc.psum_tensor("PSA", [128, 512], F32))
    c.PSD = st.enter_context(nc.psum_tensor("PSD", [128, 512], F32))
    c.PSB = st.enter_context(nc.psum_tensor("PSB", [128, 2048], BF16))

    c.AT = c.PREP.ap()[:, 0:4096].rearrange("p (a b) -> p a b", a=32)
    c.BT = c.PREP.ap()[:, 4096:8192].rearrange("p (a b) -> p a b", a=32)
    c.GT = c.PREP.ap()[:, 8192:8448].rearrange("p (a b) -> p a b", a=32)
    c.RES = c.BIG1.ap()[:, 0:65536].rearrange("p (a b) -> p a b", a=32)
    c.psbf = c.PSB.ap().bitcast(F32)

    try:
        phase0(c)
        if n_phases >= 1:
            phase1(c)
        if n_phases >= 2:
            phase2(c)
        if n_phases >= 3:
            phase34(c)
        if n_phases >= 5:
            phase5(c)
        if n_phases >= 6:
            phase6(c)
        if n_phases >= 7:
            phase7(c)
        if n_phases >= 8:
            phase8(c)
        if n_phases >= 9:
            phase9(c)
        else:
            _stub_output(c)
        _debug_dumps(c, dbg)
    finally:
        st.close()
    nc.finalize()
    return nc


def _debug_dumps(c, dbg):
    shapes = {"XFM": (c.XFM, 32, 4096), "QS": (c.QS, 32, 2048),
              "KSCR": (c.KSCR, 8, 4096), "VSCR": (c.VSCR, 8, 4096),
              "X1FM": (c.X1FM, 32, 2048)}
    todo = [nm for nm in dbg if nm in shapes]
    if not todo:
        return
    nc = c.nc
    for nm in todo:
        src, outer, inner = shapes[nm]
        out = nc.dram_tensor(nm + "_f32", [outer, 128, inner], F32, kind="ExternalOutput")
        with nc.semaphore("dbgl_" + nm) as semL, \
                nc.semaphore("dbgc_" + nm) as semC, \
                nc.semaphore("dbgs_" + nm) as semS, nc.Block() as blk:
            bufb = c.BIG2.ap()[:, 0:inner]
            buff = c.BIG2.ap().bitcast(F32)[:, 4096:4096 + inner]

            @blk.sync
            def _(sy):
                for o in range(outer):
                    if o >= 1:
                        sy.wait_ge(semC, o)
                    if nm == "XFM":
                        sy.dma_start(bufb, src.ap()[:, o, :, :]
                                     .rearrange("p a b -> p (a b)")).then_inc(semL, 16)
                    elif nm == "VSCR":
                        sy.dma_start(bufb, src.ap()[o, :, :, :]
                                     .rearrange("p a b -> p (a b)")).then_inc(semL, 16)
                    else:
                        sy.dma_start(bufb, src.ap()[o, :, :]).then_inc(semL, 16)

            @blk.vector
            def _(v):
                for o in range(outer):
                    v.wait_ge(semL, 16 * (o + 1))
                    if o >= 1:
                        v.wait_ge(semS, 16 * o)
                    v.tensor_copy(buff, bufb).then_inc(semC, 1)

            @blk.gpsimd
            def _(g):
                for o in range(outer):
                    g.wait_ge(semC, o + 1)
                    g.dma_start(out.ap()[o, :, :], buff).then_inc(semS, 16)
                    g.wait_ge(semS, 16 * (o + 1))


def _stub_output(c):
    nc = c.nc
    with nc.semaphore("stub_sem") as sem, nc.semaphore("stub_dma") as semd, \
            nc.Block() as blk:
        @blk.vector
        def _(v):
            z = c.BIG1.ap().bitcast(F32)[:, 0:4096]
            v.memset(z, 0.0).then_inc(sem, 1)

        @blk.gpsimd
        def _(g):
            g.wait_ge(sem, 1)
            z = c.BIG1.ap().bitcast(F32)[:, 0:4096]
            for tc in range(NT):
                g.dma_start(c.y.ap()[tc * 128:(tc + 1) * 128, :], z).then_inc(semd, 16)
                g.wait_ge(semd, 16 * (tc + 1))


def phase0(c):
    nc = c.nc
    sIN = _sem(c, "p0_in")
    sC = _sem(c, "p0_c")
    big1f = c.BIG1.ap().bitcast(F32)

    with nc.Block() as blk:
        @blk.sync
        def _(sy):
            n = 0
            for dst, src in [
                (c.sm_idb.ap(), c.id_bf.ap()),
                (c.sm_idf.ap(), c.id_f32.ap()),
                (c.sm_ones_row.ap(), c.ones_row.ap()),
                (c.sm_ones_col.ap(), c.ones_col.ap()),
                (c.sm_onesqf.ap(), c.ones_sq_f32.ap()),
                (c.sm_exp8.ap(), c.exp8.ap()),
            ]:
                sy.dma_start(dst, src).then_inc(sIN, 16)
                n += 16
            for i, src in enumerate([c.qsc_in, c.osc_in, c.bb_in]):
                sy.dma_start(big1f[0:32, i * 128:(i + 1) * 128], src.ap()).then_inc(sIN, 16)
                n += 16
            for i, src in enumerate([c.ksc_in, c.vsc_in]):
                sy.dma_start(big1f[0:8, 384 + i * 128:384 + (i + 1) * 128],
                             src.ap()).then_inc(sIN, 16)
                n += 16
            sy.dma_start(big1f[:, 1024:1280], c.gt_in.ap()).then_inc(sIN, 16)
            n += 16
            sy.dma_start(big1f[:, 2048:6144], c.lat_in.ap()).then_inc(sIN, 16)
            n += 16
            sy.dma_start(big1f[:, 6144:10240], c.lbt_in.ap()).then_inc(sIN, 16)
            n += 16
            c.p0_n_in = n

        @blk.tensor
        def _(t):
            t.wait_ge(sIN, c.p0_n_in)
            pf = c.psbf
            t.transpose(pf[:, 0:32], big1f[0:32, 0:128], c.sm_idf.ap()[0:32, 0:32])
            t.transpose(pf[:, 32:64], big1f[0:32, 128:256], c.sm_idf.ap()[0:32, 0:32])
            t.transpose(pf[:, 64:96], big1f[0:32, 256:384], c.sm_idf.ap()[0:32, 0:32])
            t.transpose(pf[:, 96:104], big1f[0:8, 384:512], c.sm_idf.ap()[0:8, 0:8])
            t.transpose(pf[:, 104:112], big1f[0:8, 512:640],
                        c.sm_idf.ap()[0:8, 0:8]).then_inc(sC, 1)

        @blk.vector
        def _(v):
            v.wait_ge(sC, 1)
            v.wait_ge(sIN, c.p0_n_in)
            pf = c.psbf
            v.tensor_copy(c.sm_qsc.ap(), pf[:, 0:32])
            v.tensor_copy(c.sm_osc.ap(), pf[:, 32:64])
            v.tensor_copy(c.sm_bbc.ap(), pf[:, 64:96])
            v.tensor_copy(c.sm_kscin.ap(), pf[:, 96:104])
            v.tensor_copy(c.sm_vscin.ap(), pf[:, 104:112])
            v.tensor_copy(c.GT.rearrange("p a b -> p (a b)"), big1f[:, 1024:1280])
            v.tensor_copy(c.AT.rearrange("p a b -> p (a b)"), big1f[:, 2048:6144])
            v.tensor_copy(c.BT.rearrange("p a b -> p (a b)"),
                          big1f[:, 6144:10240]).then_inc(sC, 1)


def phase1(c):
    nc = c.nc
    sX = DSem(c, "p1_x")
    sCst = _sem(c, "p1_cst")
    sTp = _sem(c, "p1_tp")
    sEv = _sem(c, "p1_ev")
    sMy = _sem(c, "p1_my")
    sOut = DSem(c, "p1_out")

    big2 = c.BIG2.ap()
    big2f = big2.bitcast(F32)
    XST = [big2f[:, 0:4096], big2f[:, 4096:8192]]
    XC = big2[:, 16384:20480]
    XSTG = [big2[:, 20480:24576], big2[:, 24576:28672]]
    psb = c.PSB.ap()

    with nc.Block() as blk:
        @blk.sync
        def _(sy):
            for tc in range(NTS):
                b = tc % 2
                if tc >= 2:
                    sy.wait_ge(sCst, tc - 1)
                sy.dma_start(XST[b],
                             c.x.ap()[tc * 128:(tc + 1) * 128, :]).then_inc(sX.h(tc), 16)

        @blk.scalar
        def _(s):
            for tc in range(NTS):
                b = tc % 2
                sX.wait(s, tc)
                if tc >= 1:
                    s.wait_ge(sTp, 4 * tc)
                s.activation(XC, XST[b], AF.Copy).then_inc(sCst, 1)

        @blk.tensor
        def _(t):
            for tc in range(NTS):
                b = tc % 2
                t.wait_ge(sCst, tc + 1)
                for g in range(4):
                    gi = tc * 4 + g
                    pb = psb[:, (gi % 2) * 1024:(gi % 2) * 1024 + 1024]
                    if gi >= 2:
                        t.wait_ge(sEv, gi - 1)
                    last = None
                    for j in range(8):
                        dc = g * 8 + j
                        last = t.transpose(pb[:, j * 128:(j + 1) * 128],
                                           XC[:, dc * 128:(dc + 1) * 128], c.sm_idb.ap())
                    last.then_inc(sTp, 1)

        @blk.vector
        def _(v):
            for tc in range(NTS):
                b = tc % 2
                for g in range(4):
                    gi = tc * 4 + g
                    pb = psb[:, (gi % 2) * 1024:(gi % 2) * 1024 + 1024]
                    v.wait_ge(sTp, gi + 1)
                    if tc >= 2 and g == 0:
                        sOut.wait(v, tc - 2)
                    v.tensor_copy(
                        XSTG[b].rearrange("p (a b) -> p a b", a=32)[:, g * 8:(g + 1) * 8, :],
                        pb.rearrange("p (a b) -> p a b", a=8)).then_inc(sEv, 1)
                if tc < NT:
                    v.wait_ge(sEv, 4 * (tc + 1))
                    v.tensor_copy(c.RES[:, :, tc * 128:(tc + 1) * 128],
                                  XSTG[b].rearrange("p (a b) -> p a b", a=32)).then_inc(sMy, 1)

        @blk.gpsimd
        def _(g):
            nmy = 0
            for tc in range(NTS):
                b = tc % 2
                g.wait_ge(sEv, 4 * (tc + 1))
                if tc < NT:
                    nmy += 1
                    g.wait_ge(sMy, nmy)
                g.dma_start(c.XFM.ap()[:, tc, :, :],
                            XSTG[b].rearrange("p (a b) -> p a b", a=32)
                            ).then_inc(sOut.h(tc), 16)
            sOut.wait(g, NTS - 2)
            sOut.wait(g, NTS - 1)


class ProjSems:

    KEYS = ["ab", "sg", "tp", "te", "mm", "ev", "ot", "oe", "qsq", "nqm",
            "nqr", "nqd", "pre"]

    def __init__(self, c):
        self.h = {k: _sem(c, "proj_" + k) for k in self.KEYS}
        self.base = {k: 0 for k in self.KEYS}
        self.dW = DSem(c, "proj_w", n=4)
        self.dD = DSem(c, "proj_dm")

    def wg(self, eng, k, v):
        if self.base[k] + v > 0:
            eng.wait_ge(self.h[k], self.base[k] + v)

    def inc(self, inst, k, n=1):
        inst.then_inc(self.h[k], n)

    def fin(self, used):
        for k, n in used.items():
            self.base[k] += n


def emit_proj(c, name, w_dram, scale_col, const_scale, has_sign, out_mode,
              out_dram, preload=None, n_preload=0, lora=False, qnorms=False,
              guard=None):
    nc = c.nc
    if not hasattr(c, "projsems"):
        c.projsems = ProjSems(c)
    P = c.projsems
    sW = P.dW
    sDm = P.dD
    gsem, gval = guard() if guard is not None else (None, 0)

    big2 = c.BIG2.ap()
    big2f = big2.bitcast(F32)
    WST = [big2f[:, j * 1024:(j + 1) * 1024] for j in range(4)]
    WSG = [big2[:, 8192 + j * 1024:8192 + (j + 1) * 1024] for j in range(4)]
    WT = [big2[:, 12288:16384].rearrange("p (a b) -> p a b", a=32),
          big2[:, 16384:20480].rearrange("p (a b) -> p a b", a=32)]
    QEVA = [big2[:, 20480:22528], big2[:, 22528:24576]]
    QSQ2 = [big2[:, 24576:26624], big2[:, 26624:28672]]
    OEVA = [big2f[:, 10240 + q * 512:10240 + (q + 1) * 512] for q in range(4)]
    OSTG = [big2f[:, 13312:13824], big2f[:, 13824:14336]]
    psq = c.PSQ.ap()
    psb = c.PSB.ap()
    pso = [c.PSA.ap(), c.PSD.ap()]
    psn = [c.PSA.ap()[0:1, 0:512], c.PSD.ap()[0:1, 0:512]]
    f32_tm = out_mode == "f32_tm"
    LOWW = big2[:, 24576:26624] if lora else None

    with nc.Block() as blk:
        @blk.sync
        def _(sy):
            if gsem is not None:
                sy.wait_ge(gsem, gval)
            if preload is not None:
                preload(sy, P.h["pre"])
            for do in range(ND):
                for qq in range(4):
                    i = do * 4 + qq
                    b = i % 4
                    if i >= 4:
                        if has_sign:
                            P.wg(sy, "ab", i - 3)
                        P.wg(sy, "sg", i - 3)
                    sy.dma_start(WST[b],
                                 w_dram.ap()[do * 128:(do + 1) * 128,
                                             qq * 1024:(qq + 1) * 1024]
                                 ).then_inc(sW.h(i), 16)

        @blk.scalar
        def _(s):
            if gsem is not None:
                s.wait_ge(gsem, gval)
            func = AF.Sign if has_sign else AF.Copy
            for do in range(ND):
                for qq in range(4):
                    i = do * 4 + qq
                    sW.wait(s, i)
                    if i >= 3:
                        P.wg(s, "tp", i - 2)
                    P.inc(s.activation(WSG[i % 4], WST[i % 4], func), "sg")

        @blk.tensor
        def _(t):
            if gsem is not None:
                t.wait_ge(gsem, gval)
            if preload is not None:
                t.wait_ge(P.h["pre"], P.base["pre"] + n_preload)
            for do in range(ND):
                bdo = do % 2
                for qq in range(4):
                    i = do * 4 + qq
                    P.wg(t, "sg", i + 1)
                    if i >= 2:
                        P.wg(t, "te", i - 1)
                    pb = psb[:, (i % 2) * 1024:(i % 2) * 1024 + 1024]
                    last = None
                    for j in range(8):
                        last = t.transpose(pb[:, j * 128:(j + 1) * 128],
                                           WSG[i % 4][:, j * 128:(j + 1) * 128],
                                           c.sm_idb.ap())
                    P.inc(last, "tp")
                if f32_tm and do >= 1:
                    _proj_out_tr(c, t, do - 1, OEVA, pso, P)
                if do >= 1:
                    P.wg(t, "ev", 4 * do)
                last = None
                for dc in range(ND):
                    if dc % 8 == 0:
                        P.wg(t, "te", 4 * do + dc // 8 + 1)
                    for qt in range(NQT):
                        last = t.matmul(psq[:, qt * 512:(qt + 1) * 512],
                                        WT[bdo][:, dc, :],
                                        c.RES[:, dc, qt * 512:(qt + 1) * 512],
                                        start=(dc == 0),
                                        stop=(dc == ND - 1 and not lora))
                if lora:
                    for qt in range(NQT):
                        last = t.matmul(psq[:, qt * 512:(qt + 1) * 512],
                                        c.BT[:, do, :],
                                        LOWW[:, qt * 512:(qt + 1) * 512],
                                        start=False, stop=True)
                P.inc(last, "mm")
                if qnorms and do >= 2:
                    _proj_nq_mm(c, t, do - 2, QSQ2, psn, P)
            if qnorms:
                _proj_nq_mm(c, t, ND - 2, QSQ2, psn, P)
                _proj_nq_mm(c, t, ND - 1, QSQ2, psn, P)
            if f32_tm:
                _proj_out_tr(c, t, ND - 1, OEVA, pso, P)

        @blk.vector
        def _(v):
            if gsem is not None:
                v.wait_ge(gsem, gval)
            for do in range(ND):
                if do >= 1:
                    _proj_out_evac(c, v, do - 1, out_mode, psq, QEVA, OEVA, OSTG, pso,
                                   P, sDm, has_sign, scale_col, qnorms, QSQ2, psn)
                for qq in range(4):
                    i = do * 4 + qq
                    P.wg(v, "tp", i + 1)
                    if do >= 2 and qq == 0:
                        P.wg(v, "mm", do - 1)
                    P.inc(v.tensor_copy(WT[do % 2][:, qq * 8:(qq + 1) * 8, :],
                                        psb[:, (i % 2) * 1024:(i % 2) * 1024 + 1024]
                                        .rearrange("p (a b) -> p a b", a=8)), "te")
                if has_sign:
                    for qq in range(4):
                        i = do * 4 + qq
                        sW.wait(v, i)
                        if i >= 1:
                            P.wg(v, "ab", i)
                        if qq == 0:
                            vw(c, v)
                            P.inc(v.reduce_sum(c.sm_sacc.ap(), WST[i % 4], axis=AX.X,
                                               apply_absolute_value=True), "ab")
                        else:
                            vc(c, v.reduce_sum(c.sm_stmp.ap(), WST[i % 4], axis=AX.X,
                                               apply_absolute_value=True))
                            vw(c, v)
                            P.inc(v.tensor_add(c.sm_sacc.ap(), c.sm_sacc.ap(),
                                               c.sm_stmp.ap()), "ab")
                    P.wg(v, "ab", 4 * do + 4)
                    vc(c, v.tensor_scalar(c.sm_scl.ap(), c.sm_sacc.ap(), 1.0 / 4096.0,
                                          1e-8, ALU.mult, ALU.max))
                    if do >= 2:
                        P.wg(v, "ev", 4 * do - 4)
                    vw(c, v)
                    vc(c, v.tensor_scalar(c.sm_scl2.ap()[:, do % 2:do % 2 + 1],
                                          c.sm_scl.ap(), scale_col[:, do:do + 1],
                                          const_scale, ALU.mult, ALU.mult))
                if qnorms and do >= 1:
                    _proj_qnorm_sq(c, v, do - 1, QEVA, QSQ2, psn, P)
            _proj_out_evac(c, v, ND - 1, out_mode, psq, QEVA, OEVA, OSTG, pso,
                           P, sDm, has_sign, scale_col, qnorms, QSQ2, psn)
            if qnorms:
                _proj_qnorm_sq(c, v, ND - 1, QEVA, QSQ2, psn, P)
                _proj_qnorm_red(c, v, ND - 1, psn, P)

        @blk.gpsimd
        def _(g):
            if out_mode == "bf16_fm":
                for do in range(ND):
                    P.wg(g, "ev", 4 * do + 4)
                    g.dma_start(out_dram.ap()[do, :, :],
                                QEVA[do % 2]).then_inc(sDm.h(do), 16)
                sDm.wait(g, ND - 2)
                sDm.wait(g, ND - 1)
            else:
                for do in range(ND):
                    for qt in range(NQT):
                        oq = do * 4 + qt
                        P.wg(g, "oe", oq + 1)
                        b = oq % 2
                        for j in range(4):
                            tc = qt * 4 + j
                            g.dma_start(out_dram.ap()[tc, :, do * 128:(do + 1) * 128],
                                        OSTG[b][:, j * 128:(j + 1) * 128]
                                        ).then_inc(sDm.h(oq), 16)
                sDm.wait(g, 4 * ND - 2, per=64)
                sDm.wait(g, 4 * ND - 1, per=64)

    used = {"sg": 128, "tp": 128, "te": 128, "mm": 32, "ev": 4 * ND}
    if has_sign:
        used["ab"] = 128
    if f32_tm:
        used["ot"] = 4 * ND
        used["oe"] = 4 * ND
    if qnorms:
        used["qsq"] = ND
        used["nqm"] = 4 * ND
        used["nqr"] = ND
        used["nqd"] = 4 * ND
        c.sNqr = P.h["nqr"]
        c.nqr_val = P.base["nqr"] + ND
    if preload is not None:
        used["pre"] = n_preload
    sW.close(128, 16)
    if out_mode == "bf16_fm":
        sDm.close(ND, 16)
    else:
        sDm.close(4 * ND, 64)
    P.fin(used)


def _proj_qnorm_sq(c, v, d, QEVA, QSQ2, psn, P):
    P.wg(v, "ev", 4 * d + 4)
    if d >= 2:
        P.wg(v, "nqm", 4 * (d - 1))
    P.inc(v.tensor_tensor(QSQ2[d % 2], QEVA[d % 2], QEVA[d % 2], ALU.mult), "qsq")
    if d >= 1:
        _proj_qnorm_red(c, v, d - 1, psn, P)


def _proj_nq_mm(c, t, dom, QSQ2, psn, P):
    P.wg(t, "qsq", dom + 1)
    if dom >= 1:
        P.wg(t, "nqr", dom)
    for qt in range(NQT):
        if qt >= 2:
            P.wg(t, "nqd", 4 * dom + qt - 1)
        P.inc(t.matmul(psn[qt % 2], c.sm_ones_col.ap(),
                       QSQ2[dom % 2][:, qt * 512:(qt + 1) * 512],
                       start=True, stop=True), "nqm")


def _proj_out_tr(c, t, do, OEVA, pso, P):
    for qt in range(NQT):
        oq = do * 4 + qt
        P.wg(t, "ev", 4 * do + qt + 1)
        if oq >= 2:
            P.wg(t, "oe", oq - 1)
        last = None
        pf = pso[oq % 2]
        for j in range(4):
            last = t.transpose(pf[:, j * 128:(j + 1) * 128],
                               OEVA[qt][:, j * 128:(j + 1) * 128], c.sm_idf.ap())
        P.inc(last, "ot")


def _proj_qnorm_red(c, v, do, psn, P):
    for qt in range(NQT):
        P.wg(v, "nqm", 4 * do + qt + 1)
        vw(c, v)
        if qt == 0:
            P.inc(v.tensor_reduce(c.sm_t1.ap()[:, 0:1], psn[0], AX.X, ALU.max), "nqd")
        else:
            P.inc(v.tensor_reduce(c.sm_t1.ap()[:, 1:2], psn[qt % 2], AX.X,
                                  ALU.max), "nqd")
            P.wg(v, "nqd", 4 * do + qt + 1)
            vc(c, v.tensor_max(c.sm_t1.ap()[:, 0:1], c.sm_t1.ap()[:, 0:1],
                               c.sm_t1.ap()[:, 1:2]))
    P.wg(v, "nqd", 4 * do + 1)
    vw(c, v)
    P.inc(v.tensor_scalar(c.sm_nq32.ap()[:, do:do + 1], c.sm_t1.ap()[:, 0:1],
                          NQF, None, ALU.mult), "nqr")


def _proj_out_evac(c, v, do, out_mode, psq, QEVA, OEVA, OSTG, pso,
                   P, sDm, has_sign, scale_col, qnorms, QSQ2, psn):
    if out_mode == "bf16_fm":
        P.wg(v, "mm", do + 1)
        vw(c, v)
        b = do % 2
        if do >= 2:
            sDm.wait(v, do - 2)
        for qt in range(NQT):
            P.inc(v.tensor_scalar(QEVA[b][:, qt * 512:(qt + 1) * 512],
                                  psq[:, qt * 512:(qt + 1) * 512],
                                  c.sm_scl2.ap()[:, b:b + 1], None, ALU.mult), "ev")

    else:
        if do >= 1:
            for qt in range(NQT):
                oqp = (do - 1) * 4 + qt
                P.wg(v, "ot", oqp + 1)
                if oqp >= 2:
                    sDm.wait(v, oqp - 2, per=64)
                P.inc(v.tensor_copy(OSTG[oqp % 2], pso[oqp % 2]), "oe")
        P.wg(v, "mm", do + 1)
        vw(c, v)
        for qt in range(NQT):
            if has_sign:
                P.inc(v.tensor_scalar(OEVA[qt], psq[:, qt * 512:(qt + 1) * 512],
                                      c.sm_scl2.ap()[:, do % 2:do % 2 + 1], None,
                                      ALU.mult), "ev")
            else:
                P.inc(v.tensor_scalar(OEVA[qt], psq[:, qt * 512:(qt + 1) * 512],
                                      scale_col[:, do:do + 1], None, ALU.add), "ev")
        if do == ND - 1:
            for qt in range(NQT):
                oq = (ND - 1) * 4 + qt
                P.wg(v, "ot", oq + 1)
                sDm.wait(v, oq - 2, per=64)
                P.inc(v.tensor_copy(OSTG[oq % 2], pso[oq % 2]), "oe")


def phase2(c):
    emit_proj(c, "p2", c.wq, c.sm_qsc.ap(), RSQ, True, "bf16_fm", c.QS, qnorms=True)


def phase6(c):
    emit_proj(c, "p6", c.wo, c.sm_osc.ap(), 1.0, True, "f32_tm", c.OOUT,
              guard=lambda: (c.sAo5, H * NQT))


def phase34(c):
    nc = c.nc
    sW = DSem(c, "p34_w")
    sAb = _sem(c, "p34_ab")
    sSg = _sem(c, "p34_sg")
    sTp = _sem(c, "p34_tp")
    sTe = _sem(c, "p34_te")
    sSc = _sem(c, "p34_sc")
    sR = _sem(c, "p34_row")
    sX = DSem(c, "p34_x")
    sMM = _sem(c, "p34_mm")
    sEv = _sem(c, "p34_ev")
    sKt = _sem(c, "p34_kt")
    sKf = _sem(c, "p34_kf")
    sKsq = _sem(c, "p34_ksq")
    sNKm = _sem(c, "p34_nkm")
    sNKr = _sem(c, "p34_nkr")
    c.sNKr = sNKr
    sVd = DSem(c, "p34_vd")
    sKd = DSem(c, "p34_kd")

    big1 = c.BIG1.ap()
    big2 = c.BIG2.ap()
    big2f = big2.bitcast(F32)
    WKT = big1[:, 0:32768].rearrange("p (a b) -> p a b", a=32)
    WVT = big1[:, 32768:65536].rearrange("p (a b) -> p a b", a=32)
    WST = [big2f[:, 0:1024], big2f[:, 1024:2048]]
    WSG = [big2[:, 4096:5120], big2[:, 5120:6144]]
    XT = [big2[:, 6144:10240].rearrange("p (a b) -> p a b", a=32),
          big2[:, 10240:14336].rearrange("p (a b) -> p a b", a=32)]
    KTM = [big2[:, 14336:15360], big2[:, 15360:16384]]
    VTM = [big2[:, 16384:17408], big2[:, 17408:18432]]
    KFM = [big2[:, 18432:19456], big2[:, 19456:20480]]
    KSQB = [big2[:, 20480:21504], big2[:, 21504:22528]]
    psq = c.PSQ.ap()
    psb = c.PSB.ap()
    psn = [c.PSA.ap()[0:1, 0:512], c.PSD.ap()[0:1, 0:512]]

    with nc.Block() as blk:
        @blk.sync
        def _(sy):
            i = 0
            for w in range(2):
                wd = c.wk if w == 0 else c.wv
                for rc in range(8):
                    for qq in range(4):
                        if i >= 2:
                            sy.wait_ge(sAb, i - 1)
                            sy.wait_ge(sSg, i - 1)
                        sy.dma_start(WST[i % 2],
                                     wd.ap()[rc * 128:(rc + 1) * 128,
                                             qq * 1024:(qq + 1) * 1024]
                                     ).then_inc(sW.h(i), 16)
                        i += 1

        @blk.scalar
        def _(s):
            for i in range(64):
                sW.wait(s, i)
                if i >= 1:
                    s.wait_ge(sTp, i)
                s.activation(WSG[i % 2], WST[i % 2], AF.Sign).then_inc(sSg, 1)

        @blk.tensor
        def _(t):
            for i in range(64):
                t.wait_ge(sSg, i + 1)
                if i >= 1:
                    t.wait_ge(sTe, i)
                pb = psb[:, 0:1024]
                last = None
                for j in range(8):
                    last = t.transpose(pb[:, j * 128:(j + 1) * 128],
                                       WSG[i % 2][:, j * 128:(j + 1) * 128], c.sm_idb.ap())
                last.then_inc(sTp, 1)

        @blk.vector
        def _(v):
            i = 0
            for w in range(2):
                WTd = WKT if w == 0 else WVT
                for rc in range(8):
                    for qq in range(4):
                        sW.wait(v, i)
                        if i >= 1:
                            v.wait_ge(sAb, i)
                        if qq == 0:
                            vw(c, v)
                            v.reduce_sum(c.sm_sacc.ap(), WST[i % 2], axis=AX.X,
                                         apply_absolute_value=True).then_inc(sAb, 1)
                        else:
                            vc(c, v.reduce_sum(c.sm_stmp.ap(), WST[i % 2], axis=AX.X,
                                               apply_absolute_value=True))
                            vw(c, v)
                            v.tensor_add(c.sm_sacc.ap(), c.sm_sacc.ap(),
                                         c.sm_stmp.ap()).then_inc(sAb, 1)
                        v.wait_ge(sTp, i + 1)
                        v.tensor_copy(WTd[:, qq * 8:(qq + 1) * 8, rc * 128:(rc + 1) * 128],
                                      psb[:, 0:1024].rearrange("p (a b) -> p a b", a=8)
                                      ).then_inc(sTe, 1)
                        i += 1
                    v.wait_ge(sAb, i)
                    vc(c, v.tensor_scalar(c.sm_scl.ap(), c.sm_sacc.ap(), 1.0 / 4096.0,
                                          1e-8, ALU.mult, ALU.max))
                    dst = c.sm_kscale.ap() if w == 0 else c.sm_vscale.ap()
                    scol = c.sm_kscin.ap() if w == 0 else c.sm_vscin.ap()
                    vw(c, v)
                    lastsc = v.tensor_scalar(dst[:, rc:rc + 1], c.sm_scl.ap(),
                                             scol[:, rc:rc + 1], None, ALU.mult)
            lastsc.then_inc(sSc, 1)

    sRd = _sem(c, "p34_rowd")
    with nc.Block() as blk:
        @blk.tensor
        def _(t):
            t.wait_ge(sSc, 1)
            t.transpose(c.psbf[0:8, 0:128], c.sm_vscale.ap(), c.sm_idf.ap()).then_inc(sR, 1)
            t.wait_ge(sRd, 32)
            t.matmul(psq[:, 0:512], c.sm_ones_row.ap(), c.sm_svrow.ap()[:, 0:512],
                     start=True, stop=True)
            t.matmul(psq[:, 512:1024], c.sm_ones_row.ap(),
                     c.sm_svrow.ap()[:, 512:1024], start=True, stop=True).then_inc(sR, 1)

        @blk.vector
        def _(v):
            v.wait_ge(sR, 1)
            v.tensor_copy(big2f[0:8, 2048:2176], c.psbf[0:8, 0:128]).then_inc(sR, 1)
            v.wait_ge(sR, 3)
            v.tensor_copy(c.sm_vscb.ap(), psq[:, 0:1024]).then_inc(sR, 1)

        @blk.gpsimd
        def _(g):
            g.wait_ge(sR, 2)
            g.dma_start(c.ROWSCR.ap(), big2f[0:8, 2048:2176]).then_inc(sRd, 16)
            g.wait_ge(sRd, 16)
            g.dma_start(c.sm_svrow.ap(),
                        c.ROWSCR.ap().rearrange("a b -> (a b)").unsqueeze(0)).then_inc(sRd, 16)

    with nc.Block() as blk:
        @blk.sync
        def _(sy):
            for tc in range(NTS):
                if tc >= 2:
                    sy.wait_ge(sMM, tc - 1)
                sy.dma_start(XT[tc % 2].rearrange("p a b -> p (a b)"),
                             c.XFM.ap()[:, tc, :, :].rearrange("p a b -> p (a b)")
                             ).then_inc(sX.h(tc), 16)

        @blk.tensor
        def _(t):
            for tc in range(NTS):
                sX.wait(t, tc)
                if tc >= 1:
                    t.wait_ge(sEv, 2 * tc)
                for dc in range(ND):
                    lhs = XT[tc % 2][:, dc, :]
                    t.matmul(psq[:, 0:512], lhs, WKT[:, dc, 0:512],
                             start=(dc == 0), stop=(dc == ND - 1))
                    t.matmul(psq[:, 512:1024], lhs, WKT[:, dc, 512:1024],
                             start=(dc == 0), stop=(dc == ND - 1))
                    t.matmul(psq[:, 1024:1536], lhs, WVT[:, dc, 0:512],
                             start=(dc == 0), stop=(dc == ND - 1))
                    lastmm = t.matmul(psq[:, 1536:2048], lhs, WVT[:, dc, 512:1024],
                                      start=(dc == 0), stop=(dc == ND - 1))
                lastmm.then_inc(sMM, 1)
                if tc >= 1:
                    t.wait_ge(sEv, 2 * (tc - 1) + 1)
                    if tc >= 3:
                        t.wait_ge(sKf, 8 * (tc - 2))
                    pb = psb[:, ((tc - 1) % 2) * 1024:((tc - 1) % 2) * 1024 + 1024]
                    last = None
                    for kvh in range(8):
                        last = t.transpose(pb[:, kvh * 128:(kvh + 1) * 128],
                                           KTM[(tc - 1) % 2][:, kvh * 128:(kvh + 1) * 128],
                                           c.sm_idb.ap())
                    last.then_inc(sKt, 1)
                if tc >= 2:
                    t.wait_ge(sKsq, tc - 1)
                    if tc >= 3:
                        t.wait_ge(sNKr, 8 * (tc - 2))
                    for kvh in range(8):
                        if kvh >= 2:
                            t.wait_ge(sNKr, 8 * (tc - 2) + kvh - 1)
                        t.matmul(psn[kvh % 2][:, 0:128], c.sm_ones_col.ap(),
                                 KSQB[(tc - 2) % 2][:, kvh * 128:(kvh + 1) * 128],
                                 start=True, stop=True).then_inc(sNKm, 1)
            for tcl in [NTS - 1]:
                t.wait_ge(sEv, 2 * tcl + 1)
                t.wait_ge(sKf, 8 * (tcl - 1))
                pb = psb[:, (tcl % 2) * 1024:(tcl % 2) * 1024 + 1024]
                last = None
                for kvh in range(8):
                    last = t.transpose(pb[:, kvh * 128:(kvh + 1) * 128],
                                       KTM[tcl % 2][:, kvh * 128:(kvh + 1) * 128],
                                       c.sm_idb.ap())
                last.then_inc(sKt, 1)
            for tcl in [NTS - 2, NTS - 1]:
                t.wait_ge(sKsq, tcl + 1)
                t.wait_ge(sNKr, 8 * tcl)
                for kvh in range(8):
                    if kvh >= 2:
                        t.wait_ge(sNKr, 8 * tcl + kvh - 1)
                    t.matmul(psn[kvh % 2][:, 0:128], c.sm_ones_col.ap(),
                             KSQB[tcl % 2][:, kvh * 128:(kvh + 1) * 128],
                             start=True, stop=True).then_inc(sNKm, 1)

        @blk.vector
        def _(v):
            for tc in range(NTS):
                v.wait_ge(sMM, tc + 1)
                if tc == 0:
                    v.wait_ge(sR, 4)
                if tc >= 2:
                    v.wait_ge(sKt, tc - 1)
                    sVd.wait(v, tc - 2, per=128)
                v.tensor_copy(KTM[tc % 2], psq[:, 0:1024]).then_inc(sEv, 1)
                v.tensor_tensor(VTM[tc % 2], psq[:, 1024:2048], c.sm_vscb.ap(),
                                ALU.mult).then_inc(sEv, 1)
                if tc >= 1:
                    tcm = tc - 1
                    v.wait_ge(sKt, tcm + 1)
                    if tcm >= 2:
                        sKd.wait(v, tcm - 2, per=128)
                    if tcm >= 2:
                        v.wait_ge(sNKm, 8 * (tcm - 1))
                    pb = psb[:, (tcm % 2) * 1024:(tcm % 2) * 1024 + 1024]
                    for kvh in range(8):
                        v.tensor_scalar(KFM[tcm % 2][:, kvh * 128:(kvh + 1) * 128],
                                        pb[:, kvh * 128:(kvh + 1) * 128],
                                        c.sm_kscale.ap()[:, kvh:kvh + 1], None,
                                        ALU.mult).then_inc(sKf, 1)
                    v.wait_ge(sKf, 8 * (tcm + 1))
                    v.tensor_tensor(KSQB[tcm % 2], KFM[tcm % 2], KFM[tcm % 2],
                                    ALU.mult).then_inc(sKsq, 1)
                if tc >= 2:
                    _p34_nkred(c, v, tc - 2, psn, sNKm, sNKr)
            tcm = NTS - 1
            v.wait_ge(sKt, tcm + 1)
            sKd.wait(v, tcm - 2, per=128)
            v.wait_ge(sNKm, 8 * (tcm - 1))
            pb = psb[:, (tcm % 2) * 1024:(tcm % 2) * 1024 + 1024]
            for kvh in range(8):
                v.tensor_scalar(KFM[tcm % 2][:, kvh * 128:(kvh + 1) * 128],
                                pb[:, kvh * 128:(kvh + 1) * 128],
                                c.sm_kscale.ap()[:, kvh:kvh + 1], None,
                                ALU.mult).then_inc(sKf, 1)
            v.wait_ge(sKf, 8 * (tcm + 1))
            v.tensor_tensor(KSQB[tcm % 2], KFM[tcm % 2], KFM[tcm % 2],
                            ALU.mult).then_inc(sKsq, 1)
            _p34_nkred(c, v, NTS - 2, psn, sNKm, sNKr)
            _p34_nkred(c, v, NTS - 1, psn, sNKm, sNKr)
            v.wait_ge(sNKr, 8 * NTS)
            v.tensor_scalar(c.sm_nk5.ap(), c.sm_nk5.ap(), NKF, None, ALU.mult
                            ).then_inc(sNKr, 1)

        @blk.gpsimd
        def _(g):
            for tc in range(NTS):
                g.wait_ge(sEv, 2 * (tc + 1))
                for kvh in range(8):
                    g.dma_start(c.VSCR.ap()[kvh, :, tc, :],
                                VTM[tc % 2][:, kvh * 128:(kvh + 1) * 128]
                                ).then_inc(sVd.h(tc), 16)
                g.wait_ge(sKf, 8 * (tc + 1))
                for kvh in range(8):
                    g.dma_start(c.KSCR.ap()[kvh, :, tc * 128:(tc + 1) * 128],
                                KFM[tc % 2][:, kvh * 128:(kvh + 1) * 128]
                                ).then_inc(sKd.h(tc), 16)
            sVd.wait(g, NTS - 2, per=128)
            sVd.wait(g, NTS - 1, per=128)
            sKd.wait(g, NTS - 2, per=128)
            sKd.wait(g, NTS - 1, per=128)


def _p34_nkred(c, v, tc, psn, sNKm, sNKr):
    for kvh in range(8):
        v.wait_ge(sNKm, 8 * tc + kvh + 1)
        if tc == 0:
            v.tensor_reduce(c.sm_nk5.ap()[:, kvh:kvh + 1], psn[kvh % 2][:, 0:128],
                            AX.X, ALU.max).then_inc(sNKr, 1)
        else:
            v.wait_ge(sNKr, 8 * tc + kvh)
            vc(c, v.tensor_reduce(c.sm_t1.ap()[:, 3:4], psn[kvh % 2][:, 0:128],
                                  AX.X, ALU.max))
            v.wait_ge(sNKr, 8 * (tc - 1) + kvh + 1)
            vw(c, v)
            v.tensor_max(c.sm_nk5.ap()[:, kvh:kvh + 1],
                         c.sm_nk5.ap()[:, kvh:kvh + 1],
                         c.sm_t1.ap()[:, 3:4]).then_inc(sNKr, 1)


def phase5(c):
    nc = c.nc
    sKV = DSem(c, "p5_kv")
    sQ = DSem(c, "p5_q")
    sNr = _sem(c, "p5_nr")
    sBm = _sem(c, "p5_bm")
    sNg = _sem(c, "p5_ng")
    sQK = _sem(c, "p5_qk")
    sEx = _sem(c, "p5_ex")
    sF1 = _sem(c, "p5_f1")
    sPV = _sem(c, "p5_pv")
    sDn = _sem(c, "p5_dnm")
    sRc = _sem(c, "p5_rc")
    sBc = _sem(c, "p5_bc")
    sAo = _sem(c, "p5_ao")
    c.sAo5 = sAo

    big2 = c.BIG2.ap()
    KBUF = [big2[:, 0:4096], big2[:, 4096:8192]]
    VBUF = [big2[:, 8192:12288], big2[:, 12288:16384]]
    QBUF = [big2[:, 16384:18432], big2[:, 18432:20480]]
    PT = [big2[:, 20480 + j * 1024:20480 + (j + 1) * 1024] for j in range(3)]
    FOLD = [big2[:, 23552 + j * 512:23552 + (j + 1) * 512] for j in range(3)]
    BCS = c.sm_bcs.ap()
    psq = c.PSQ.ap()
    psa = c.PSA.ap()
    psd = c.PSD.ap()
    DEN = c.psbf[0:1, 512:1024]

    with nc.Block() as blk:
        @blk.sync
        def _(sy):
            for kvh in range(HK):
                if kvh >= 2:
                    sy.wait_ge(sPV, NP * 16 * (kvh - 1))
                sy.dma_start(KBUF[kvh % 2],
                             c.KSCR.ap()[kvh, :, :]).then_inc(sKV.h(kvh), 16)
                sy.dma_start(VBUF[kvh % 2], c.VSCR.ap()[kvh, :, :, :]
                             .rearrange("p a b -> p (a b)")).then_inc(sKV.h(kvh), 16)
                for r in range(4):
                    h = kvh * 4 + r
                    if h >= 2:
                        sy.wait_ge(sQK, NP * 4 * (h - 1))
                    sy.dma_start(QBUF[h % 2], c.QS.ap()[h, :, :]).then_inc(sQ.h(h), 16)

        @blk.tensor
        def _(t):
            t.wait_ge(sNr, 1)
            t.matmul(c.psbf[:, 0:1], c.sm_ones_row.ap(), c.sm_t1b.ap()[:, 0:1],
                     start=True, stop=True).then_inc(sBm, 1)
            for kvh in range(HK):
                bk = kvh % 2
                sKV.wait(t, kvh, per=32)
                for r in range(4):
                    h = kvh * 4 + r
                    bq = h % 2
                    sQ.wait(t, h)
                    for qt in range(NQT):
                        i = h * 4 + qt
                        for p in range(NP):
                            if NP * i + p - 1 > 0:
                                t.wait_ge(sEx, NP * i + p - 1)
                            last = None
                            for u in range(2):
                                cc = 2 * p + u
                                last = t.matmul(psq[:, (cc % 4) * 512:(cc % 4) * 512 + 512],
                                                KBUF[bk][:, cc * 128:(cc + 1) * 128],
                                                QBUF[bq][:, qt * 512:(qt + 1) * 512],
                                                start=True, stop=True)
                            last.then_inc(sQK, 1)
                            if p == 0 and i >= 1:
                                t.wait_ge(sRc, i)
                                t.wait_ge(sAo, i - 1)
                                t.matmul(psd[:, 0:512], c.sm_ones_row.ap(),
                                         c.sm_recb.ap(), start=True,
                                         stop=True).then_inc(sBc, 1)
                            if p >= 2:
                                _p5_pv(c, t, i, p - 2, bk, VBUF, PT, psa,
                                       sEx, sPV, sAo)
                            if p >= 3:
                                _p5_den(c, t, i, p - 3, FOLD, sF1, sDn, sRc)
                            if qt == 3 and p == 4 and h + 1 < H:
                                t.wait_ge(sNr, h + 2)
                                t.wait_ge(sNg, h + 1)
                                t.matmul(c.psbf[:, 0:1], c.sm_ones_row.ap(),
                                         c.sm_t1b.ap()[:, (h + 1) % 2:(h + 1) % 2 + 1],
                                         start=True, stop=True).then_inc(sBm, 1)
                        _p5_pv(c, t, i, NP - 2, bk, VBUF, PT, psa, sEx, sPV, sAo)
                        _p5_den(c, t, i, NP - 3, FOLD, sF1, sDn, sRc)
                        _p5_pv(c, t, i, NP - 1, bk, VBUF, PT, psa, sEx, sPV, sAo)
                        _p5_den(c, t, i, NP - 2, FOLD, sF1, sDn, sRc)
                        _p5_den(c, t, i, NP - 1, FOLD, sF1, sDn, sRc)
                        if i == H * NQT - 1:
                            t.wait_ge(sRc, i + 1)
                            t.wait_ge(sAo, i)
                            t.matmul(psd[:, 0:512], c.sm_ones_row.ap(),
                                     c.sm_recb.ap(), start=True,
                                     stop=True).then_inc(sBc, 1)

        @blk.vector
        def _(v):
            v.wait_ge(c.sNqr, c.nqr_val)
            v.wait_ge(c.sNKr, 8 * NTS + 1)
            vc(c, v.tensor_add(c.sm_t1.ap()[:, 0:1], c.sm_nq32.ap()[:, 0:1],
                               c.sm_nk5.ap()[:, 0:1]))
            vw(c, v)
            v.tensor_copy(c.sm_t1b.ap()[:, 0:1], c.sm_t1.ap()[:, 0:1]).then_inc(sNr, 1)
            v.wait_ge(sBm, 1)
            v.tensor_copy(c.sm_negu.ap()[:, 0:1], c.psbf[:, 0:1]).then_inc(sNg, 1)
            for h in range(H):
                for qt in range(NQT):
                    i = h * 4 + qt
                    for p in range(NP):
                        G = NP * i + p
                        v.wait_ge(sEx, G + 1)
                        if G >= 3:
                            v.wait_ge(sDn, G - 2)
                        v.tensor_tensor(FOLD[p % 3], PT[p % 3][:, 0:512],
                                        PT[p % 3][:, 512:1024],
                                        ALU.add).then_inc(sF1, 1)
                        if qt == 2 and p == 7 and h + 1 < H:
                            hn = h + 1
                            vc(c, v.tensor_add(c.sm_t1.ap()[:, hn % 2:hn % 2 + 1],
                                               c.sm_nq32.ap()[:, hn:hn + 1],
                                               c.sm_nk5.ap()[:, hn // 4:hn // 4 + 1]))
                            v.wait_ge(sBm, hn)
                            vw(c, v)
                            v.tensor_copy(c.sm_t1b.ap()[:, hn % 2:hn % 2 + 1],
                                          c.sm_t1.ap()[:, hn % 2:hn % 2 + 1]
                                          ).then_inc(sNr, 1)
                        if qt == 3 and p == 7 and h + 1 < H:
                            hn = h + 1
                            v.wait_ge(sBm, hn + 1)
                            v.tensor_copy(c.sm_negu.ap()[:, hn % 2:hn % 2 + 1],
                                          c.psbf[:, 0:1]).then_inc(sNg, 1)
                    v.wait_ge(sBc, i + 1)
                    v.wait_ge(sPV, NP * (i + 1))
                    vc(c, v.tensor_copy(BCS, psd[:, 0:512]))
                    vw(c, v)
                    v.tensor_tensor(c.RES[:, h, qt * 512:(qt + 1) * 512],
                                    psa[:, 0:512], BCS, ALU.mult).then_inc(sAo, 1)

        @blk.scalar
        def _(s):
            for h in range(H):
                for qt in range(NQT):
                    i = h * 4 + qt
                    for p in range(NP):
                        G = NP * i + p
                        s.wait_ge(sQK, G + 1)
                        if p == 0 and qt == 0:
                            s.wait_ge(sNg, h + 1)
                        if G - 2 > 0:
                            s.wait_ge(sPV, G - 2)
                            s.wait_ge(sF1, G - 2)
                        bank = 0 if (p % 2 == 0) else 1024
                        s.activation(PT[p % 3], psq[:, bank:bank + 1024], AF.Exp,
                                     bias=c.sm_negu.ap()[:, h % 2:h % 2 + 1]
                                     ).then_inc(sEx, 1)
                    s.wait_ge(sDn, NP * (i + 1))
                    sc_(c, s.activation(c.sm_recl.ap(), DEN, AF.Ln))
                    sw_(c, s)
                    s.activation(c.sm_recb.ap(), c.sm_recl.ap(), AF.Exp,
                                 scale=-1.0).then_inc(sRc, 1)


def _p5_pv(c, t, i, p, bk, VBUF, PT, psa, sEx, sPV, sAo):
    t.wait_ge(sEx, NP * i + p + 1)
    if p == 0 and i >= 1:
        t.wait_ge(sAo, i)
    last = None
    for u in range(2):
        cc = 2 * p + u
        last = t.matmul(psa[:, 0:512], VBUF[bk][:, cc * 128:(cc + 1) * 128],
                        PT[p % 3][:, u * 512:(u + 1) * 512],
                        start=(cc == 0), stop=(cc == 31))
    last.then_inc(sPV, 1)


def _p5_den(c, t, i, p, FOLD, sF1, sDn, sRc):
    t.wait_ge(sF1, NP * i + p + 1)
    if p == 0 and i >= 1:
        t.wait_ge(sRc, i)
    t.matmul(c.psbf[0:1, 512:1024], c.sm_ones_col.ap(), FOLD[p % 3],
             start=(p == 0), stop=(p == NP - 1)).then_inc(sDn, 1)


class LnSems:
    KEYS = ["st", "sd", "ap", "cs", "tp", "ev", "add", "nm"]

    def __init__(self, c):
        self.h = {k: _sem(c, "ln_" + k) for k in self.KEYS}
        self.base = {k: 0 for k in self.KEYS}
        self.dA = DSem(c, "ln_a")
        self.dB = DSem(c, "ln_b")
        self.dO = DSem(c, "ln_od")
        self.dF = DSem(c, "ln_fd")
        self.dF2 = DSem(c, "ln_fd2")

    def wg(self, eng, k, v):
        if self.base[k] + v > 0:
            eng.wait_ge(self.h[k], self.base[k] + v)

    def inc(self, inst, k, n=1):
        inst.then_inc(self.h[k], n)

    def fin(self, used):
        for k, n in used.items():
            self.base[k] += n


def emit_ln(c, name, in_a_dram, in_b_getter, out_writer, fm_out):
    nc = c.nc
    if not hasattr(c, "lnsems"):
        c.lnsems = LnSems(c)
    L = c.lnsems
    sA, sB, sOd, sFd, sFd2 = L.dA, L.dB, L.dO, L.dF, L.dF2

    big1f = c.BIG1.ap().bitcast(F32)
    OB = [big1f[:, 0:4096], big1f[:, 4096:8192]]
    XB = [big1f[:, 8192:12288], big1f[:, 12288:16384]]
    big2 = c.BIG2.ap()
    XB16 = [big2[:, 0:4096], big2[:, 4096:8192]]
    XSTG = [big2[:, 8192:12288], big2[:, 12288:16384]]
    psb = c.PSB.ap()

    with nc.Block() as blk:
        @blk.sync
        def _(sy):
            for tc in range(NT):
                b = tc % 2
                if tc >= 2:
                    L.wg(sy, "ap", tc - 1)
                    sOd.wait(sy, tc - 2)
                    if fm_out:
                        L.wg(sy, "cs", tc - 1)
                sy.dma_start(OB[b], in_a_dram.ap()[tc, :, :]).then_inc(sA.h(tc), 16)
                sy.dma_start(XB[b], in_b_getter(tc)).then_inc(sB.h(tc), 16)

        @blk.vector
        def _(v):
            for tc in range(NT):
                b = tc % 2
                sA.wait(v, tc)
                sB.wait(v, tc)
                vc(c, v.tensor_add(OB[b], OB[b], XB[b]))
                vw(c, v)
                for j in range(8):
                    if j == 7:
                        vc(c, v.bn_stats(c.sm_bn.ap()[:, j * 6:(j + 1) * 6],
                                         OB[b][:, j * 512:(j + 1) * 512]))
                    else:
                        v.bn_stats(c.sm_bn.ap()[:, j * 6:(j + 1) * 6],
                                   OB[b][:, j * 512:(j + 1) * 512])
                vw(c, v)
                vc(c, v.bn_aggr(c.sm_mv.ap(), c.sm_bn.ap()))
                vw(c, v)
                L.inc(v.tensor_scalar(c.sm_var.ap(), c.sm_mv.ap()[:, 1:2], 1e-5, None,
                                      ALU.add), "st")
                L.wg(v, "sd", tc + 1)
                vc(c, v.reciprocal(c.sm_rstd.ap(), c.sm_sd.ap()))
                vw(c, v)
                L.inc(v.scalar_tensor_tensor(c.sm_nmr.ap(), c.sm_mv.ap()[:, 0:1], -1.0,
                                             c.sm_rstd.ap(), ALU.mult, ALU.mult), "nm")
                if fm_out:
                    for g in range(4):
                        gi = tc * 4 + g
                        pb = psb[:, (gi % 2) * 1024:(gi % 2) * 1024 + 1024]
                        L.wg(v, "tp", gi + 1)
                        if tc >= 2 and g == 0:
                            sFd.wait(v, tc - 2, per=256)
                            sFd2.wait(v, tc - 2, per=256)
                        L.inc(v.tensor_copy(
                            XSTG[b].rearrange("p (a b) -> p a b", a=32)
                            [:, g * 8:(g + 1) * 8, :],
                            pb.rearrange("p (a b) -> p a b", a=8)), "ev")

        @blk.scalar
        def _(s):
            for tc in range(NT):
                b = tc % 2
                L.wg(s, "st", tc + 1)
                L.inc(s.activation(c.sm_sd.ap(), c.sm_var.ap(), AF.Sqrt), "sd")
                L.wg(s, "nm", tc + 1)
                L.inc(s.activation(XB[b], OB[b], AF.Identity, scale=c.sm_rstd.ap(),
                                   bias=c.sm_nmr.ap()), "ap")
                if fm_out:
                    L.wg(s, "ap", tc + 1)
                    if tc >= 2:
                        L.wg(s, "tp", 4 * (tc - 1))
                    L.inc(s.activation(XB16[b], XB[b], AF.Copy), "cs")
                    s.wait_ge(L.h["ev"], L.base["ev"] + 4 * (tc + 1))
                    for dc in range(1, ND, 2):
                        s.dma_start(c.X1FM.ap()[dc, :, tc * 128:(tc + 1) * 128],
                                    XSTG[b].rearrange("p (a b) -> p a b", a=32)[:, dc, :]
                                    ).then_inc(sFd2.h(tc), 16)

        if fm_out:
            @blk.tensor
            def _(t):
                for tc in range(NT):
                    b = tc % 2
                    L.wg(t, "cs", tc + 1)
                    for g in range(4):
                        gi = tc * 4 + g
                        if gi >= 2:
                            L.wg(t, "ev", gi - 1)
                        pb = psb[:, (gi % 2) * 1024:(gi % 2) * 1024 + 1024]
                        last = None
                        for j in range(8):
                            dc = g * 8 + j
                            last = t.transpose(pb[:, j * 128:(j + 1) * 128],
                                               XB16[b][:, dc * 128:(dc + 1) * 128],
                                               c.sm_idb.ap())
                        L.inc(last, "tp")

        @blk.gpsimd
        def _(g):
            for tc in range(NT):
                b = tc % 2
                L.wg(g, "ap", tc + 1)
                out_writer(g, tc, XB[b], sOd.h(tc))
                if fm_out:
                    L.wg(g, "ev", 4 * (tc + 1))
                    for dc in range(0, ND, 2):
                        g.dma_start(c.X1FM.ap()[dc, :, tc * 128:(tc + 1) * 128],
                                    XSTG[b].rearrange("p (a b) -> p a b", a=32)[:, dc, :]
                                    ).then_inc(sFd.h(tc), 16)
            sOd.wait(g, NT - 2)
            sOd.wait(g, NT - 1)
            if fm_out:
                sFd.wait(g, NT - 2, per=256)
                sFd.wait(g, NT - 1, per=256)
                sFd2.wait(g, NT - 2, per=256)
                sFd2.wait(g, NT - 1, per=256)

    used = {"st": NT, "sd": NT, "ap": NT, "nm": NT}
    if fm_out:
        used["cs"] = NT
        used["tp"] = 4 * NT
        used["ev"] = 4 * NT
        sFd.close(NT, 256)
        sFd2.close(NT, 256)
    sA.close(NT, 16)
    sB.close(NT, 16)
    sOd.close(NT, 16)
    L.fin(used)


def phase7(c):
    def in_b(tc):
        return c.x.ap()[tc * 128:(tc + 1) * 128, :]

    def out_w(g, tc, src, semh):
        g.dma_start(c.X1TM.ap()[tc, :, :], src).then_inc(semh, 16)

    emit_ln(c, "p7", c.OOUT, in_b, out_w, fm_out=True)


def phase9(c):
    def in_b(tc):
        return c.X1TM.ap()[tc, :, :]

    def out_w(g, tc, src, semh):
        g.dma_start(c.y.ap()[tc * 128:(tc + 1) * 128, :], src).then_inc(semh, 16)

    emit_ln(c, "p9", c.FFOUT, in_b, out_w, fm_out=False)


def phase8(c):
    nc = c.nc
    big2 = c.BIG2.ap()
    big2f = big2.bitcast(F32)
    LOWW = big2[:, 24576:26624]
    WBC = big2[:, 26624:28672]
    GL = big2f[0:8, 0:2048]
    psq = c.PSQ.ap()
    psb = c.PSB.ap()
    W83 = c.sm_w8.ap().rearrange("p (a b) -> p a b", a=16)
    LGT3 = c.sm_lgt.ap().rearrange("p (a b) -> p a b", a=16)

    sL = _sem(c, "p8_l")
    sG = _sem(c, "p8_g")
    sT2 = _sem(c, "p8_t2")
    sE2 = _sem(c, "p8_e2")
    sW2 = _sem(c, "p8_w2")

    with nc.Block() as blk:
        @blk.sync
        def _(sy):
            for dc in range(ND):
                sy.dma_start(c.RES[:, dc, :], c.X1FM.ap()[dc, :, :]).then_inc(sL, 16)

        @blk.tensor
        def _(t):
            t.wait_ge(sL, 16 * ND)
            last = None
            for qt in range(NQT):
                for dc in range(ND):
                    last = t.matmul(psq[0:8, qt * 512:(qt + 1) * 512], c.GT[:, dc, :],
                                    c.RES[:, dc, qt * 512:(qt + 1) * 512],
                                    start=(dc == 0), stop=(dc == ND - 1))
            last.then_inc(sG, 1)
            t.wait_ge(sG, 2)
            for tc in range(NT):
                last = t.transpose(c.psbf[:, tc * 8:tc * 8 + 8],
                                   GL[:, tc * 128:(tc + 1) * 128], c.sm_idf.ap()[0:8, 0:8])
            last.then_inc(sG, 1)
            t.wait_ge(sT2, 2 * NT)
            for tc in range(NT):
                last = t.transpose(psb[0:8, tc * 128:(tc + 1) * 128], W83[:, tc, :],
                                   c.sm_idb.ap())
            last.then_inc(sW2, 1)
            t.wait_ge(sW2, 2)
            for qt in range(NQT):
                last = t.matmul(psq[:, qt * 512:(qt + 1) * 512], c.sm_exp8.ap(),
                                c.sm_wt8.ap()[:, qt * 512:(qt + 1) * 512],
                                start=True, stop=True)
            last.then_inc(sW2, 1)
            t.wait_ge(sW2, 4)
            for qt in range(NQT):
                for dc in range(ND):
                    last = t.matmul(psq[:, qt * 512:(qt + 1) * 512], c.AT[:, dc, :],
                                    c.RES[:, dc, qt * 512:(qt + 1) * 512],
                                    start=(dc == 0), stop=(dc == ND - 1))
            last.then_inc(sW2, 1)

        @blk.vector
        def _(v):
            v.wait_ge(sG, 1)
            v.tensor_copy(GL, psq[0:8, 0:2048]).then_inc(sG, 1)
            v.wait_ge(sG, 3)
            v.tensor_copy(c.sm_lgt.ap(), c.psbf[:, 0:128]).then_inc(sG, 1)
            v.wait_ge(sG, 4)
            for tc in range(NT):
                vw(c, v)
                vc(c, v.max(c.sm_m8.ap(), LGT3[:, tc, :]))
                vw(c, v)
                v.tensor_scalar(c.sm_ngm.ap(), c.sm_m8.ap()[:, 0:1], -1.0, None,
                                ALU.mult).then_inc(sT2, 1)
                v.wait_ge(sE2, tc + 1)
                vc(c, v.tensor_scalar(c.sm_sel.ap(), LGT3[:, tc, :],
                                      c.sm_m8.ap()[:, 1:2], None, ALU.is_ge))
                vw(c, v)
                vc(c, v.tensor_tensor(c.sm_psel.ap(), c.sm_p8e.ap(), c.sm_sel.ap(),
                                      ALU.mult))
                vw(c, v)
                vc(c, v.reduce_sum(c.sm_den8.ap(), c.sm_psel.ap(), axis=AX.X))
                vw(c, v)
                vc(c, v.reciprocal(c.sm_re8.ap(), c.sm_den8.ap()))
                vw(c, v)
                v.tensor_scalar(W83[:, tc, :], c.sm_psel.ap(), c.sm_re8.ap(), None,
                                ALU.mult).then_inc(sT2, 1)
            v.wait_ge(sW2, 1)
            v.tensor_copy(c.sm_wt8.ap(), psb[0:8, 0:2048]).then_inc(sW2, 1)
            v.wait_ge(sW2, 3)
            v.tensor_copy(WBC, psq[:, 0:2048]).then_inc(sW2, 1)
            v.wait_ge(sW2, 5)
            for qt in range(NQT):
                last = v.scalar_tensor_tensor(LOWW[:, qt * 512:(qt + 1) * 512],
                                              psq[:, qt * 512:(qt + 1) * 512], 2.0,
                                              WBC[:, qt * 512:(qt + 1) * 512],
                                              ALU.mult, ALU.mult)
            last.then_inc(sW2, 1)

        @blk.scalar
        def _(s):
            for tc in range(NT):
                s.wait_ge(sT2, 2 * tc + 1)
                s.activation(c.sm_p8e.ap(), LGT3[:, tc, :], AF.Exp,
                             bias=c.sm_ngm.ap()).then_inc(sE2, 1)

    emit_proj(c, "p8b", c.wb, c.sm_bbc.ap(), 1.0, False, "f32_tm", c.FFOUT,
              lora=True)


def kernel(L0_attn_hidden_states, L0_attn_q_weight, L0_attn_q_scale,
           L0_attn_k_weight, L0_attn_k_scale, L0_attn_v_weight, L0_attn_v_scale,
           L0_attn_o_weight, L0_attn_o_scale, L0_ffn_gate_w, L0_ffn_base_w,
           L0_ffn_base_b, L0_ffn_lora_A, L0_ffn_lora_B):
    hs = np.asarray(L0_attn_hidden_states)
    gt = np.ascontiguousarray(
        np.asarray(L0_ffn_gate_w, np.float32).T.reshape(32, 128, 8)
        .transpose(1, 0, 2).reshape(128, 256))
    lat = np.ascontiguousarray(
        np.asarray(L0_ffn_lora_A, np.float32).reshape(128, D).T
        .reshape(32, 128, 128).transpose(1, 0, 2).reshape(128, 4096))
    lbt = np.ascontiguousarray(
        np.asarray(L0_ffn_lora_B, np.float32).transpose(0, 2, 1).reshape(128, D))
    consts = dict(
        wq=np.ascontiguousarray(L0_attn_q_weight, dtype=np.float32),
        wk=np.ascontiguousarray(L0_attn_k_weight, dtype=np.float32),
        wv=np.ascontiguousarray(L0_attn_v_weight, dtype=np.float32),
        wo=np.ascontiguousarray(L0_attn_o_weight, dtype=np.float32),
        wb=np.ascontiguousarray(L0_ffn_base_w, dtype=np.float32),
        qsc_in=np.ascontiguousarray(np.asarray(L0_attn_q_scale, np.float32).reshape(ND, 128)),
        ksc_in=np.ascontiguousarray(np.asarray(L0_attn_k_scale, np.float32).reshape(HK, 128)),
        vsc_in=np.ascontiguousarray(np.asarray(L0_attn_v_scale, np.float32).reshape(HK, 128)),
        osc_in=np.ascontiguousarray(np.asarray(L0_attn_o_scale, np.float32).reshape(ND, 128)),
        bb_in=np.ascontiguousarray(np.asarray(L0_ffn_base_b, np.float32).reshape(ND, 128)),
        gt_in=gt, lat_in=lat, lbt_in=lbt,
        id_bf=np.eye(128, dtype=BF),
        id_f32=np.eye(128, dtype=np.float32),
        ones_row=np.ones((1, 128), dtype=BF),
        ones_col=np.ones((128, 1), dtype=BF),
        ones_sq_f32=np.ones((128, 128), dtype=np.float32),
        exp8=_exp8_const(),
    )
    in_maps = []
    for core in range(8):
        seq, half = core // 2, core % 2
        m = dict(consts)
        xs = np.ascontiguousarray(hs[seq], dtype=np.float32)
        if half == 1:
            xs = np.ascontiguousarray(np.roll(xs, -T, axis=0))
        m["x"] = xs
        in_maps.append(m)

    nc = build_program()
    res = run_bass_kernel_spmd(nc, in_maps, core_ids=list(range(8)))
    full = np.empty((4, S, D), np.float32)
    for core in range(8):
        seq, half = core // 2, core % 2
        full[seq, half * T:(half + 1) * T, :] = res.results[core]["y"]
    return full


def _exp8_const():
    e = np.zeros((8, 128), dtype=BF)
    for i in range(8):
        e[i, i * 16:(i + 1) * 16] = 1
    return e
